# version 01
# baseline (speedup 1.0000x reference)
import numpy as np
import ml_dtypes
import concourse.bass as bass
import concourse.mybir as mybir
from concourse.bass_utils import run_bass_kernel_spmd
from concourse.alu_op_type import AluOpType

F32 = mybir.dt.float32
BF16 = mybir.dt.bfloat16
AF = mybir.ActivationFunctionType
AX = mybir.AxisListType

NCORES = 8
D = 1024
DC = 8
NB = 2
DSTATE = 16
WIN = 512
NHEAD = 8
DH = 64
INNER = 512
HID = 2730
H1W = 5460
LN_EPS_D2 = 1e-5 * D * D
LN_LOGD = float(np.log(D))


class Ch:
    def __init__(self, nc, name):
        self._cm = nc.semaphore(name)
        self.sem = self._cm.__enter__()
        self.n = 0


def build(T=2048, stage=10, ths_override=None):
    nc = bass.Bass("TRN2", target_bir_lowering=False)
    nc.detect_race_conditions = False
    NTT = T // 128

    x_in = nc.dram_tensor("x", [NB, T, D], F32, kind="ExternalInput").ap()
    out = nc.dram_tensor("out", [NB, T, D], F32, kind="ExternalOutput").ap()

    w_x = nc.dram_tensor("W_x", [D, 96], F32, kind="ExternalInput").ap()
    w_dtb = nc.dram_tensor("W_dtb", [65, D], F32, kind="ExternalInput").ap()
    a_log_t = nc.dram_tensor("A_log_t", [128, DC * DSTATE], F32, kind="ExternalInput").ap()
    d_t = nc.dram_tensor("D_t", [128, DC], F32, kind="ExternalInput").ap()

    wqkv_in = nc.dram_tensor("wqkv", [2, D, 3 * INNER], F32, kind="ExternalInput").ap()
    qs_t = nc.dram_tensor("qs_t", [2, 128, 1], F32, kind="ExternalInput").ap()
    ks_t = nc.dram_tensor("ks_t", [2, 128, 1], F32, kind="ExternalInput").ap()
    wo_in = nc.dram_tensor("wo", [2, INNER, D], F32, kind="ExternalInput").ap()
    lng_t = nc.dram_tensor("lng_t", [2, 128, DC], F32, kind="ExternalInput").ap()
    lnb_t = nc.dram_tensor("lnb_t", [2, 128, DC], F32, kind="ExternalInput").ap()
    w1_in = nc.dram_tensor("w1", [2, D, H1W], F32, kind="ExternalInput").ap()
    b1a_in = nc.dram_tensor("b1a_t", [2, 128, 22], F32, kind="ExternalInput").ap()
    b1g_in = nc.dram_tensor("b1g_t", [2, 128, 22], F32, kind="ExternalInput").ap()
    w2_in = nc.dram_tensor("w2", [2, HID, D], F32, kind="ExternalInput").ap()
    b2_t = nc.dram_tensor("b2_t", [2, 128, DC], F32, kind="ExternalInput").ap()

    ident = nc.dram_tensor("ident", [128, 128], F32, kind="ExternalInput").ap()
    cvec = nc.dram_tensor("cvec", [128, 8], F32, kind="ExternalInput").ap()
    selc_in = nc.dram_tensor("selc", [32, 32 * 128], F32, kind="ExternalInput").ap()
    hd2_in = nc.dram_tensor("hd2", [128, 2], F32, kind="ExternalInput").ap()
    hd2T_in = nc.dram_tensor("hd2T", [2, 128], F32, kind="ExternalInput").ap()
    onesf_in = nc.dram_tensor("onesf", [128, 128], F32, kind="ExternalInput").ap()
    rperm_in = nc.dram_tensor("rperm", [128, 128], F32, kind="ExternalInput").ap()
    rpermq_in = nc.dram_tensor("rpermq", [128, 128], F32, kind="ExternalInput").ap()
    trimadd_in = nc.dram_tensor("trimadd", [128, 256], F32, kind="ExternalInput").ap()
    trilo_in = nc.dram_tensor("trilo", [128, 128], F32, kind="ExternalInput").ap()
    triup_in = nc.dram_tensor("triup", [128, 128], F32, kind="ExternalInput").ap()
    csq_in = nc.dram_tensor("csq", [128, T], F32, kind="ExternalInput").ap()
    ssq_in = nc.dram_tensor("ssq", [128, T], F32, kind="ExternalInput").ap()
    csk_in = nc.dram_tensor("csk", [128, T], F32, kind="ExternalInput").ap()
    ssk_in = nc.dram_tensor("ssk", [128, T], F32, kind="ExternalInput").ap()

    xn16 = nc.dram_tensor("xn16", [NB, DC, 128, T], BF16, kind="Internal").ap()
    s16 = nc.dram_tensor("s16", [NB, DC, 128, T], BF16, kind="Internal").ap()
    strmA = nc.dram_tensor("strmA", [NB, DC, 128, T], BF16, kind="Internal").ap()
    strmB = nc.dram_tensor("strmB", [NB, DC, 128, T], BF16, kind="Internal").ap()

    ctxs = []
    def sbuf(name, shape, dt):
        t = nc.sbuf_tensor(name, shape, dt)
        h = t.__enter__(); ctxs.append(t); return h
    def psum(name, shape, dt):
        t = nc.psum_tensor(name, shape, dt)
        h = t.__enter__(); ctxs.append(t); return h

    ident16 = sbuf("ident16", [128, 128], BF16)
    cv = sbuf("cv", [128, 8], F32)

    p0_ctxs = []
    def sbuf_p0(name, shape, dt):
        t = nc.sbuf_tensor(name, shape, dt)
        h = t.__enter__(); p0_ctxs.append(t); return h
    xb = [sbuf_p0(f"xb{i}", [128, D], F32) for i in range(2)]
    sqb = [sbuf_p0(f"sqb{i}", [128, D], F32) for i in range(2)]
    stat = sbuf_p0("stat", [128, 16], F32)
    xnb = [sbuf_p0(f"xnb{i}", [128, D], BF16) for i in range(2)]
    xt_stage = [sbuf_p0(f"xt_stage{i}", [128, 256], BF16) for i in range(DC)]
    NPS_T = 16
    ps_tt_ctx = [nc.psum_tensor(f"ps_tt{i}", [128, 1024], BF16) for i in range(2)]
    ps_tt = [t.__enter__() for t in ps_tt_ctx]
    ps_tt_ctx = list(reversed(ps_tt_ctx))
    def ps_t_slot(k):
        sl = k % NPS_T
        return ps_tt[sl // 8][:, (sl % 8) * 128:(sl % 8) * 128 + 128]

    ch_hwl = Ch(nc, "hwl")
    ch_hwl2 = Ch(nc, "hwl2")
    ch_swl = Ch(nc, "swl")
    ch_sq = Ch(nc, "c_sq")
    ch_st = Ch(nc, "c_st")
    ch_pe = Ch(nc, "c_pe")
    ch_ev = Ch(nc, "c_ev")
    ch_sp = Ch(nc, "c_sp")
    ch_out = Ch(nc, "c_out")

    blk = nc.Block()
    block = blk.__enter__()

    tiles = [(b, tt) for b in range(NB) for tt in range(NTT)]
    NTILE = len(tiles)
    VP, SP = 4, 4

    @block.sync
    def _(sync):
        for i, (b, tt) in enumerate(tiles):
            if i >= 2:
                sync.wait_ge(ch_st.sem, (i - 1) * VP)
            _c = ch_hwl if i % 2 == 0 else ch_hwl2
            sync.dma_start(out=xb[i % 2][:],
                           in_=x_in[b, tt * 128:(tt + 1) * 128, :]
                           ).then_inc(_c.sem, 16)
            _c.n += 16

    @block.gpsimd
    def _(g):
        g.dma_start(out=ident16[:], in_=ident).then_inc(ch_swl.sem, 16)
        ch_swl.n += 16
        g.dma_start(out=cv[:], in_=cvec).then_inc(ch_swl.sem, 16)
        ch_swl.n += 16

    @block.scalar
    def _(s):
        for i, (b, tt) in enumerate(tiles):
            _c = ch_hwl if i % 2 == 0 else ch_hwl2
            s.wait_ge(_c.sem, 16 * (i // 2 + 1))
            if i == 0:
                s.wait_ge(ch_swl.sem, 32)
            if i >= 2:
                s.wait_ge(ch_st.sem, (i - 1) * VP)
            s.activation(out=sqb[i % 2][:], in_=xb[i % 2][:],
                         func=AF.Square).then_inc(ch_sq.sem, 1)
            s.wait_ge(ch_st.sem, i * VP + 1)
            s.activation(out=stat[:, 2:3], in_=stat[:, 0:1],
                         func=AF.Identity, scale=1.0 / D)
            s.activation(out=stat[:, 6:7], in_=stat[:, 0:1],
                         func=AF.Square).then_inc(ch_sq.sem, 1)
            s.wait_ge(ch_st.sem, i * VP + 3)
            s.activation(out=stat[:, 8:9], in_=stat[:, 3:4], func=AF.Ln,
                         bias=cv[:, 0:1]).then_inc(ch_sq.sem, 1)
            s.wait_ge(ch_sq.sem, i * SP + 3)
            s.activation(out=stat[:, 4:5], in_=stat[:, 8:9], func=AF.Exp,
                         scale=-0.5, bias=cv[:, 1:2]).then_inc(ch_sq.sem, 1)
            ch_sq.n = (i + 1) * SP

            def _evac_tile(j):
                bj, ttj = tiles[j]
                s.wait_ge(ch_pe.sem, (j + 1) * DC)
                if j >= 2:
                    s.wait_ge(ch_sp.sem, 128 * (j // 2))
                for dc in range(DC):
                    k = j * DC + dc
                    s.copy(out=xt_stage[dc][:, (ttj % 2) * 128:(ttj % 2) * 128 + 128],
                           in_=ps_t_slot(k)).then_inc(ch_ev.sem, 1)
                    ch_ev.n += 1
                if ttj % 2 == 1:
                    s.wait_ge(ch_ev.sem, ch_ev.n)
                    for dc in range(DC):
                        s.dma_start(out=xn16[bj, dc, :, (ttj - 1) * 128:(ttj + 1) * 128],
                                    in_=xt_stage[dc][:]).then_inc(ch_sp.sem, 16)
                        ch_sp.n += 16
            if i >= 1:
                _evac_tile(i - 1)
            if i == NTILE - 1:
                _evac_tile(i)

    @block.vector
    def _(v):
        for i, (b, tt) in enumerate(tiles):
            _c = ch_hwl if i % 2 == 0 else ch_hwl2
            v.wait_ge(_c.sem, 16 * (i // 2 + 1))
            if i >= 1:
                v.wait_ge(ch_sq.sem, i * SP)
            v.tensor_reduce(out=stat[:, 0:1], in_=xb[i % 2][:], axis=AX.X,
                            op=AluOpType.add).then_inc(ch_st.sem, 1)
            v.wait_ge(ch_sq.sem, i * SP + 1)
            v.tensor_reduce(out=stat[:, 1:2], in_=sqb[i % 2][:], axis=AX.X,
                            op=AluOpType.add).then_inc(ch_st.sem, 1)
            v.wait_ge(ch_st.sem, i * VP + 2)
            v.wait_ge(ch_sq.sem, i * SP + 2)
            v.scalar_tensor_tensor(out=stat[:, 3:4], in0=stat[:, 1:2],
                                   scalar=float(D), in1=stat[:, 6:7],
                                   op0=AluOpType.mult,
                                   op1=AluOpType.subtract
                                   ).then_inc(ch_st.sem, 1)
            v.wait_ge(ch_sq.sem, i * SP + 4)
            if i >= 2:
                v.wait_ge(ch_pe.sem, (i - 1) * DC)
                if stage < 6:
                    v.wait_ge(ch_out.sem, 16 * (i - 1))
            v.tensor_scalar(out=xnb[i % 2][:], in0=xb[i % 2][:],
                            scalar1=stat[:, 2:3], scalar2=stat[:, 4:5],
                            op0=AluOpType.subtract,
                            op1=AluOpType.mult).then_inc(ch_st.sem, 1)
            ch_st.n = (i + 1) * VP

    @block.tensor
    def _(te):
        te.wait_ge(ch_swl.sem, 32)
        for i, (b, tt) in enumerate(tiles):
            te.wait_ge(ch_st.sem, i * VP + 4)
            for dc in range(DC):
                k = i * DC + dc
                if k >= NPS_T:
                    te.wait_ge(ch_ev.sem, k - (NPS_T - 1))
                te.transpose(out=ps_t_slot(k),
                             in_=xnb[i % 2][:, dc * 128:(dc + 1) * 128],
                             identity=ident16[:]).then_inc(ch_pe.sem, 1)
                ch_pe.n += 1

    if stage < 6:
        @block.gpsimd
        def _(g):
            for i, (b, tt) in enumerate(tiles):
                g.wait_ge(ch_st.sem, i * VP + 4)
                g.dma_start(out=out[b, tt * 128:(tt + 1) * 128, :],
                            in_=xnb[i % 2][:]).then_inc(ch_out.sem, 16)
                ch_out.n += 16
            g.wait_ge(ch_out.sem, ch_out.n)
            g.wait_ge(ch_sp.sem, ch_sp.n)


    if stage >= 2:
        for t_ in ps_tt_ctx:
            t_.__exit__(None, None, None)

        THS = ths_override or min(1024, T)
        NTH = T // THS
        NTC = THS // 512

        mamba_ctxs = []
        def sbuf_m(name, shape, dt):
            t = nc.sbuf_tensor(name, shape, dt)
            h = t.__enter__(); mamba_ctxs.append(t); return h
        def psum_m(name, shape, dt):
            t = nc.psum_tensor(name, shape, dt)
            h = t.__enter__(); mamba_ctxs.append(t); return h

        wx16 = sbuf_m("wx16", [128, DC * 96], BF16)
        wdtb16 = sbuf_m("wdtb16", [65, D], BF16)
        selc16 = sbuf_m("selc16", [32, 32 * 128], BF16)
        alog_sb = sbuf_m("alog_sb", [128, 128], F32)
        a_sb = sbuf_m("a_sb", [128, 128], F32)
        dvec = sbuf_m("dvec", [128, DC], F32)
        xpa = sbuf_m("xpa", [65, THS], BF16)
        bc16 = sbuf_m("bc16", [32, THS], BF16)
        xnT = [sbuf_m(f"m_xnT{c}", [128, THS], BF16) for c in range(DC)]
        dtT = [sbuf_m(f"m_dtT{c}", [128, THS], F32) for c in range(DC)]
        dtx = [sbuf_m(f"m_dtx{c}", [128, THS], BF16) for c in range(DC)]
        ybuf = [sbuf_m(f"m_y{c}", [128, THS], F32) for c in range(DC)]
        sp_t = [sbuf_m(f"m_spt{i}", [128, 512], F32) for i in range(3)]
        abar = [sbuf_m(f"m_ab{i}", [128, THS], F32) for i in range(2)]
        ubuf = [sbuf_m(f"m_u{i}", [128, THS], BF16) for i in range(2)]
        hbuf = [sbuf_m(f"m_h{i}", [128, THS], BF16) for i in range(2)]
        tmpb = [sbuf_m(f"m_tmp{i}", [128, THS], BF16) for i in range(2)]
        bb_sb = sbuf_m("m_bb", [128, THS], BF16)
        cb_sb = sbuf_m("m_cb", [128, THS], BF16)
        hcar = sbuf_m("m_hcar", [128, 128], F32)
        soutb = sbuf_m("m_sout", [128, THS], BF16)

        pm_dt = psum_m("pm_dt", [64, 512], F32)
        pm_bc = psum_m("pm_bc", [32, 512], F32)
        pm_dtT = [psum_m(f"pm_dtT{i}", [128, 512], F32) for i in range(2)]
        pm_bb = psum_m("pm_bb", [128, THS], F32)
        pm_cb = psum_m("pm_cb", [128, THS], F32)

        cL = Ch(nc, "m_L"); cG = Ch(nc, "m_G"); cP = Ch(nc, "m_P")
        cS = Ch(nc, "m_S"); cV = Ch(nc, "m_V"); cW = Ch(nc, "m_W")
        cGm = Ch(nc, "m_Gm")
        P0_EV = ch_ev.n
        P0_SP = ch_sp.n

        @block.gpsimd
        def _(g):
            for c in range(DC):
                g.dma_start(out=wx16[:, c * 96:(c + 1) * 96],
                            in_=w_x[c * 128:(c + 1) * 128, :]).then_inc(cG.sem, 16)
                cG.n += 16
            for ap_dst, ap_src in [(wdtb16[:], w_dtb), (selc16[:], selc_in),
                                   (alog_sb[:], a_log_t), (dvec[:], d_t)]:
                g.dma_start(out=ap_dst, in_=ap_src).then_inc(cG.sem, 16)
                cG.n += 16
            g.memset(xpa[64:65, :], 1.0).then_inc(cGm.sem, 1)
            cGm.n += 1
        G_W = cG.n

        @block.scalar
        def _(s):
            s.wait_ge(cG.sem, G_W)
            s.activation(out=a_sb[:], in_=alog_sb[:], func=AF.Exp).then_inc(cS.sem, 1)
        cS.n += 1
        S_EXPA = cS.n

        @block.vector
        def _(v):
            v.wait_ge(cS.sem, S_EXPA)
            v.tensor_scalar(out=a_sb[:], in0=a_sb[:], scalar1=-1.0, scalar2=None,
                            op0=AluOpType.mult).then_inc(cV.sem, 1)
        cV.n += 1
        V_NEGA = cV.n

        blocks2 = [(b, th) for b in range(NB) for th in range(NTH)]
        prev_blk_end_v = [0]
        prev_blk_end_p = [0]
        prev_blk_end_w = [0]
        pm_dtT_war = [[0], [0]]
        sp_war = [0]
        bb_war = [0]
        bbu_war = [0]
        ab_war = [[0], [0]]
        u_war = [[0], [0]]
        sout_war = [0]

        for m, (b, th) in enumerate(blocks2):
            t0 = th * THS

            @block.sync
            def _(sync):
                if m == 0:
                    sync.wait_ge(ch_sp.sem, P0_SP)
                else:
                    sync.wait_ge(cV.sem, prev_blk_end_v[0])
                    sync.wait_ge(cP.sem, prev_blk_end_p[0])
                for c in range(DC):
                    sync.dma_start(out=xnT[c][:], in_=xn16[b, c, :, t0:t0 + THS]
                                   ).then_inc(cL.sem, 16)
                    cL.n += 16
            L_XNT = cL.n

            sp_v_base = cV.n
            for tc in range(NTC):
                S_XPA_PREV = cS.n

                @block.tensor
                def _(te):
                    te.wait_ge(cL.sem, L_XNT)
                    if m == 0 and tc == 0:
                        te.wait_ge(cV.sem, V_NEGA)
                        te.wait_ge(cGm.sem, 1)
                        te.wait_ge(ch_ev.sem, P0_EV)
                    if m > 0 or tc > 0:
                        te.wait_ge(cS.sem, S_XPA_PREV)
                    for c in range(DC):
                        mm = te.matmul(out=pm_dt[:], lhsT=wx16[:, c * 96:c * 96 + 64],
                                       rhs=xnT[c][:, tc * 512:(tc + 1) * 512],
                                       start=(c == 0), stop=(c == DC - 1))
                    mm.then_inc(cP.sem, 1)
                    for c in range(DC):
                        mm2 = te.matmul(out=pm_bc[:], lhsT=wx16[:, c * 96 + 64:c * 96 + 96],
                                        rhs=xnT[c][:, tc * 512:(tc + 1) * 512],
                                        start=(c == 0), stop=(c == DC - 1))
                    mm2.then_inc(cP.sem, 1)
                cP.n += 2
                P_XPROJ = cP.n

                @block.scalar
                def _(s):
                    s.wait_ge(cP.sem, P_XPROJ)
                    s.copy(out=xpa[0:64, tc * 512:(tc + 1) * 512],
                           in_=pm_dt[:]).then_inc(cS.sem, 1)
                    s.copy(out=bc16[:, tc * 512:(tc + 1) * 512],
                           in_=pm_bc[:]).then_inc(cS.sem, 1)
                cS.n += 2
                S_XPA = cS.n

                for c in range(DC):
                    pi = c % 2

                    @block.tensor
                    def _(te):
                        te.wait_ge(cS.sem, S_XPA)
                        if not (m == 0 and tc == 0 and c < 2):
                            te.wait_ge(cV.sem, pm_dtT_war[pi][0])
                        te.matmul(out=pm_dtT[pi][:], lhsT=wdtb16[:, c * 128:(c + 1) * 128],
                                  rhs=xpa[:, tc * 512:(tc + 1) * 512],
                                  start=True, stop=True).then_inc(cP.sem, 1)
                    cP.n += 1
                    P_DTT = cP.n

                    @block.scalar
                    def _(s):
                        s.wait_ge(cP.sem, P_DTT)
                        if not (m == 0 and tc == 0 and c == 0):
                            s.wait_ge(cV.sem, sp_war[0])
                        s.activation(out=sp_t[0][:], in_=pm_dtT[pi][:],
                                     func=AF.Abs).then_inc(cS.sem, 1)
                        s.wait_ge(cS.sem, cS.n + 1)
                        s.activation(out=sp_t[1][:], in_=sp_t[0][:],
                                     func=AF.Exp, scale=-1.0).then_inc(cS.sem, 1)
                        s.wait_ge(cS.sem, cS.n + 2)
                        s.activation(out=sp_t[2][:], in_=sp_t[1][:],
                                     func=AF.Ln, bias=1.0).then_inc(cS.sem, 1)
                    cS.n += 3
                    S_SPL = cS.n

                    @block.vector
                    def _(v):
                        v.wait_ge(cP.sem, P_DTT)
                        if not (m == 0 and tc == 0 and c == 0):
                            v.wait_ge(cV.sem, sp_war[0])
                        v.tensor_scalar(out=dtT[c][:, tc * 512:(tc + 1) * 512],
                                        in0=pm_dtT[pi][:], scalar1=0.0, scalar2=None,
                                        op0=AluOpType.max).then_inc(cV.sem, 1)
                        v.wait_ge(cS.sem, S_SPL)
                        v.wait_ge(cV.sem, cV.n + 1)
                        v.tensor_tensor(out=dtT[c][:, tc * 512:(tc + 1) * 512],
                                        in0=dtT[c][:, tc * 512:(tc + 1) * 512],
                                        in1=sp_t[2][:], op=AluOpType.add
                                        ).then_inc(cV.sem, 1)
                    cV.n += 2
                    sp_war[0] = cV.n
                    pm_dtT_war[pi][0] = cV.n

            for c in range(DC):
                @block.vector
                def _(v):
                    v.wait_ge(cV.sem, cV.n)
                    v.tensor_tensor(out=dtx[c][:], in0=dtT[c][:], in1=xnT[c][:],
                                    op=AluOpType.mult).then_inc(cV.sem, 1)
                cV.n += 1
            V_DTX = cV.n

            for si in range(DSTATE):
                @block.tensor
                def _(te):
                    te.wait_ge(cV.sem, V_DTX)
                    if not (m == 0 and si == 0):
                        te.wait_ge(cS.sem, bb_war[0])
                        te.wait_ge(cV.sem, bbu_war[0])
                    for h2 in range(NTC):
                        mm = te.matmul(out=pm_bb[:, h2 * 512:(h2 + 1) * 512],
                                       lhsT=selc16[:, si * 128:(si + 1) * 128],
                                       rhs=bc16[:, h2 * 512:(h2 + 1) * 512],
                                       start=True, stop=True)
                    mm.then_inc(cP.sem, 1)
                    for h2 in range(NTC):
                        mm2 = te.matmul(out=pm_cb[:, h2 * 512:(h2 + 1) * 512],
                                        lhsT=selc16[:, (16 + si) * 128:(16 + si + 1) * 128],
                                        rhs=bc16[:, h2 * 512:(h2 + 1) * 512],
                                        start=True, stop=True)
                    mm2.then_inc(cP.sem, 1)
                cP.n += 2
                P_BB = cP.n

                @block.scalar
                def _(s):
                    s.wait_ge(cP.sem, P_BB)
                    s.copy(out=bb_sb[:], in_=pm_bb[:]).then_inc(cS.sem, 1)
                    s.copy(out=cb_sb[:], in_=pm_cb[:]).then_inc(cS.sem, 1)
                cS.n += 2
                bb_war[0] = cS.n
                S_BB = cS.n

                for c in range(DC):
                    rot = (si * DC + c) % 2
                    @block.scalar
                    def _(s):
                        if not (m == 0 and si == 0 and c < 2):
                            s.wait_ge(cV.sem, ab_war[rot][0])
                        s.activation(out=abar[rot][:], in_=dtT[c][:], func=AF.Exp,
                                     scale=a_sb[:, c * 16 + si:c * 16 + si + 1]
                                     ).then_inc(cS.sem, 1)
                    cS.n += 1
                    S_AB = cS.n

                    @block.vector
                    def _(v):
                        v.wait_ge(cS.sem, S_BB)
                        if not (m == 0 and si == 0 and c < 2):
                            v.wait_ge(cV.sem, u_war[rot][0])
                        v.tensor_tensor(out=ubuf[rot][:], in0=dtx[c][:], in1=bb_sb[:],
                                        op=AluOpType.mult).then_inc(cV.sem, 1)
                        v.wait_ge(cS.sem, S_AB)
                        v.wait_ge(cV.sem, cV.n + 1)
                        ini = 0.0 if th == 0 else hcar[:, c * 16 + si:c * 16 + si + 1]
                        v.tensor_tensor_scan(out=hbuf[rot][:], data0=abar[rot][:],
                                             data1=ubuf[rot][:], initial=ini,
                                             op0=AluOpType.mult,
                                             op1=AluOpType.add).then_inc(cV.sem, 1)
                        v.wait_ge(cV.sem, cV.n + 2)
                        v.tensor_tensor(out=tmpb[rot][:], in0=hbuf[rot][:], in1=cb_sb[:],
                                        op=AluOpType.mult).then_inc(cV.sem, 1)
                        v.wait_ge(cV.sem, cV.n + 3)
                        if si == 0:
                            v.tensor_copy(out=ybuf[c][:], in_=tmpb[rot][:]
                                          ).then_inc(cV.sem, 1)
                        else:
                            v.tensor_tensor(out=ybuf[c][:], in0=ybuf[c][:],
                                            in1=tmpb[rot][:], op=AluOpType.add
                                            ).then_inc(cV.sem, 1)
                        if NTH > 1 and th == 0:
                            v.wait_ge(cV.sem, cV.n + 4)
                            v.tensor_copy(out=hcar[:, c * 16 + si:c * 16 + si + 1],
                                          in_=hbuf[rot][:, THS - 1:THS]
                                          ).then_inc(cV.sem, 1)
                    cV.n += 4 + (1 if (NTH > 1 and th == 0) else 0)
                    ab_war[rot][0] = cV.n
                    u_war[rot][0] = cV.n
                    bbu_war[0] = cV.n

            for c in range(DC):
                @block.vector
                def _(v):
                    v.wait_ge(cV.sem, cV.n)
                    if not (m == 0 and c == 0):
                        v.wait_ge(cW.sem, sout_war[0])
                    v.scalar_tensor_tensor(out=soutb[:], in0=xnT[c][:],
                                           scalar=dvec[:, c:c + 1], in1=ybuf[c][:],
                                           op0=AluOpType.mult,
                                           op1=AluOpType.add).then_inc(cV.sem, 1)
                cV.n += 1
                V_SOUT = cV.n

                @block.scalar
                def _(s):
                    s.wait_ge(cV.sem, V_SOUT)
                    s.dma_start(out=s16[b, c, :, t0:t0 + THS],
                                in_=soutb[:]).then_inc(cW.sem, 16)
                cW.n += 16
                sout_war[0] = cW.n

            prev_blk_end_v[0] = cV.n
            prev_blk_end_p[0] = cP.n
            prev_blk_end_w[0] = cW.n

        @block.gpsimd
        def _(g):
            g.wait_ge(cW.sem, cW.n)


    if stage >= 3:
        for t_ in reversed(mamba_ctxs):
            t_.__exit__(None, None, None)
        for t_ in reversed(p0_ctxs):
            t_.__exit__(None, None, None)

        NW = T // WIN

        fctxs = []
        def fsb(name, shape, dt):
            t = nc.sbuf_tensor(name, shape, dt)
            h = t.__enter__(); fctxs.append(t); return h
        def fps(name, shape, dt):
            t = nc.psum_tensor(name, shape, dt)
            h = t.__enter__(); fctxs.append(t); return h

        wq16 = [fsb(f"f_wq{c}", [128, 1536], BF16) for c in range(DC)]
        wo16h = [fsb(f"f_wo{h}", [64, 1024], BF16) for h in range(NHEAD)]
        ssq16 = fsb("f_ssq", [128, T], BF16); ssk16 = fsb("f_ssk", [128, T], BF16)
        csqf = fsb("f_csqf", [128, T], BF16); cskf = fsb("f_cskf", [128, T], BF16)
        rperm16 = fsb("f_rp", [128, 128], BF16)
        rpermq16 = fsb("f_rpq", [128, 128], BF16)
        rq16 = fsb("f_rq", [128, 128], BF16); rk16 = fsb("f_rk", [128, 128], BF16)
        qscol = fsb("f_qs", [128, 1], F32); kscol = fsb("f_ks", [128, 1], F32)
        hd2_16 = fsb("f_hd2", [128, 2], BF16); hd2T16 = fsb("f_hd2T", [2, 128], BF16)
        ones16 = fsb("f_ones", [128, 128], BF16)
        onesf32 = fsb("f_onesf32", [128, 128], F32)
        trim_f = fsb("f_trim", [128, 256], F32)
        trilo16 = fsb("f_trilo", [128, 128], BF16); triup16 = fsb("f_triup", [128, 128], BF16)
        lngc = fsb("f_lng", [128, DC], F32); lnbc = fsb("f_lnb", [128, DC], F32)
        b1ac = fsb("f_b1a", [128, 22], F32); b1gc = fsb("f_b1g", [128, 22], F32)
        b2c = fsb("f_b2", [128, DC], F32)

        resid = [fsb(f"f_res{c}", [128, 512], BF16) for c in range(DC)]
        hsh = [fsb(f"f_hsh{j}", [128, 512], BF16) for j in range(4)]
        sq_sb = fsb("f_sq", [128, 512], BF16)
        n2sb = fsb("f_n2", [2, 512], F32)
        rstd2 = fsb("f_rstd", [2, 512], BF16)
        rb_sb = fsb("f_rb", [128, 512], BF16)
        qn_sb = fsb("f_qn", [128, 512], BF16)
        t1_sb = fsb("f_t1", [128, 512], BF16)
        qT = [fsb(f"f_qT{p}", [128, 512], BF16) for p in range(4)]
        kTA = [fsb(f"f_kTA{p}", [128, 512], BF16) for p in range(4)]
        kTB = [fsb(f"f_kTB{p}", [128, 512], BF16) for p in range(4)]
        vtA = [fsb(f"f_vtA{t}", [128, 8 * 65], BF16) for t in range(4)]
        vtB = [fsb(f"f_vtB{t}", [128, 8 * 65], BF16) for t in range(4)]
        nm32 = fsb("f_nm32", [128, 34], F32)
        nm32b = fsb("f_nm32b", [128, 32], BF16)
        nmT_sb = fsb("f_nmT", [32, 128], BF16)
        mrowcat = fsb("f_mrow", [128, 4096], BF16)
        pT = [fsb(f"f_pT{k}", [128, 512], BF16) for k in range(8)]
        rdenb = fsb("f_rden", [128, 512], F32)
        rd_sb = fsb("f_rd", [64, 512], F32)
        oT = [fsb(f"f_oT{h}", [64, 512], BF16) for h in range(NHEAD)]
        strt = fsb("f_strt", [128, 512], BF16)
        sh2 = [fsb(f"f_sh2{c}", [128, 512], BF16) for c in range(DC)]
        res2 = [fsb(f"f_res2{c}", [128, 512], BF16) for c in range(DC)]
        murow = fsb("f_mu", [1, 512], BF16)
        m2row = fsb("f_m2", [1, 512], F32)
        varrow = fsb("f_var", [1, 512], F32)
        rstdrow = fsb("f_rsr", [1, 512], BF16)
        lntmp = fsb("f_lnt", [128, 512], F32)
        lntmp2 = fsb("f_lnt2", [128, 512], F32)
        h2 = [fsb(f"f_h2{c}", [128, 512], BF16) for c in range(DC)]
        w1sl = [fsb(f"f_w1s{p}", [128, 2048], BF16) for p in range(2)]
        gel_sb = fsb("f_gel", [128, 512], BF16)
        hidst = [fsb(f"f_hid{j}", [128, 512], BF16) for j in range(22)]
        w2sl = [fsb(f"f_w2s{p}", [128, 1024], BF16) for p in range(2)]
        xrl = fsb("f_xrl", [128, 1024], F32)
        obuf = fsb("f_obuf", [128, 1024], F32)

        mrow_dram = nc.dram_tensor("mrow_dram", [NB, NW, 4096], BF16, kind="Internal").ap()

        pf_ss = fps("pf_ss", [128, 1024], F32)
        pool = [fps(f"pf_p{i}", [128, 512], F32) for i in range(6)]

        fL = Ch(nc, "f_L"); fG = Ch(nc, "f_G"); fP = Ch(nc, "f_P")
        fS = Ch(nc, "f_S"); fV = Ch(nc, "f_V"); fW = Ch(nc, "f_W")
        fGm = Ch(nc, "f_Gm")
        bar = Ch(nc, "f_bar")

        def inc(instr, c):
            c.n += 1
            instr.then_inc(c.sem, 1)
        def dinc(instr, c):
            c.n += 16
            instr.then_inc(c.sem, 16)
        def wall(eng, *chs):
            for c in chs:
                eng.wait_ge(c.sem, c.n)

        @block.gpsimd
        def _(g):
            g.wait_ge(cL.sem, cL.n); g.wait_ge(cG.sem, cG.n)
            g.wait_ge(cP.sem, cP.n); g.wait_ge(cS.sem, cS.n)
            g.wait_ge(cV.sem, cV.n); g.wait_ge(cW.sem, cW.n)
            g.wait_ge(ch_sp.sem, ch_sp.n); g.wait_ge(ch_out.sem, ch_out.n)
            inc(g.memset(nm32[0:1, 0:1], 0.0), bar)
        BAR0 = bar.n

        @block.gpsimd
        def _(g):
            for dst, src_ in [(ssq16[:], ssq_in), (ssk16[:], ssk_in),
                              (rperm16[:], rperm_in), (rpermq16[:], rpermq_in),
                              (hd2_16[:], hd2_in),
                              (hd2T16[:], hd2T_in), (ones16[:], onesf_in),
                              (trim_f[:], trimadd_in), (trilo16[:], trilo_in),
                              (triup16[:], triup_in), (onesf32[:], onesf_in)]:
                dinc(g.dma_start(out=dst, in_=src_), fG)
            for t_ in range(4):
                inc(g.memset(vtA[t_][:, :].rearrange("p (h c) -> p h c", h=8)[:, :, 64:65], 1.0), fGm)
                inc(g.memset(vtB[t_][:, :].rearrange("p (h c) -> p h c", h=8)[:, :, 64:65], 1.0), fGm)
            for j in range(4):
                if j < 3:
                    inc(g.memset(pT[j][:, (j + 1) * 128:512], 0.0), fGm)
                if j > 0:
                    inc(g.memset(pT[4 + j][:, 0:j * 128], 0.0), fGm)
            inc(g.memset(mrowcat[:, 0:1], 0.0), fGm)

        def emit_layer(l, strm_in, strm_out, last_stage):
            @block.gpsimd
            def _(g):
                g.wait_ge(bar.sem, BAR0)
                wall(g, fP, fV, fS)
                for c in range(DC):
                    dinc(g.dma_start(out=wq16[c][:], in_=wqkv_in[l, c * 128:(c + 1) * 128, :]), fG)
                for h in range(NHEAD):
                    dinc(g.dma_start(out=wo16h[h][:], in_=wo_in[l, h * 64:(h + 1) * 64, :]), fG)
                dinc(g.dma_start(out=qscol[:], in_=qs_t[l]), fG)
                dinc(g.dma_start(out=kscol[:], in_=ks_t[l]), fG)
                dinc(g.dma_start(out=lngc[:], in_=lng_t[l]), fG)
                dinc(g.dma_start(out=lnbc[:], in_=lnb_t[l]), fG)
                dinc(g.dma_start(out=b1ac[:], in_=b1a_in[l]), fG)
                dinc(g.dma_start(out=b1gc[:], in_=b1g_in[l]), fG)
                dinc(g.dma_start(out=b2c[:], in_=b2_t[l]), fG)
                dinc(g.dma_start(out=csqf[:], in_=csq_in), fG)
                dinc(g.dma_start(out=cskf[:], in_=csk_in), fG)

            @block.vector
            def _(v):
                wall(v, fG, fGm)
                wall(v, fP, fS)
                inc(v.tensor_scalar(out=csqf[:], in0=csqf[:], scalar1=qscol[:, 0:1],
                                    scalar2=None, op0=AluOpType.mult), fV)
                inc(v.tensor_scalar(out=cskf[:], in0=cskf[:], scalar1=kscol[:, 0:1],
                                    scalar2=None, op0=AluOpType.mult), fV)
                inc(v.tensor_scalar(out=rq16[:], in0=rpermq16[:], scalar1=qscol[:, 0:1],
                                    scalar2=None, op0=AluOpType.mult), fV)
                inc(v.tensor_scalar(out=rk16[:], in0=rperm16[:], scalar1=kscol[:, 0:1],
                                    scalar2=None, op0=AluOpType.mult), fV)

            for b in range(NB):
                for w in range(NW):
                    t0 = w * WIN
                    kTc = kTA if w % 2 == 0 else kTB
                    kTp = kTB if w % 2 == 0 else kTA
                    vtc = vtA if w % 2 == 0 else vtB
                    vtp = vtB if w % 2 == 0 else vtA

                    @block.sync
                    def _(sync):
                        wall(sync, fP, fV)
                        for c in range(DC):
                            dinc(sync.dma_start(out=resid[c][:],
                                                in_=strm_in[b, c, :, t0:t0 + 512]), fL)
                        for j in range(4):
                            if w > 0:
                                dinc(sync.dma_start(out=hsh[j][:],
                                                    in_=strm_in[b, 4 + j, :, t0 - 1:t0 + 511]), fL)
                            else:
                                dinc(sync.dma_start(out=hsh[j][:, 1:512],
                                                    in_=strm_in[b, 4 + j, :, 0:511]), fL)

                    if w == 0:
                        @block.vector
                        def _(v):
                            wall(v, fP)
                            for j in range(4):
                                inc(v.memset(hsh[j][:, 0:1], 0.0), fV)

                    def srhs(kc):
                        return resid[kc][:] if kc < 4 else hsh[kc - 4][:]

                    for qx in range(2):
                        off = qx * 512
                        rmat = rq16 if qx == 0 else rk16
                        ctab = csqf if qx == 0 else cskf
                        stab = ssq16 if qx == 0 else ssk16
                        dstT = qT if qx == 0 else kTc
                        for hp in range(4):
                            @block.tensor
                            def _(te, qx=qx, hp=hp, off=off):
                                wall(te, fL, fV, fS, fGm)
                                for kc in range(DC):
                                    mm = te.matmul(out=pool[0][:],
                                                   lhsT=wq16[kc][:, off + hp * 128:off + (hp + 1) * 128],
                                                   rhs=srhs(kc),
                                                   start=(kc == 0), stop=(kc == DC - 1))
                                inc(mm, fP)

                            @block.scalar
                            def _(s):
                                wall(s, fP)
                                inc(s.activation(out=sq_sb[:], in_=pool[0][:],
                                                 func=AF.Square), fS)

                            @block.tensor
                            def _(te):
                                wall(te, fS)
                                inc(te.matmul(out=pool[2][0:2, :], lhsT=hd2_16[:],
                                              rhs=sq_sb[:], start=True, stop=True), fP)

                            @block.vector
                            def _(v):
                                wall(v, fP)
                                inc(v.tensor_scalar(out=n2sb[:], in0=pool[2][0:2, :],
                                                    scalar1=cv[0:2, 2:3], scalar2=None,
                                                    op0=AluOpType.max), fV)

                            @block.scalar
                            def _(s):
                                wall(s, fV)
                                inc(s.activation(out=n2sb[:], in_=n2sb[:], func=AF.Ln), fS)
                                s.wait_ge(fS.sem, fS.n)
                                inc(s.activation(out=rstd2[:], in_=n2sb[:], func=AF.Exp,
                                                 scale=-0.5), fS)

                            @block.tensor
                            def _(te):
                                wall(te, fS)
                                inc(te.matmul(out=pool[3][:], lhsT=hd2T16[:],
                                              rhs=rstd2[:], start=True, stop=True), fP)

                            @block.scalar
                            def _(s):
                                wall(s, fP)
                                inc(s.copy(out=rb_sb[:], in_=pool[3][:]), fS)

                            @block.vector
                            def _(v):
                                wall(v, fS)
                                inc(v.tensor_tensor(out=qn_sb[:], in0=pool[0][:],
                                                    in1=rb_sb[:], op=AluOpType.mult), fV)

                            @block.tensor
                            def _(te):
                                wall(te, fV)
                                inc(te.matmul(out=pool[4][:], lhsT=rmat[:],
                                              rhs=qn_sb[:], start=True, stop=True), fP)

                            @block.vector
                            def _(v, hp=hp):
                                wall(v, fP)
                                inc(v.tensor_tensor(out=t1_sb[:], in0=qn_sb[:],
                                                    in1=ctab[:, t0:t0 + 512],
                                                    op=AluOpType.mult), fV)
                                v.wait_ge(fV.sem, fV.n)
                                inc(v.tensor_tensor(out=qn_sb[:], in0=pool[4][:],
                                                    in1=stab[:, t0:t0 + 512],
                                                    op=AluOpType.mult), fV)
                                v.wait_ge(fV.sem, fV.n)
                                inc(v.tensor_tensor(out=dstT[hp][:], in0=t1_sb[:],
                                                    in1=qn_sb[:], op=AluOpType.add), fV)

                    for ts in range(4):
                        @block.tensor
                        def _(te, ts=ts):
                            wall(te, fV)
                            for kc in range(DC):
                                mm = te.matmul(out=pool[5][:],
                                               lhsT=srhs(kc)[:, ts * 128:(ts + 1) * 128],
                                               rhs=wq16[kc][:, 1024:1536],
                                               start=(kc == 0), stop=(kc == DC - 1))
                            inc(mm, fP)

                        @block.vector
                        def _(v, ts=ts):
                            wall(v, fP)
                            inc(v.tensor_copy(
                                out=vtc[ts][:, :].rearrange("p (h c) -> p h c", h=8)[:, :, 0:64],
                                in_=pool[5][:, :].rearrange("p (h c) -> p h c", h=8)),
                                fV)

                    for h in range(NHEAD):
                        hp, hi = h // 2, h % 2
                        base = hi * 64
                        for qc in range(4):
                            w1_ = 512 - qc * 128
                            w2_ = qc * 128 + 128

                            @block.tensor
                            def _(te, h=h, qc=qc, hp=hp, base=base, w1_=w1_, w2_=w2_):
                                wall(te, fV)
                                if w > 0:
                                    te.matmul(out=pf_ss[:, 0:w1_],
                                              lhsT=qT[hp][base:base + 64, qc * 128:(qc + 1) * 128],
                                              rhs=kTp[hp][base:base + 64, qc * 128:512],
                                              start=True, stop=True)
                                    mm = te.matmul(out=pf_ss[:, 512:512 + w2_],
                                                   lhsT=qT[hp][base:base + 64, qc * 128:(qc + 1) * 128],
                                                   rhs=kTc[hp][base:base + 64, 0:w2_],
                                                   start=True, stop=True)
                                else:
                                    mm = te.matmul(out=pf_ss[:, 0:w2_],
                                                   lhsT=qT[hp][base:base + 64, qc * 128:(qc + 1) * 128],
                                                   rhs=kTc[hp][base:base + 64, 0:w2_],
                                                   start=True, stop=True)
                                inc(mm, fP)

                            @block.vector
                            def _(v, h=h, qc=qc, w1_=w1_, w2_=w2_):
                                wall(v, fP)
                                if w > 0:
                                    inc(v.tensor_tensor(out=pf_ss[:, 0:128], in0=pf_ss[:, 0:128],
                                                        in1=trim_f[:, 0:128],
                                                        op=AluOpType.add), fV)
                                    inc(v.tensor_tensor(out=pf_ss[:, 512 + w2_ - 128:512 + w2_],
                                                        in0=pf_ss[:, 512 + w2_ - 128:512 + w2_],
                                                        in1=trim_f[:, 128:256],
                                                        op=AluOpType.add), fV)
                                    v.wait_ge(fV.sem, fV.n)
                                    inc(v.tensor_reduce(out=nm32[:, 32:33], in_=pf_ss[:, 0:w1_],
                                                        axis=AX.X, op=AluOpType.max), fV)
                                    inc(v.tensor_reduce(out=nm32[:, 33:34], in_=pf_ss[:, 512:512 + w2_],
                                                        axis=AX.X, op=AluOpType.max), fV)
                                    v.wait_ge(fV.sem, fV.n)
                                    inc(v.tensor_tensor(out=nm32[:, 32:33], in0=nm32[:, 32:33],
                                                        in1=nm32[:, 33:34], op=AluOpType.max), fV)
                                else:
                                    inc(v.tensor_tensor(out=pf_ss[:, w2_ - 128:w2_],
                                                        in0=pf_ss[:, w2_ - 128:w2_],
                                                        in1=trim_f[:, 128:256],
                                                        op=AluOpType.add), fV)
                                    v.wait_ge(fV.sem, fV.n)
                                    inc(v.tensor_reduce(out=nm32[:, 32:33], in_=pf_ss[:, 0:w2_],
                                                        axis=AX.X, op=AluOpType.max), fV)
                                v.wait_ge(fV.sem, fV.n)
                                inc(v.tensor_scalar(out=nm32[:, h * 4 + qc:h * 4 + qc + 1],
                                                    in0=nm32[:, 32:33], scalar1=-1.0,
                                                    scalar2=None, op0=AluOpType.mult), fV)

                    @block.vector
                    def _(v):
                        v.wait_ge(fV.sem, fV.n)
                        inc(v.tensor_copy(out=nm32b[:], in_=nm32[:, 0:32]), fV)

                    p2bf = pool[2][:].bitcast(BF16)

                    @block.tensor
                    def _(te):
                        wall(te, fV, fGm)
                        inc(te.transpose(out=p2bf[0:32, 0:128],
                                         in_=nm32b[:], identity=ident16[:]), fP)

                    @block.scalar
                    def _(s):
                        wall(s, fP)
                        inc(s.copy(out=nmT_sb[:], in_=p2bf[0:32, 0:128]), fS)

                    @block.gpsimd
                    def _(g, b=b, w=w):
                        wall(g, fS)
                        dinc(g.dma_start(out=mrow_dram[b, w, :].rearrange("(a c) -> a c", a=32),
                                         in_=nmT_sb[:]), fG)
                        g.wait_ge(fG.sem, fG.n)
                        dinc(g.dma_start(out=mrowcat[0:1, :], in_=mrow_dram[b, w, :][None, :]), fG)
                        dinc(g.dma_start(out=mrowcat[64:65, :], in_=mrow_dram[b, w, :][None, :]), fG)

                    for h in range(NHEAD):
                        hp, hi = h // 2, h % 2
                        base = hi * 64
                        kts = ([("p", j) for j in range(4)] if w > 0 else []) +                               [("c", j) for j in range(4)]
                        for ki, (src_, j) in enumerate(kts):
                            kt_idx = j if src_ == "p" else 4 + j
                            if src_ == "p":
                                r0, r1 = 0, (j + 1) * 128
                                ktsrc = kTp
                            else:
                                r0, r1 = j * 128, 512
                                ktsrc = kTc
                            pp = pool[ki % 2]

                            @block.tensor
                            def _(te, h=h, j=j, hp=hp, base=base, r0=r0, r1=r1,
                                  ktsrc=ktsrc, pp=pp):
                                wall(te, fS, fV, fG)
                                te.matmul(out=pp[:, r0:r1],
                                          lhsT=ktsrc[hp][base:base + 64, j * 128:(j + 1) * 128],
                                          rhs=qT[hp][base:base + 64, r0:r1],
                                          start=True, stop=False)
                                inc(te.matmul(out=pp[:, r0:r1],
                                              lhsT=ones16[base:base + 1, 0:128],
                                              rhs=mrowcat[base:base + 1, h * 512 + r0:h * 512 + r1],
                                              start=False, stop=True), fP)

                            @block.scalar
                            def _(s, kt_idx=kt_idx, r0=r0, r1=r1, pp=pp):
                                wall(s, fP)
                                inc(s.activation(out=pT[kt_idx][:, r0:r1],
                                                 in_=pp[:, r0:r1], func=AF.Exp), fS)

                            @block.vector
                            def _(v, kt_idx=kt_idx, src_=src_, j=j):
                                wall(v, fS)
                                dg = j * 128
                                msk = trilo16 if src_ == "p" else triup16
                                inc(v.tensor_tensor(out=pT[kt_idx][:, dg:dg + 128],
                                                    in0=pT[kt_idx][:, dg:dg + 128],
                                                    in1=msk[:], op=AluOpType.mult), fV)

                        @block.tensor
                        def _(te, h=h, w=w):
                            wall(te, fV)
                            first = True
                            for src_, j in kts:
                                kt_idx = j if src_ == "p" else 4 + j
                                vt_ = vtp if src_ == "p" else vtc
                                mm = te.matmul(out=pool[2][0:65, :],
                                               lhsT=vt_[j][:, h * 65:(h + 1) * 65],
                                               rhs=pT[kt_idx][:, 0:512],
                                               start=first, stop=(kt_idx == 7))
                                first = False
                            inc(mm, fP)

                        @block.vector
                        def _(v):
                            wall(v, fP)
                            inc(v.reciprocal(out=rdenb[64:65, :], in_=pool[2][64:65, :]), fV)

                        @block.tensor
                        def _(te):
                            wall(te, fV)
                            inc(te.matmul(out=pool[3][0:64, :], lhsT=onesf32[64:65, 0:64],
                                          rhs=rdenb[64:65, :], start=True, stop=True), fP)

                        @block.scalar
                        def _(s):
                            wall(s, fP)
                            inc(s.copy(out=rd_sb[:], in_=pool[3][0:64, :]), fS)

                        @block.vector
                        def _(v, h=h):
                            wall(v, fS)
                            inc(v.tensor_tensor(out=oT[h][:], in0=pool[2][0:64, :],
                                                in1=rd_sb[:], op=AluOpType.mult), fV)

                    for dc in range(DC):
                        @block.tensor
                        def _(te, dc=dc):
                            wall(te, fV)
                            for h in range(NHEAD):
                                mm = te.matmul(out=pool[4 + dc % 2][:],
                                               lhsT=wo16h[h][:, dc * 128:(dc + 1) * 128],
                                               rhs=oT[h][:], start=(h == 0),
                                               stop=(h == NHEAD - 1))
                            inc(mm, fP)

                        @block.vector
                        def _(v, dc=dc):
                            wall(v, fP, fW)
                            inc(v.tensor_tensor(out=strt[:], in0=resid[dc][:],
                                                in1=pool[4 + dc % 2][:],
                                                op=AluOpType.add), fV)

                        @block.scalar
                        def _(s, b=b, dc=dc, t0=t0):
                            wall(s, fV)
                            dinc(s.dma_start(out=strm_out[b, dc, :, t0:t0 + 512],
                                             in_=strt[:]), fW)

            if stage >= last_stage:
                _emit_phaseB(l, strm_out)

        def _emit_phaseB(l, strm):
            NU = T // 512
            for b in range(NB):
                for u in range(NU):
                    t0 = u * 512

                    @block.sync
                    def _(sync, b=b, t0=t0):
                        wall(sync, fP, fV, fS, fW)
                        for c in range(DC):
                            dinc(sync.dma_start(out=res2[c][:],
                                                in_=strm[b, c, :, t0:t0 + 512]), fL)
                        for c in range(4, DC):
                            if t0 > 0:
                                dinc(sync.dma_start(out=sh2[c][:],
                                                    in_=strm[b, c, :, t0 - 1:t0 + 511]), fL)
                            else:
                                dinc(sync.dma_start(out=sh2[c][:, 1:512],
                                                    in_=strm[b, c, :, 0:511]), fL)

                    @block.vector
                    def _(v, t0=t0):
                        wall(v, fL, fP)
                        if t0 == 0:
                            for c in range(4, DC):
                                inc(v.memset(sh2[c][:, 0:1], 0.0), fV)

                    shf = [res2[0], res2[1], res2[2], res2[3],
                           sh2[4], sh2[5], sh2[6], sh2[7]]

                    @block.scalar
                    def _(s):
                        wall(s, fV, fL)
                        for c in range(DC):
                            inc(s.activation(out=h2[c][:], in_=shf[c][:],
                                             func=AF.Square), fS)

                    @block.tensor
                    def _(te):
                        wall(te, fS)
                        for c in range(DC):
                            mm = te.matmul(out=pool[0][0:1, :], lhsT=ones16[:, 0:1],
                                           rhs=shf[c][:], start=(c == 0), stop=(c == DC - 1))
                        inc(mm, fP)
                        for c in range(DC):
                            mm = te.matmul(out=pool[1][0:1, :], lhsT=ones16[:, 0:1],
                                           rhs=h2[c][:], start=(c == 0), stop=(c == DC - 1))
                        inc(mm, fP)

                    @block.vector
                    def _(v):
                        wall(v, fP)
                        inc(v.tensor_scalar(out=murow[:], in0=pool[0][0:1, :],
                                            scalar1=1.0 / D, scalar2=None,
                                            op0=AluOpType.mult), fV)

                    @block.scalar
                    def _(s):
                        wall(s, fV)
                        inc(s.activation(out=m2row[:], in_=murow[:], func=AF.Square), fS)

                    @block.vector
                    def _(v):
                        wall(v, fS)
                        inc(v.scalar_tensor_tensor(out=varrow[:], in0=pool[1][0:1, :],
                                                   scalar=1.0 / D, in1=m2row[:],
                                                   op0=AluOpType.mult,
                                                   op1=AluOpType.subtract), fV)

                    @block.scalar
                    def _(s):
                        wall(s, fV)
                        inc(s.activation(out=varrow[:], in_=varrow[:], func=AF.Ln,
                                         bias=cv[0:1, 3:4]), fS)
                        s.wait_ge(fS.sem, fS.n)
                        inc(s.activation(out=rstdrow[:], in_=varrow[:], func=AF.Exp,
                                         scale=-0.5), fS)

                    @block.tensor
                    def _(te):
                        wall(te, fV, fS)
                        inc(te.matmul(out=pool[2][:], lhsT=ones16[0:1, :],
                                      rhs=murow[:], start=True, stop=True), fP)
                        inc(te.matmul(out=pool[3][:], lhsT=ones16[0:1, :],
                                      rhs=rstdrow[:], start=True, stop=True), fP)

                    for c in range(DC):
                        @block.vector
                        def _(v, c=c):
                            wall(v, fP)
                            inc(v.tensor_tensor(out=lntmp[:], in0=shf[c][:],
                                                in1=pool[2][:], op=AluOpType.subtract), fV)
                            v.wait_ge(fV.sem, fV.n)
                            inc(v.tensor_tensor(out=lntmp2[:], in0=lntmp[:],
                                                in1=pool[3][:], op=AluOpType.mult), fV)
                            v.wait_ge(fV.sem, fV.n)
                            inc(v.tensor_scalar(out=h2[c][:], in0=lntmp2[:],
                                                scalar1=lngc[:, c:c + 1],
                                                scalar2=lnbc[:, c:c + 1],
                                                op0=AluOpType.mult,
                                                op1=AluOpType.add), fV)

                    for ja in range(22):
                        aw = 128 if ja < 21 else HID - 21 * 128
                        pg = ja % 2

                        @block.gpsimd
                        def _(g, ja=ja, aw=aw, pg=pg, l=l):
                            wall(g, fP)
                            dinc(g.dma_start(
                                out=w1sl[pg][:, 0:8 * 128].rearrange("p (k j) -> p k j", k=8)[:, :, 0:aw],
                                in_=w1_in[l, :, ja * 128:ja * 128 + aw]
                                    .rearrange("(k p) j -> p k j", p=128)), fG)
                            dinc(g.dma_start(
                                out=w1sl[pg][:, 8 * 128:16 * 128].rearrange("p (k j) -> p k j", k=8)[:, :, 0:aw],
                                in_=w1_in[l, :, HID + ja * 128:HID + ja * 128 + aw]
                                    .rearrange("(k p) j -> p k j", p=128)), fG)

                        @block.tensor
                        def _(te, ja=ja, aw=aw, pg=pg):
                            wall(te, fG, fV, fS)
                            for kc in range(DC):
                                mm = te.matmul(out=pool[4][0:aw, :],
                                               lhsT=w1sl[pg][:, kc * 128:kc * 128 + aw],
                                               rhs=h2[kc][:], start=(kc == 0),
                                               stop=(kc == DC - 1))
                            inc(mm, fP)
                            for kc in range(DC):
                                mm = te.matmul(out=pool[5][0:aw, :],
                                               lhsT=w1sl[pg][:, 1024 + kc * 128:1024 + kc * 128 + aw],
                                               rhs=h2[kc][:], start=(kc == 0),
                                               stop=(kc == DC - 1))
                            inc(mm, fP)

                        @block.scalar
                        def _(s, ja=ja, aw=aw):
                            wall(s, fP)
                            inc(s.activation(out=gel_sb[0:aw, :], in_=pool[5][0:aw, :],
                                             func=AF.Gelu, bias=b1gc[0:aw, ja:ja + 1]), fS)

                        @block.vector
                        def _(v, ja=ja, aw=aw):
                            wall(v, fS)
                            inc(v.scalar_tensor_tensor(out=hidst[ja][0:aw, :],
                                                       in0=pool[4][0:aw, :],
                                                       scalar=b1ac[0:aw, ja:ja + 1],
                                                       in1=gel_sb[0:aw, :],
                                                       op0=AluOpType.add,
                                                       op1=AluOpType.mult), fV)

                    for grp in range(2):
                        for jc in range(22):
                            aw = 128 if jc < 21 else HID - 21 * 128
                            pg = jc % 2

                            @block.gpsimd
                            def _(g, jc=jc, aw=aw, pg=pg, l=l):
                                wall(g, fP)
                                dinc(g.dma_start(out=w2sl[pg][0:aw, :],
                                                 in_=w2_in[l, jc * 128:jc * 128 + aw, :]), fG)

                            for ci in range(4):
                                dc = grp * 4 + ci

                                @block.tensor
                                def _(te, jc=jc, aw=aw, pg=pg, dc=dc, ci=ci):
                                    wall(te, fG, fV)
                                    inc(te.matmul(out=pool[ci][:],
                                                  lhsT=w2sl[pg][0:aw, dc * 128:(dc + 1) * 128],
                                                  rhs=hidst[jc][0:aw, :],
                                                  start=(jc == 0), stop=(jc == 21)), fP)

                        for ci in range(4):
                            dc = grp * 4 + ci

                            @block.vector
                            def _(v, dc=dc, ci=ci):
                                wall(v, fP, fW)
                                inc(v.scalar_tensor_tensor(out=strt[:], in0=pool[ci][:],
                                                           scalar=b2c[:, dc:dc + 1],
                                                           in1=res2[dc][:],
                                                           op0=AluOpType.add,
                                                           op1=AluOpType.add), fV)

                            @block.scalar
                            def _(s, b=b, dc=dc, t0=t0):
                                wall(s, fV)
                                dinc(s.dma_start(out=strm[b, dc, :, t0:t0 + 512],
                                                 in_=strt[:]), fW)

        emit_layer(0, xn16, strmA, last_stage=4)
        if stage >= 5:
            emit_layer(1, strmA, strmB, last_stage=5)

        if stage >= 6:
            fOut = Ch(nc, "f_out")
            ostage = w1sl[0][:, 0:1024]
            p2bf3 = pool[2][:].bitcast(BF16)
            for b in range(NB):
                for u in range(T // 512):
                    t0 = u * 512
                    for dc in range(DC):
                        sti = sq_sb if dc % 2 == 0 else qn_sb
                        fti = t1_sb if dc % 2 == 0 else rb_sb

                        @block.sync
                        def _(sync, b=b, t0=t0, dc=dc, sti=sti, fti=fti):
                            wall(sync, fV, fS)
                            wall(sync, fW, cW)
                            dinc(sync.dma_start(out=sti[:], in_=s16[b, dc, :, t0:t0 + 512]), fL)
                            dinc(sync.dma_start(out=fti[:], in_=strmB[b, dc, :, t0:t0 + 512]), fL)

                        @block.scalar
                        def _(s, fti=fti):
                            wall(s, fL, fV)
                            inc(s.activation(out=gel_sb[:], in_=fti[:], func=AF.Silu), fS)

                        @block.vector
                        def _(v, dc=dc, sti=sti):
                            wall(v, fS, fP)
                            inc(v.tensor_tensor(out=h2[dc][:], in0=sti[:],
                                                in1=gel_sb[:], op=AluOpType.mult), fV)

                    for tsub in range(4):
                        @block.tensor
                        def _(te, tsub=tsub):
                            wall(te, fV, fS)
                            for dc in range(DC):
                                inc(te.transpose(out=p2bf3[:, dc * 128:(dc + 1) * 128],
                                                 in_=h2[dc][:, tsub * 128:(tsub + 1) * 128],
                                                 identity=ident16[:]), fP)

                        @block.scalar
                        def _(s):
                            wall(s, fP)
                            inc(s.copy(out=ostage, in_=p2bf3[:, 0:1024]), fS)

                        @block.sync
                        def _(sync, b=b, t0=t0, tsub=tsub):
                            wall(sync, fV)
                            dinc(sync.dma_start(out=xrl[:],
                                                in_=x_in[b, t0 + tsub * 128:t0 + (tsub + 1) * 128, :]), fL)

                        @block.vector
                        def _(v):
                            wall(v, fL, fS)
                            v.wait_ge(fOut.sem, fOut.n)
                            inc(v.tensor_tensor(out=obuf[:], in0=xrl[:],
                                                in1=ostage, op=AluOpType.add), fV)

                        @block.sync
                        def _(sync, b=b, t0=t0, tsub=tsub):
                            wall(sync, fV)
                            dinc(sync.dma_start(out=out[b, t0 + tsub * 128:t0 + (tsub + 1) * 128, :],
                                                in_=obuf[:]), fOut)

    @block.sync
    def _(sync):
        for c in [ch_hwl, ch_swl, ch_sp, ch_out]:
            if c.n:
                sync.wait_ge(c.sem, c.n)
        if stage >= 2:
            for c in [cL, cG, cW]:
                if c.n:
                    sync.wait_ge(c.sem, c.n)
        if stage >= 3:
            for c in [fL, fG, fW]:
                if c.n:
                    sync.wait_ge(c.sem, c.n)
        if stage >= 6:
            sync.wait_ge(fOut.sem, fOut.n)

    blk.__exit__(None, None, None)
    if stage >= 3:
        for t_ in reversed(fctxs):
            t_.__exit__(None, None, None)
    else:
        if stage >= 2:
            for t_ in reversed(mamba_ctxs):
                t_.__exit__(None, None, None)
        for t_ in reversed(p0_ctxs):
            t_.__exit__(None, None, None)
        if stage < 2:
            for t_ in ps_tt_ctx:
                t_.__exit__(None, None, None)
    for c in reversed(ctxs):
        c.__exit__(None, None, None)
    return nc



def _hd2_const():
    m = np.zeros((128, 2), np.float32)
    m[0:64, 0] = 1.0; m[64:128, 1] = 1.0
    return m


def _hd2T_const():
    m = np.zeros((2, 128), np.float32)
    m[0, 0:64] = 1.0; m[1, 64:128] = 1.0
    return m


def _rperm_const():
    m = np.zeros((128, 128), np.float32)
    for blk in range(2):
        o = blk * 64
        for i in range(32):
            m[o + i + 32, o + i] = -1.0
            m[o + i, o + i + 32] = 1.0
    return m


def _trimadd_const():
    m = np.zeros((128, 256), np.float32)
    for r in range(128):
        m[r, 0:r] = -1e30
        m[r, 128 + r + 1:256] = -1e30
    return m


def _trilo_const():
    return np.tril(np.ones((128, 128), np.float32))


def _triup_const():
    return np.triu(np.ones((128, 128), np.float32))


def _rot_tables(T):
    half = np.arange(0, DH, 2, dtype=np.float32)
    inv_freq = 1.0 / 10000.0 ** (half / DH)
    t = np.arange(T, dtype=np.float32)
    fr = t[:, None] * inv_freq[None, :]
    fr = np.concatenate([fr, fr], axis=-1)
    base = (half + 0.4 * DH) / (1.4 * DH)
    power = (t - T // 2) / float(WIN // 2)
    sc = base[None, :] ** power[:, None]
    sc = np.concatenate([sc, sc], axis=-1)
    cos, sin = np.cos(fr), np.sin(fr)
    csq = (cos * sc) * 8.0; ssq = (sin * sc)
    csk = (cos / sc); ssk = (sin / sc)
    def lay(m):
        r = np.transpose(m)
        return np.concatenate([r, r], axis=0).astype(np.float32)
    return lay(csq), lay(ssq), lay(csk), lay(ssk)


def _selc_const():
    m = np.zeros((32, 32 * 128), np.float32)
    for s in range(32):
        m[s, s * 128:(s + 1) * 128] = 1.0
    return m


def _prep_inputs(inputs):
    T = np.asarray(inputs["L0_attn_x"]).shape[1]
    f = lambda a: np.ascontiguousarray(a, dtype=np.float32)
    x = np.asarray(inputs["L0_attn_x"])
    W_dt = np.asarray(inputs["L0_attn_W_dt"])
    b_dt = np.asarray(inputs["L0_attn_b_dt"])
    A_log = np.asarray(inputs["L0_attn_A_log"])
    Dv = np.asarray(inputs["L0_attn_D"])

    W_dtb = np.concatenate([W_dt, b_dt[None, :]], axis=0)
    A_log_t = np.transpose(A_log.reshape(DC, 128, DSTATE), (1, 0, 2)).reshape(128, DC * DSTATE)
    D_t = np.transpose(Dv.reshape(DC, 128), (1, 0))

    qs = np.asarray(inputs["L0_ffn_q_scale"])
    ks = np.asarray(inputs["L0_ffn_k_scale"])
    qs_t = np.tile(qs, (1, 2)).reshape(2, 128, 1)
    ks_t = np.tile(ks, (1, 2)).reshape(2, 128, 1)
    lng = np.asarray(inputs["L0_ffn_ln_g"])
    lnb = np.asarray(inputs["L0_ffn_ln_b"])
    lng_t = np.transpose(lng.reshape(2, DC, 128), (0, 2, 1))
    lnb_t = np.transpose(lnb.reshape(2, DC, 128), (0, 2, 1))
    b1 = np.asarray(inputs["L0_ffn_b1"])
    b1a = b1[:, :HID]
    b1g = b1[:, HID:]
    pad = np.zeros((2, 22 * 128), np.float32)
    b1a_p = pad.copy(); b1a_p[:, :HID] = b1a
    b1g_p = pad.copy(); b1g_p[:, :HID] = b1g
    b1a_t = np.transpose(b1a_p.reshape(2, 22, 128), (0, 2, 1))
    b1g_t = np.transpose(b1g_p.reshape(2, 22, 128), (0, 2, 1))
    b2 = np.asarray(inputs["L0_ffn_b2"])
    b2_t = np.transpose(b2.reshape(2, DC, 128), (0, 2, 1))

    _RT = _rot_tables(T)
    common = {
        "W_x": f(inputs["L0_attn_W_x"]),
        "W_dtb": f(W_dtb),
        "A_log_t": f(A_log_t),
        "D_t": f(D_t),
        "wqkv": f(inputs["L0_ffn_wqkv"]),
        "qs_t": f(qs_t), "ks_t": f(ks_t),
        "wo": f(inputs["L0_ffn_wo"]),
        "lng_t": f(lng_t), "lnb_t": f(lnb_t),
        "w1": f(inputs["L0_ffn_w1"]), "b1a_t": f(b1a_t), "b1g_t": f(b1g_t),
        "w2": f(inputs["L0_ffn_w2"]), "b2_t": f(b2_t),
        "ident": np.eye(128, dtype=np.float32),
        "cvec": np.tile(np.array([LN_EPS_D2, LN_LOGD, 1e-24, 1e-5, 0, 0, 0, 0],
                                 np.float32), (128, 1)),
        "selc": _selc_const(),
        "hd2": _hd2_const(), "hd2T": _hd2T_const(), "onesf": np.ones((128, 128), np.float32),
        "rperm": _rperm_const(), "rpermq": _rperm_const() * 8.0,
        "trimadd": _trimadd_const(), "trilo": _trilo_const(), "triup": _triup_const(),
        "csq": _RT[0], "ssq": _RT[1], "csk": _RT[2], "ssk": _RT[3],
    }
    in_maps = []
    for c in range(NCORES):
        m = dict(common)
        m["x"] = f(x[c * NB:(c + 1) * NB])
        in_maps.append(m)
    return in_maps


def kernel(**inputs):
    T = np.asarray(inputs["L0_attn_x"]).shape[1]
    in_maps = _prep_inputs(inputs)
    nc = build(T=T, stage=6)
    res = run_bass_kernel_spmd(nc, in_maps, core_ids=list(range(NCORES)))
    outs = [np.asarray(r["out"], np.float32) for r in res.results]
    return np.concatenate(outs, axis=0)


# version 03
# speedup vs baseline: 1.6730x; 1.6730x over previous
import numpy as np
import ml_dtypes
import concourse.bass as bass
import concourse.mybir as mybir
from concourse.bass_utils import run_bass_kernel_spmd
from concourse.alu_op_type import AluOpType

F32 = mybir.dt.float32
BF16 = mybir.dt.bfloat16
AF = mybir.ActivationFunctionType
AX = mybir.AxisListType

NCORES = 8
D = 1024
DC = 8
NB = 2
DSTATE = 16
WIN = 512
NHEAD = 8
DH = 64
INNER = 512
HID = 2730
H1W = 5460
LN_EPS_D2 = 1e-5 * D * D
LN_LOGD = float(np.log(D))


class Ch:
    def __init__(self, nc, name):
        self._cm = nc.semaphore(name)
        self.sem = self._cm.__enter__()
        self.n = 0


def build(T=2048, stage=10, ths_override=None, evmap=None, race=False, dbg=False, mstop=99):
    recording = evmap is None
    EV = {} if recording else evmap

    def rec(name, ch):
        if recording:
            EV[name] = ch.n

    def wge(eng, ch, name):
        eng.wait_ge(ch.sem, 0 if recording else EV[name])

    nc = bass.Bass("TRN2", target_bir_lowering=False)
    nc.detect_race_conditions = race and not recording
    NTT = T // 128

    x_in = nc.dram_tensor("x", [NB, T, D], F32, kind="ExternalInput").ap()
    out = nc.dram_tensor("out", [NB, T, D], F32, kind="ExternalOutput").ap()

    w_x = nc.dram_tensor("W_x", [D, 96], F32, kind="ExternalInput").ap()
    w_dtb = nc.dram_tensor("W_dtb", [65, D], F32, kind="ExternalInput").ap()
    a_log_t = nc.dram_tensor("A_log_t", [128, DC * DSTATE], F32, kind="ExternalInput").ap()
    d_t = nc.dram_tensor("D_t", [128, DC], F32, kind="ExternalInput").ap()

    wqkv_in = nc.dram_tensor("wqkv", [2, D, 3 * INNER], F32, kind="ExternalInput").ap()
    qs_t = nc.dram_tensor("qs_t", [2, 128, 1], F32, kind="ExternalInput").ap()
    ks_t = nc.dram_tensor("ks_t", [2, 128, 1], F32, kind="ExternalInput").ap()
    wo_in = nc.dram_tensor("wo", [2, INNER, D], F32, kind="ExternalInput").ap()
    lng_t = nc.dram_tensor("lng_t", [2, 128, DC], F32, kind="ExternalInput").ap()
    lnb_t = nc.dram_tensor("lnb_t", [2, 128, DC], F32, kind="ExternalInput").ap()
    w1_in = nc.dram_tensor("w1", [2, D, H1W], F32, kind="ExternalInput").ap()
    b1a_in = nc.dram_tensor("b1a_t", [2, 128, 22], F32, kind="ExternalInput").ap()
    b1g_in = nc.dram_tensor("b1g_t", [2, 128, 22], F32, kind="ExternalInput").ap()
    w2_in = nc.dram_tensor("w2", [2, HID, D], F32, kind="ExternalInput").ap()
    b2_t = nc.dram_tensor("b2_t", [2, 128, DC], F32, kind="ExternalInput").ap()

    ident = nc.dram_tensor("ident", [128, 128], F32, kind="ExternalInput").ap()
    cvec = nc.dram_tensor("cvec", [128, 8], F32, kind="ExternalInput").ap()
    selc_in = nc.dram_tensor("selc", [32, 32 * 128], F32, kind="ExternalInput").ap()
    hd2_in = nc.dram_tensor("hd2", [128, 2], F32, kind="ExternalInput").ap()
    hd2T_in = nc.dram_tensor("hd2T", [2, 128], F32, kind="ExternalInput").ap()
    onesf_in = nc.dram_tensor("onesf", [128, 128], F32, kind="ExternalInput").ap()
    rperm_in = nc.dram_tensor("rperm", [128, 128], F32, kind="ExternalInput").ap()
    rpermq_in = nc.dram_tensor("rpermq", [128, 128], F32, kind="ExternalInput").ap()
    trimadd_in = nc.dram_tensor("trimadd", [128, 256], F32, kind="ExternalInput").ap()
    trilo_in = nc.dram_tensor("trilo", [128, 128], F32, kind="ExternalInput").ap()
    triup_in = nc.dram_tensor("triup", [128, 128], F32, kind="ExternalInput").ap()
    csq_in = nc.dram_tensor("csq", [128, T], F32, kind="ExternalInput").ap()
    ssq_in = nc.dram_tensor("ssq", [128, T], F32, kind="ExternalInput").ap()
    csk_in = nc.dram_tensor("csk", [128, T], F32, kind="ExternalInput").ap()
    ssk_in = nc.dram_tensor("ssk", [128, T], F32, kind="ExternalInput").ap()

    _ik = "ExternalOutput" if dbg else "Internal"
    xn16 = nc.dram_tensor("xn16", [NB, DC, 128, T], BF16, kind=_ik).ap()
    s16 = nc.dram_tensor("s16", [NB, DC, 128, T], BF16, kind=_ik).ap()
    strmA = nc.dram_tensor("strmA", [NB, DC, 128, T], BF16, kind=_ik).ap()
    strmB = nc.dram_tensor("strmB", [NB, DC, 128, T], BF16, kind=_ik).ap()
    NW = T // WIN
    mrow_dram = nc.dram_tensor("mrow_dram", [NB, NW, 4096], BF16, kind="Internal").ap()

    ctxs = []
    def sbuf(name, shape, dt):
        t = nc.sbuf_tensor(name, shape, dt)
        h = t.__enter__(); ctxs.append(t); return h

    ident16 = sbuf("ident16", [128, 128], BF16)
    cv = sbuf("cv", [128, 8], F32)

    p0_ctxs = []
    def sbuf_p0(name, shape, dt):
        t = nc.sbuf_tensor(name, shape, dt)
        h = t.__enter__(); p0_ctxs.append(t); return h
    xb = [sbuf_p0(f"xb{i}", [128, D], F32) for i in range(2)]
    sqb = [sbuf_p0(f"sqb{i}", [128, D], F32) for i in range(2)]
    stat = sbuf_p0("stat", [128, 16], F32)
    xnb = [sbuf_p0(f"xnb{i}", [128, D], BF16) for i in range(2)]
    xt_stage = [sbuf_p0(f"xt_stage{i}", [128, 256], BF16) for i in range(DC)]
    NPS_T = 16
    ps_tt_ctx = [nc.psum_tensor(f"ps_tt{i}", [128, 1024], BF16) for i in range(2)]
    ps_tt = [t.__enter__() for t in ps_tt_ctx]
    ps_tt_ctx = list(reversed(ps_tt_ctx))
    def ps_t_slot(k):
        sl = k % NPS_T
        return ps_tt[sl // 8][:, (sl % 8) * 128:(sl % 8) * 128 + 128]

    ch_hwl = Ch(nc, "hwl")
    ch_hwl2 = Ch(nc, "hwl2")
    ch_swl = Ch(nc, "swl")
    ch_sq = Ch(nc, "c_sq")
    ch_st = Ch(nc, "c_st")
    ch_pe = Ch(nc, "c_pe")
    ch_ev = Ch(nc, "c_ev")
    ch_sp = Ch(nc, "c_sp")

    blk = nc.Block()
    block = blk.__enter__()

    tiles = [(b, tt) for b in range(NB) for tt in range(NTT)]
    NTILE = len(tiles)
    VP, SP = 4, 4

    @block.sync
    def _(sync):
        for i, (b, tt) in enumerate(tiles):
            if i >= 2:
                sync.wait_ge(ch_st.sem, (i - 1) * VP)
            _c = ch_hwl if i % 2 == 0 else ch_hwl2
            sync.dma_start(out=xb[i % 2][:],
                           in_=x_in[b, tt * 128:(tt + 1) * 128, :]).then_inc(_c.sem, 16)
            _c.n += 16

    @block.gpsimd
    def _(g):
        g.dma_start(out=ident16[:], in_=ident).then_inc(ch_swl.sem, 16)
        ch_swl.n += 16
        g.dma_start(out=cv[:], in_=cvec).then_inc(ch_swl.sem, 16)
        ch_swl.n += 16

    @block.scalar
    def _(s):
        for i, (b, tt) in enumerate(tiles):
            _c = ch_hwl if i % 2 == 0 else ch_hwl2
            s.wait_ge(_c.sem, 16 * (i // 2 + 1))
            if i == 0:
                s.wait_ge(ch_swl.sem, 32)
            if i >= 2:
                s.wait_ge(ch_st.sem, (i - 1) * VP)
            s.activation(out=sqb[i % 2][:], in_=xb[i % 2][:],
                         func=AF.Square).then_inc(ch_sq.sem, 1)
            s.wait_ge(ch_st.sem, i * VP + 1)
            s.activation(out=stat[:, 2:3], in_=stat[:, 0:1],
                         func=AF.Identity, scale=1.0 / D)
            s.activation(out=stat[:, 6:7], in_=stat[:, 0:1],
                         func=AF.Square).then_inc(ch_sq.sem, 1)
            s.wait_ge(ch_st.sem, i * VP + 3)
            s.activation(out=stat[:, 8:9], in_=stat[:, 3:4], func=AF.Ln,
                         bias=cv[:, 0:1]).then_inc(ch_sq.sem, 1)
            s.wait_ge(ch_sq.sem, i * SP + 3)
            s.activation(out=stat[:, 4:5], in_=stat[:, 8:9], func=AF.Exp,
                         scale=-0.5, bias=cv[:, 1:2]).then_inc(ch_sq.sem, 1)
            ch_sq.n = (i + 1) * SP

            def _evac_tile(j):
                bj, ttj = tiles[j]
                s.wait_ge(ch_pe.sem, (j + 1) * DC)
                if j >= 2:
                    s.wait_ge(ch_sp.sem, 128 * (j // 2))
                for dc in range(DC):
                    k = j * DC + dc
                    s.copy(out=xt_stage[dc][:, (ttj % 2) * 128:(ttj % 2) * 128 + 128],
                           in_=ps_t_slot(k)).then_inc(ch_ev.sem, 1)
                    ch_ev.n += 1
                if ttj % 2 == 1:
                    s.wait_ge(ch_ev.sem, ch_ev.n)
                    for dc in range(DC):
                        s.dma_start(out=xn16[bj, dc, :, (ttj - 1) * 128:(ttj + 1) * 128],
                                    in_=xt_stage[dc][:]).then_inc(ch_sp.sem, 16)
                        ch_sp.n += 16
            if i >= 1:
                _evac_tile(i - 1)
            if i == NTILE - 1:
                _evac_tile(i)

    @block.vector
    def _(v):
        for i, (b, tt) in enumerate(tiles):
            _c = ch_hwl if i % 2 == 0 else ch_hwl2
            v.wait_ge(_c.sem, 16 * (i // 2 + 1))
            if i >= 1:
                v.wait_ge(ch_sq.sem, i * SP)
            v.tensor_reduce(out=stat[:, 0:1], in_=xb[i % 2][:], axis=AX.X,
                            op=AluOpType.add).then_inc(ch_st.sem, 1)
            v.wait_ge(ch_sq.sem, i * SP + 1)
            v.tensor_reduce(out=stat[:, 1:2], in_=sqb[i % 2][:], axis=AX.X,
                            op=AluOpType.add).then_inc(ch_st.sem, 1)
            v.wait_ge(ch_st.sem, i * VP + 2)
            v.wait_ge(ch_sq.sem, i * SP + 2)
            v.scalar_tensor_tensor(out=stat[:, 3:4], in0=stat[:, 1:2],
                                   scalar=float(D), in1=stat[:, 6:7],
                                   op0=AluOpType.mult,
                                   op1=AluOpType.subtract).then_inc(ch_st.sem, 1)
            v.wait_ge(ch_sq.sem, i * SP + 4)
            if i >= 2:
                v.wait_ge(ch_pe.sem, (i - 1) * DC)
            v.tensor_scalar(out=xnb[i % 2][:], in0=xb[i % 2][:],
                            scalar1=stat[:, 2:3], scalar2=stat[:, 4:5],
                            op0=AluOpType.subtract,
                            op1=AluOpType.mult).then_inc(ch_st.sem, 1)
            ch_st.n = (i + 1) * VP

    @block.tensor
    def _(te):
        te.wait_ge(ch_swl.sem, 32)
        for i, (b, tt) in enumerate(tiles):
            te.wait_ge(ch_st.sem, i * VP + 4)
            for dc in range(DC):
                k = i * DC + dc
                if k >= NPS_T:
                    te.wait_ge(ch_ev.sem, k - (NPS_T - 1))
                te.transpose(out=ps_t_slot(k),
                             in_=xnb[i % 2][:, dc * 128:(dc + 1) * 128],
                             identity=ident16[:]).then_inc(ch_pe.sem, 1)
                ch_pe.n += 1

    P0_SP_ALL = NTILE // 2 * DC * 16

    if stage >= 2:
        THS = ths_override or min(1024, T)
        NTH = T // THS
        NTC = THS // 512

        for t_ in ps_tt_ctx:
            t_.__exit__(None, None, None)
        ps_tt_freed = True

        mamba_ctxs = []
        def sbuf_m(name, shape, dt):
            t = nc.sbuf_tensor(name, shape, dt)
            h = t.__enter__(); mamba_ctxs.append(t); return h
        def psum_m(name, shape, dt):
            t = nc.psum_tensor(name, shape, dt)
            h = t.__enter__(); mamba_ctxs.append(t); return h

        wx16 = sbuf_m("wx16", [128, DC * 96], BF16)
        wdtb16 = sbuf_m("wdtb16", [65, D], BF16)
        selc16 = sbuf_m("selc16", [32, 32 * 128], BF16)
        alog_sb = sbuf_m("alog_sb", [128, 128], F32)
        a_sb = sbuf_m("a_sb", [128, 128], F32)
        dvec = sbuf_m("dvec", [128, DC], F32)
        xpa = sbuf_m("xpa", [65, THS], BF16)
        bc16 = sbuf_m("bc16", [32, THS], BF16)
        xnT = [sbuf_m(f"m_xnT{c}", [128, THS], BF16) for c in range(DC)]
        dtT = [sbuf_m(f"m_dtT{c}", [128, THS], F32) for c in range(DC)]
        dtx = [sbuf_m(f"m_dtx{c}", [128, THS], BF16) for c in range(DC)]
        ybuf = [sbuf_m(f"m_y{c}", [128, THS], F32) for c in range(DC)]
        sp_t = [sbuf_m(f"m_spt{i}", [128, 512], F32) for i in range(6)]
        abar = [sbuf_m(f"m_ab{i}", [128, THS], F32) for i in range(4)]
        ubuf = [sbuf_m(f"m_u{i}", [128, THS], BF16) for i in range(2)]
        hbuf = [sbuf_m(f"m_h{i}", [128, THS], BF16) for i in range(2)]
        tmpb = [sbuf_m(f"m_tmp{i}", [128, THS], BF16) for i in range(2)]
        bb_sb = [sbuf_m(f"m_bb{i}", [128, THS], BF16) for i in range(2)]
        cb_sb = [sbuf_m(f"m_cb{i}", [128, THS], BF16) for i in range(2)]
        hcar = sbuf_m("m_hcar", [128, 128], F32)
        soutb = [sbuf_m(f"m_sout{i}", [128, THS], BF16) for i in range(2)]

        pm_dt = psum_m("pm_dt", [64, 512], F32)
        pm_bc = psum_m("pm_bc", [32, 512], F32)
        pm_dtT = [psum_m(f"pm_dtT{i}", [128, 512], F32) for i in range(2)]
        pm_bb = psum_m("pm_bb", [128, THS], F32)
        pm_cb = psum_m("pm_cb", [128, THS], F32)

        cL = Ch(nc, "m_L"); cG = Ch(nc, "m_G"); cP = Ch(nc, "m_P")
        cS = Ch(nc, "m_S"); cV = Ch(nc, "m_V")
        cWs = [Ch(nc, "m_W0"), Ch(nc, "m_W1")]
        cGm = Ch(nc, "m_Gm")

        @block.gpsimd
        def _(g):
            for c in range(DC):
                g.dma_start(out=wx16[:, c * 96:(c + 1) * 96],
                            in_=w_x[c * 128:(c + 1) * 128, :]).then_inc(cG.sem, 16)
                cG.n += 16
            for ap_dst, ap_src in [(wdtb16[:], w_dtb), (selc16[:], selc_in),
                                   (alog_sb[:], a_log_t), (dvec[:], d_t)]:
                g.dma_start(out=ap_dst, in_=ap_src).then_inc(cG.sem, 16)
                cG.n += 16
            g.memset(xpa[64:65, :], 1.0).then_inc(cGm.sem, 1)
            cGm.n += 1
        G_W = cG.n

        @block.scalar
        def _(s):
            s.wait_ge(cG.sem, G_W)
            s.activation(out=a_sb[:], in_=alog_sb[:], func=AF.Exp).then_inc(cS.sem, 1)
        cS.n += 1
        S_EXPA = cS.n

        @block.vector
        def _(v):
            v.wait_ge(cS.sem, S_EXPA)
            v.tensor_scalar(out=a_sb[:], in0=a_sb[:], scalar1=-1.0, scalar2=None,
                            op0=AluOpType.mult).then_inc(cV.sem, 1)
        cV.n += 1
        V_NEGA = cV.n

        blocks2 = [(b, th) for b in range(NB) for th in range(NTH)]
        NKK = NTC * DC
        for m, (b, th) in enumerate(blocks2):
            t0 = th * THS
            mm_ = f"M{m}_"
            pm_ = f"M{m-1}_" if m > 0 else None

            @block.sync
            def _(sync):
                if m == 0:
                    sync.wait_ge(ch_sp.sem, P0_SP_ALL)
                else:
                    wge(sync, cV, pm_ + "vend")
                    wge(sync, cP, pm_ + "pend")
                for c in range(DC):
                    sync.dma_start(out=xnT[c][:], in_=xn16[b, c, :, t0:t0 + THS]
                                   ).then_inc(cL.sem, 16)
                    cL.n += 16
            rec(mm_ + "loads", cL)

            @block.tensor
            def _(te):
                wge(te, cL, mm_ + "loads")
                if m == 0:
                    te.wait_ge(cV.sem, V_NEGA)
                    te.wait_ge(cGm.sem, 1)
                    te.wait_ge(ch_ev.sem, NTILE * DC)
                for tc in range(NTC):
                    if m > 0 or tc > 0:
                        wge(te, cS, (mm_ + f"xw{tc-1}") if tc > 0 else (pm_ + f"xw{NTC-1}"))
                    for c in range(DC):
                        mm1 = te.matmul(out=pm_dt[:], lhsT=wx16[:, c * 96:c * 96 + 64],
                                        rhs=xnT[c][:, tc * 512:(tc + 1) * 512],
                                        start=(c == 0), stop=(c == DC - 1))
                    mm1.then_inc(cP.sem, 1)
                    cP.n += 1
                    for c in range(DC):
                        mm2 = te.matmul(out=pm_bc[:], lhsT=wx16[:, c * 96 + 64:c * 96 + 96],
                                        rhs=xnT[c][:, tc * 512:(tc + 1) * 512],
                                        start=(c == 0), stop=(c == DC - 1))
                    mm2.then_inc(cP.sem, 1)
                    cP.n += 1
                    rec(mm_ + f"xproj{tc}", cP)

            for tc in range(NTC if mstop >= 21 else 0):
                for c in range(DC):
                    @block.tensor
                    def _(te, tc=tc, c=c):
                        kk = tc * DC + c
                        pi = kk % 2
                        gk = m * NKK + kk
                        if c == 0:
                            wge(te, cS, mm_ + f"xpa{tc}")
                        if gk >= 2:
                            pgk = gk - 2
                            pmm = f"M{pgk // NKK}_"
                            ptc, pc = divmod(pgk % NKK, DC)
                            if mstop >= 23:
                                wge(te, cV, pmm + f"spw{ptc}_{pc}")
                            if mstop >= 22:
                                wge(te, cS, pmm + f"spl{ptc}_{pc}")
                        te.matmul(out=pm_dtT[pi][:], lhsT=wdtb16[:, c * 128:(c + 1) * 128],
                                  rhs=xpa[:, tc * 512:(tc + 1) * 512],
                                  start=True, stop=True).then_inc(cP.sem, 1)
                        cP.n += 1
                        rec(mm_ + f"dtT{tc}_{c}", cP)

            for si in range(DSTATE if mstop >= 30 else 0):
                @block.tensor
                def _(te, si=si):
                    gsi = m * DSTATE + si
                    wge(te, cV, mm_ + "dtxall")
                    if gsi >= 1:
                        pgs = gsi - 1
                        pmm = f"M{pgs // DSTATE}_"
                        psi = pgs % DSTATE
                        wge(te, cS, pmm + f"bbev{psi}")
                    for h2_ in range(NTC):
                        mmb = te.matmul(out=pm_bb[:, h2_ * 512:(h2_ + 1) * 512],
                                        lhsT=selc16[:, si * 128:(si + 1) * 128],
                                        rhs=bc16[:, h2_ * 512:(h2_ + 1) * 512],
                                        start=True, stop=True)
                    mmb.then_inc(cP.sem, 1)
                    cP.n += 1
                    for h2_ in range(NTC):
                        mmc = te.matmul(out=pm_cb[:, h2_ * 512:(h2_ + 1) * 512],
                                        lhsT=selc16[:, (16 + si) * 128:(17 + si) * 128],
                                        rhs=bc16[:, h2_ * 512:(h2_ + 1) * 512],
                                        start=True, stop=True)
                    mmc.then_inc(cP.sem, 1)
                    cP.n += 1
                    rec(mm_ + f"bb{si}", cP)
            rec(mm_ + "pend", cP)

            @block.scalar
            def _(s):
                for tc in range(NTC):
                    wge(s, cP, mm_ + f"xproj{tc}")
                    s.copy(out=xpa[0:64, tc * 512:(tc + 1) * 512],
                           in_=pm_dt[:]).then_inc(cS.sem, 1)
                    cS.n += 1
                    s.copy(out=bc16[:, tc * 512:(tc + 1) * 512],
                           in_=pm_bc[:]).then_inc(cS.sem, 1)
                    cS.n += 1
                    rec(mm_ + f"xw{tc}", cS)
                    rec(mm_ + f"xpa{tc}", cS)
                    for c in range(DC if mstop >= 22 else 0):
                        kk = tc * DC + c
                        pi = kk % 2
                        gk = m * NKK + kk
                        sb3 = 3 * (kk % 2)
                        wge(s, cP, mm_ + f"dtT{tc}_{c}")
                        if gk >= 2 and mstop >= 23:
                            pgk = gk - 2
                            pmm = f"M{pgk // NKK}_"
                            ptc, pc = divmod(pgk % NKK, DC)
                            wge(s, cV, pmm + f"spw{ptc}_{pc}")
                        s.activation(out=sp_t[sb3][:], in_=pm_dtT[pi][:],
                                     func=AF.Abs).then_inc(cS.sem, 1)
                        cS.n += 1
                        s.wait_ge(cS.sem, cS.n)
                        s.activation(out=sp_t[sb3 + 1][:], in_=sp_t[sb3][:],
                                     func=AF.Exp, scale=-1.0).then_inc(cS.sem, 1)
                        cS.n += 1
                        s.wait_ge(cS.sem, cS.n)
                        s.activation(out=sp_t[sb3 + 2][:], in_=sp_t[sb3 + 1][:],
                                     func=AF.Ln, bias=1.0).then_inc(cS.sem, 1)
                        cS.n += 1
                        rec(mm_ + f"spl{tc}_{c}", cS)
                for si in range(DSTATE if mstop >= 30 else 0):
                    pr = si % 2
                    gsi = m * DSTATE + si
                    wge(s, cP, mm_ + f"bb{si}")
                    if gsi >= 2 and mstop >= 40:
                        pgs = gsi - 2
                        pmm = f"M{pgs // DSTATE}_"
                        wge(s, cV, pmm + f"bbuse{pgs % DSTATE}")
                    s.copy(out=bb_sb[pr][:], in_=pm_bb[:]).then_inc(cS.sem, 1)
                    cS.n += 1
                    s.copy(out=cb_sb[pr][:], in_=pm_cb[:]).then_inc(cS.sem, 1)
                    cS.n += 1
                    rec(mm_ + f"bbev{si}", cS)
                    for c in range(DC):
                        gr = (m * DSTATE + si) * DC + c
                        rot4 = gr % 4
                        if gr >= 4 and mstop >= 40:
                            pgr = gr - 4
                            pmm = f"M{pgr // (DSTATE * DC)}_"
                            r_ = pgr % (DSTATE * DC)
                            wge(s, cV, pmm + f"abw{r_ // DC}_{r_ % DC}")
                        s.activation(out=abar[rot4][:], in_=dtT[c][:], func=AF.Exp,
                                     scale=a_sb[:, c * 16 + si:c * 16 + si + 1]
                                     ).then_inc(cS.sem, 1)
                        cS.n += 1
                        rec(mm_ + f"ab{si}_{c}", cS)
            rec(mm_ + "send", cS)

            @block.vector
            def _(v):
                for tc in range(NTC if mstop >= 23 else 0):
                    for c in range(DC):
                        pi = (tc * DC + c) % 2
                        wge(v, cP, mm_ + f"dtT{tc}_{c}")
                        wge(v, cS, mm_ + f"spl{tc}_{c}")
                        v.tensor_scalar(out=dtT[c][:, tc * 512:(tc + 1) * 512],
                                        in0=pm_dtT[pi][:], scalar1=0.0, scalar2=None,
                                        op0=AluOpType.max).then_inc(cV.sem, 1)
                        cV.n += 1
                        rec(mm_ + f"dtw{tc}_{c}", cV)
                        v.wait_ge(cV.sem, cV.n)
                        v.tensor_tensor(out=dtT[c][:, tc * 512:(tc + 1) * 512],
                                        in0=dtT[c][:, tc * 512:(tc + 1) * 512],
                                        in1=sp_t[3 * ((tc * DC + c) % 2) + 2][:],
                                        op=AluOpType.add
                                        ).then_inc(cV.sem, 1)
                        cV.n += 1
                        rec(mm_ + f"spw{tc}_{c}", cV)
                for c in range(DC if mstop >= 24 else 0):
                    v.wait_ge(cV.sem, cV.n)
                    v.tensor_tensor(out=dtx[c][:], in0=dtT[c][:], in1=xnT[c][:],
                                    op=AluOpType.mult).then_inc(cV.sem, 1)
                    cV.n += 1
                rec(mm_ + "dtxall", cV)

            for si in range(DSTATE if mstop >= 40 else 0):
                @block.vector
                def _(v, si=si):
                    pr = si % 2
                    for c in range(DC):
                        gr = (m * DSTATE + si) * DC + c
                        rot2 = gr % 2
                        rot4 = gr % 4
                        wge(v, cS, mm_ + f"bbev{si}")
                        v.wait_ge(cV.sem, cV.n)
                        v.tensor_tensor(out=ubuf[rot2][:], in0=dtx[c][:],
                                        in1=bb_sb[pr][:], op=AluOpType.mult
                                        ).then_inc(cV.sem, 1)
                        cV.n += 1
                        wge(v, cS, mm_ + f"ab{si}_{c}")
                        v.wait_ge(cV.sem, cV.n)
                        ini = 0.0 if th == 0 else hcar[:, c * 16 + si:c * 16 + si + 1]
                        v.tensor_tensor_scan(out=hbuf[rot2][:], data0=abar[rot4][:],
                                             data1=ubuf[rot2][:], initial=ini,
                                             op0=AluOpType.mult,
                                             op1=AluOpType.add).then_inc(cV.sem, 1)
                        cV.n += 1
                        rec(mm_ + f"abw{si}_{c}", cV)
                        v.wait_ge(cV.sem, cV.n)
                        v.tensor_tensor(out=tmpb[rot2][:], in0=hbuf[rot2][:],
                                        in1=cb_sb[pr][:], op=AluOpType.mult
                                        ).then_inc(cV.sem, 1)
                        cV.n += 1
                        v.wait_ge(cV.sem, cV.n)
                        if si == 0:
                            v.tensor_copy(out=ybuf[c][:], in_=tmpb[rot2][:]
                                          ).then_inc(cV.sem, 1)
                        else:
                            v.tensor_tensor(out=ybuf[c][:], in0=ybuf[c][:],
                                            in1=tmpb[rot2][:], op=AluOpType.add
                                            ).then_inc(cV.sem, 1)
                        cV.n += 1
                        if th < NTH - 1:
                            v.wait_ge(cV.sem, cV.n)
                            v.tensor_copy(out=hcar[:, c * 16 + si:c * 16 + si + 1],
                                          in_=hbuf[rot2][:, THS - 1:THS]
                                          ).then_inc(cV.sem, 1)
                            cV.n += 1
                    rec(mm_ + f"bbuse{si}", cV)

            @block.vector
            def _(v):
                for c in range(DC if mstop >= 50 else 0):
                    v.wait_ge(cV.sem, cV.n)
                    gsc = m * DC + c
                    if gsc >= 2:
                        pgs = gsc - 2
                        wge(v, cWs[c % 2], f"M{pgs // DC}_spw{pgs % DC}w")
                    v.scalar_tensor_tensor(out=soutb[c % 2][:], in0=xnT[c][:],
                                           scalar=dvec[:, c:c + 1], in1=ybuf[c][:],
                                           op0=AluOpType.mult,
                                           op1=AluOpType.add).then_inc(cV.sem, 1)
                    cV.n += 1
                    rec(mm_ + f"sout{c}", cV)
            rec(mm_ + "vend", cV)

            @block.scalar
            def _(s):
                for c in range(DC if mstop >= 50 else 0):
                    wge(s, cV, mm_ + f"sout{c}")
                    s.dma_start(out=s16[b, c, :, t0:t0 + THS],
                                in_=soutb[c % 2][:]).then_inc(cWs[c % 2].sem, 16)
                    cWs[c % 2].n += 16
                    rec(mm_ + f"spw{c}w", cWs[c % 2])
        MAMBA_W0, MAMBA_W1 = cWs[0].n, cWs[1].n

    if stage >= 3:
        for t_ in reversed(mamba_ctxs):
            t_.__exit__(None, None, None)
        for t_ in reversed(p0_ctxs):
            t_.__exit__(None, None, None)

        fctxs = []
        def fsb(name, shape, dt):
            t = nc.sbuf_tensor(name, shape, dt)
            h = t.__enter__(); fctxs.append(t); return h
        def fps(name, shape, dt):
            t = nc.psum_tensor(name, shape, dt)
            h = t.__enter__(); fctxs.append(t); return h

        wq16 = [fsb(f"f_wq{c}", [128, 1536], BF16) for c in range(DC)]
        wo16h = [fsb(f"f_wo{h}", [64, 1024], BF16) for h in range(NHEAD)]
        ssq16 = fsb("f_ssq", [128, T], BF16); ssk16 = fsb("f_ssk", [128, T], BF16)
        csqf = fsb("f_csqf", [128, T], BF16); cskf = fsb("f_cskf", [128, T], BF16)
        rperm16 = fsb("f_rp", [128, 128], BF16)
        rpermq16 = fsb("f_rpq", [128, 128], BF16)
        rq16 = fsb("f_rq", [128, 128], BF16); rk16 = fsb("f_rk", [128, 128], BF16)
        qscol = fsb("f_qs", [128, 1], F32); kscol = fsb("f_ks", [128, 1], F32)
        hd2_16 = fsb("f_hd2", [128, 2], BF16); hd2T16 = fsb("f_hd2T", [2, 128], BF16)
        ones16 = fsb("f_ones", [128, 128], BF16)
        onesf32 = fsb("f_onesf32", [128, 128], F32)
        trim_f = fsb("f_trim", [128, 256], F32)
        trilo16 = fsb("f_trilo", [128, 128], BF16); triup16 = fsb("f_triup", [128, 128], BF16)
        lngc = fsb("f_lng", [128, DC], F32); lnbc = fsb("f_lnb", [128, DC], F32)
        b1ac = fsb("f_b1a", [128, 22], F32); b1gc = fsb("f_b1g", [128, 22], F32)
        b2c = fsb("f_b2", [128, DC], F32)

        resid = [fsb(f"f_res{c}", [128, 512], BF16) for c in range(DC)]
        hsh = [fsb(f"f_hsh{j}", [128, 512], BF16) for j in range(4)]
        sq_sb = [fsb(f"f_sq{i}", [128, 512], BF16) for i in range(2)]
        n2sb = fsb("f_n2", [2, 512], F32)
        rstd2 = [fsb(f"f_rstd{i}", [2, 512], BF16) for i in range(2)]
        rb_sb = [fsb(f"f_rb{i}", [128, 512], BF16) for i in range(2)]
        qn_sb = [fsb(f"f_qn{i}", [128, 512], BF16) for i in range(2)]
        t1_sb = [fsb(f"f_t1{i}", [128, 512], BF16) for i in range(2)]
        qT = [fsb(f"f_qT{p}", [128, 512], BF16) for p in range(4)]
        kTA = [fsb(f"f_kTA{p}", [128, 512], BF16) for p in range(4)]
        kTB = [fsb(f"f_kTB{p}", [128, 512], BF16) for p in range(4)]
        vtA = [fsb(f"f_vtA{t}", [128, 8 * 65], BF16) for t in range(4)]
        vtB = [fsb(f"f_vtB{t}", [128, 8 * 65], BF16) for t in range(4)]
        nm32 = fsb("f_nm32", [128, 34], F32)
        nm32b = fsb("f_nm32b", [128, 32], BF16)
        nmT_sb = fsb("f_nmT", [32, 128], BF16)
        mrowcat = fsb("f_mrow", [128, 4096], BF16)
        pT = [fsb(f"f_pT{k}", [128, 512], BF16) for k in range(8)]
        rdenb = fsb("f_rden", [128, 512], F32)
        rd_sb = fsb("f_rd", [64, 512], F32)
        oT = [fsb(f"f_oT{h}", [64, 512], BF16) for h in range(NHEAD)]
        strt = [fsb(f"f_strt{i}", [128, 512], BF16) for i in range(2)]
        sh2 = {c: fsb(f"f_sh2{c}", [128, 512], BF16) for c in range(4, DC)}
        shcolb = fsb("f_shcol", [128, 32], BF16)
        res2 = [fsb(f"f_res2{c}", [128, 512], BF16) for c in range(DC)]
        murow = fsb("f_mu", [1, 512], BF16)
        m2row = fsb("f_m2", [1, 512], F32)
        varrow = fsb("f_var", [1, 512], F32)
        rstdrow = fsb("f_rsr", [1, 512], BF16)
        h2 = [fsb(f"f_h2{c}", [128, 512], BF16) for c in range(DC)]
        w1sl = [fsb(f"f_w1s{p}", [128, 2048], BF16) for p in range(2)]
        gel_sb = [fsb(f"f_gel{i}", [128, 512], BF16) for i in range(2)]
        hidst = [fsb(f"f_hid{j}", [128, 512], BF16) for j in range(22)]
        w2sl = [fsb(f"f_w2s{p}", [128, 1024], BF16) for p in range(2)]
        xrl = fsb("f_xrl", [128, 1024], F32)
        obuf = fsb("f_obuf", [128, 1024], F32)
        lntmp_ap = rdenb[:]
        lntmp2_ap = obuf[:, 0:512]

        pool = [fps(f"pf_p{i}", [128, 512], F32) for i in range(6)]
        pf_ss = fps("pf_ss", [128, 1024], F32)

        fL = [Ch(nc, "f_L0"), Ch(nc, "f_L1")]
        fG = Ch(nc, "f_G")
        fGw1 = [Ch(nc, "f_Gw1a"), Ch(nc, "f_Gw1b")]
        fGw2 = [Ch(nc, "f_Gw2a"), Ch(nc, "f_Gw2b")]
        fP = Ch(nc, "f_P"); fS = Ch(nc, "f_S"); fV = Ch(nc, "f_V")
        fWs = [Ch(nc, "f_W0"), Ch(nc, "f_W1")]
        fGm = Ch(nc, "f_Gm")
        bar = Ch(nc, "f_bar")

        @block.gpsimd
        def _(g):
            for c in [cL, cG, cP, cS, cV, cWs[0], cWs[1]]:
                g.wait_ge(c.sem, c.n)
            g.wait_ge(ch_sp.sem, ch_sp.n)
            g.memset(nm32[0:1, 0:1], 0.0).then_inc(bar.sem, 1)
            bar.n += 1
        BAR0 = bar.n

        @block.gpsimd
        def _(g):
            g.wait_ge(bar.sem, BAR0)
            for dst, src_ in [(ssq16[:], ssq_in), (ssk16[:], ssk_in),
                              (rperm16[:], rperm_in), (rpermq16[:], rpermq_in),
                              (hd2_16[:], hd2_in), (hd2T16[:], hd2T_in),
                              (ones16[:], onesf_in), (onesf32[:], onesf_in),
                              (trim_f[:], trimadd_in), (trilo16[:], trilo_in),
                              (triup16[:], triup_in)]:
                g.dma_start(out=dst, in_=src_).then_inc(fG.sem, 16)
                fG.n += 16
            for t_ in range(4):
                g.memset(vtA[t_][:, :].rearrange("p (h c) -> p h c", h=8)[:, :, 64:65],
                         1.0).then_inc(fGm.sem, 1)
                fGm.n += 1
                g.memset(vtB[t_][:, :].rearrange("p (h c) -> p h c", h=8)[:, :, 64:65],
                         1.0).then_inc(fGm.sem, 1)
                fGm.n += 1
            for j in range(4):
                if j < 3:
                    g.memset(pT[j][:, (j + 1) * 128:512], 0.0).then_inc(fGm.sem, 1)
                    fGm.n += 1
                if j > 0:
                    g.memset(pT[4 + j][:, 0:j * 128], 0.0).then_inc(fGm.sem, 1)
                    fGm.n += 1
        FGM_ALL = fGm.n

        def _emit_attn_window(l, b, w, strm_in, strm_out):
            t0 = w * WIN
            kTc = kTA if w % 2 == 0 else kTB
            kTp = kTB if w % 2 == 0 else kTA
            vtc = vtA if w % 2 == 0 else vtB
            vtp = vtB if w % 2 == 0 else vtA
            wp = f"F{l}A{b}_{w}_"
            idx = b * NW + w
            gwp = None
            if idx >= 1:
                b1_, w1_ = divmod(idx - 1, NW)
                gwp = f"F{l}A{b1_}_{w1_}_"
            g2p = None
            if idx >= 2:
                b2_, w2_ = divmod(idx - 2, NW)
                g2p = f"F{l}A{b2_}_{w2_}_"
            qku = [("q", hp) for hp in range(4)] + [("k", hp) for hp in range(4)]
            kts = ([("p", j) for j in range(4)] if w > 0 else []) + \
                  [("c", j) for j in range(4)]

            @block.sync
            def _(sync):
                if gwp is not None:
                    wge(sync, fP, gwp + "pe_end")
                    wge(sync, fV, gwp + "v_end")
                elif l > 0:
                    wge(sync, fWs[0], f"F{l-1}_B_done_W0")
                    wge(sync, fWs[1], f"F{l-1}_B_done_W1")
                    wge(sync, fV, f"F{l-1}_B_v_end")
                    wge(sync, fP, f"F{l-1}_B_pe_end")
                else:
                    sync.wait_ge(bar.sem, BAR0)
                fl = fL[idx % 2]
                for c in range(DC):
                    sync.dma_start(out=resid[c][:],
                                   in_=strm_in[b, c, :, t0:t0 + 512]).then_inc(fl.sem, 16)
                    fl.n += 16
                for j in range(4):
                    if w > 0:
                        sync.dma_start(out=hsh[j][:],
                                       in_=strm_in[b, 4 + j, :, t0 - 1:t0 + 511]
                                       ).then_inc(fl.sem, 16)
                    else:
                        sync.dma_start(out=hsh[j][:, 1:512],
                                       in_=strm_in[b, 4 + j, :, 0:511]).then_inc(fl.sem, 16)
                    fl.n += 16
            rec(wp + "loads", fL[idx % 2])

            def srhs(kc):
                return resid[kc][:] if kc < 4 else hsh[kc - 4][:]

            @block.gpsimd
            def _(g):
                wge(g, fS, wp + "nmT")
                if gwp is not None:
                    wge(g, fP, gwp + "st_all")
                g.dma_start(out=mrow_dram[b, w, :].rearrange("(a c) -> a c", a=32),
                            in_=nmT_sb[:]).then_inc(fG.sem, 16)
                fG.n += 16
                g.wait_ge(fG.sem, fG.n)
                g.dma_start(out=mrowcat[0:1, :],
                            in_=mrow_dram[b, w, :][None, :]).then_inc(fG.sem, 16)
                fG.n += 16
                g.dma_start(out=mrowcat[64:65, :],
                            in_=mrow_dram[b, w, :][None, :]).then_inc(fG.sem, 16)
                fG.n += 16
            rec(wp + "mrow", fG)

            qkbank = [pool[0][:], pool[1][:], pf_ss[:, 0:512]]

            @block.tensor
            def _(te):
                wge(te, fL[idx % 2], wp + "loads")
                if idx == 0:
                    te.wait_ge(fGm.sem, FGM_ALL)
                    wge(te, fG, f"F{l}_wl")
                    wge(te, fV, f"F{l}_tabs_done")
                if w == 0:
                    wge(te, fV, wp + "hshz")
                for k in range(len(qku) + 3):
                    if k < 8:
                        qx, hp = qku[k]
                        if k >= 3:
                            wge(te, fV, wp + f"qn{k-3}")
                        off = 0 if qx == "q" else 512
                        for kc in range(DC):
                            mmq = te.matmul(out=qkbank[k % 3],
                                            lhsT=wq16[kc][:, off + hp * 128:off + (hp + 1) * 128],
                                            rhs=srhs(kc),
                                            start=(kc == 0), stop=(kc == DC - 1))
                        mmq.then_inc(fP.sem, 1)
                        fP.n += 1
                        rec(wp + f"qk{k}", fP)
                    if 0 <= k - 1 < 8:
                        kk = k - 1
                        wge(te, fS, wp + f"sq{kk}")
                        if kk >= 1:
                            wge(te, fV, wp + f"nmax{kk-1}")
                        te.matmul(out=pool[2][0:2, :], lhsT=hd2_16[:],
                                  rhs=sq_sb[kk % 2][:], start=True,
                                  stop=True).then_inc(fP.sem, 1)
                        fP.n += 1
                        rec(wp + f"n2{kk}", fP)
                    if 0 <= k - 2 < 8:
                        kk = k - 2
                        wge(te, fS, wp + f"rstd{kk}")
                        if kk >= 1:
                            wge(te, fS, wp + f"rbc{kk-1}")
                        te.matmul(out=pool[3][:], lhsT=hd2T16[:],
                                  rhs=rstd2[kk % 2][:], start=True,
                                  stop=True).then_inc(fP.sem, 1)
                        fP.n += 1
                        rec(wp + f"rbb{kk}", fP)
                    if 0 <= k - 3 < 8:
                        kk = k - 3
                        qx, hp = qku[kk]
                        wge(te, fV, wp + f"qn{kk}")
                        if kk >= 1:
                            wge(te, fV, wp + f"t2{kk-1}")
                        te.matmul(out=pool[4][:], lhsT=(rq16 if qx == "q" else rk16)[:],
                                  rhs=qn_sb[kk % 2][:], start=True,
                                  stop=True).then_inc(fP.sem, 1)
                        fP.n += 1
                        rec(wp + f"rot{kk}", fP)
                for ts in range(4):
                    if ts >= 1:
                        wge(te, fV, wp + f"vt{ts-1}")
                    for kc in range(DC):
                        mmv = te.matmul(out=pool[5][:],
                                        lhsT=srhs(kc)[:, ts * 128:(ts + 1) * 128],
                                        rhs=wq16[kc][:, 1024:1536],
                                        start=(kc == 0), stop=(kc == DC - 1))
                    mmv.then_inc(fP.sem, 1)
                    fP.n += 1
                    rec(wp + f"vp{ts}", fP)
                for h in range(NHEAD):
                    hp, hi = h // 2, h % 2
                    base = hi * 64
                    wge(te, fV, wp + "qT" + str(("q", hp)))
                    wge(te, fV, wp + "qT" + str(("k", hp)))
                    for qc in range(4):
                        w1_ = 512 - qc * 128
                        w2_ = qc * 128 + 128
                        if (h, qc) != (0, 0):
                            wge(te, fV, wp + (f"stats{h}_{qc-1}" if qc > 0
                                              else f"stats{h-1}_3"))
                        else:
                            wge(te, fV, wp + "qn7")
                        if w > 0:
                            te.matmul(out=pf_ss[:, 0:w1_],
                                      lhsT=qT[hp][base:base + 64, qc * 128:(qc + 1) * 128],
                                      rhs=kTp[hp][base:base + 64, qc * 128:512],
                                      start=True, stop=True)
                            te.matmul(out=pf_ss[:, 512:512 + w2_],
                                      lhsT=qT[hp][base:base + 64, qc * 128:(qc + 1) * 128],
                                      rhs=kTc[hp][base:base + 64, 0:w2_],
                                      start=True, stop=True).then_inc(fP.sem, 1)
                        else:
                            te.matmul(out=pf_ss[:, 0:w2_],
                                      lhsT=qT[hp][base:base + 64, qc * 128:(qc + 1) * 128],
                                      rhs=kTc[hp][base:base + 64, 0:w2_],
                                      start=True, stop=True).then_inc(fP.sem, 1)
                        fP.n += 1
                        rec(wp + f"ss{h}_{qc}", fP)
                p2bf = pool[2][:].bitcast(BF16)
                wge(te, fV, wp + "nm32b")
                te.transpose(out=p2bf[0:32, 0:128], in_=nm32b[:],
                             identity=ident16[:]).then_inc(fP.sem, 1)
                fP.n += 1
                rec(wp + "nmT_pe", fP)
                for h in range(NHEAD):
                    hp, hi = h // 2, h % 2
                    base = hi * 64
                    for ki, (src_, j) in enumerate(kts):
                        if src_ == "p":
                            r0, r1 = 0, (j + 1) * 128
                            ktsrc = kTp
                        else:
                            r0, r1 = j * 128, 512
                            ktsrc = kTc
                        pp = pool[ki % 2]
                        if h > 0 or ki >= 2:
                            ph, pki = (h, ki - 2) if ki >= 2 else (h - 1, len(kts) - 2 + ki)
                            wge(te, fS, wp + f"exp{ph}_{pki}")
                        if h == 0 and ki < 2:
                            wge(te, fG, wp + "mrow")
                        te.matmul(out=pp[:, r0:r1],
                                  lhsT=ktsrc[hp][base:base + 64, j * 128:(j + 1) * 128],
                                  rhs=qT[hp][base:base + 64, r0:r1],
                                  start=True, stop=False)
                        te.matmul(out=pp[:, r0:r1],
                                  lhsT=ones16[base:base + 1, 0:128],
                                  rhs=mrowcat[base:base + 1, h * 512 + r0:h * 512 + r1],
                                  start=False, stop=True).then_inc(fP.sem, 1)
                        fP.n += 1
                        rec(wp + f"st{h}_{ki}", fP)
                    if h == 0:
                        wge(te, fV, wp + "vt3")
                    else:
                        wge(te, fV, wp + f"oT{h-1}")
                    first = True
                    for ki, (src_, j) in enumerate(kts):
                        kt_idx = j if src_ == "p" else 4 + j
                        vt_ = vtp if src_ == "p" else vtc
                        wge(te, fV, wp + f"pm{h}_{ki}")
                        mmp = te.matmul(out=pool[5][0:65, :],
                                        lhsT=vt_[j][:, h * 65:(h + 1) * 65],
                                        rhs=pT[kt_idx][:, 0:512],
                                        start=first, stop=(ki == len(kts) - 1))
                        first = False
                    mmp.then_inc(fP.sem, 1)
                    fP.n += 1
                    rec(wp + f"pv{h}", fP)
                    wge(te, fS, wp + f"rden{h}")
                    te.matmul(out=pool[3][0:64, :], lhsT=onesf32[64:65, 0:64],
                              rhs=rdenb[64:65, :], start=True,
                              stop=True).then_inc(fP.sem, 1)
                    fP.n += 1
                    rec(wp + f"rdb{h}", fP)
                rec(wp + "st_all", fP)
                for dc in range(DC):
                    wge(te, fV, wp + "oT7")
                    if dc >= 2:
                        wge(te, fV, wp + f"strm{dc-2}")
                    for h in range(NHEAD):
                        mmw = te.matmul(out=pool[dc % 2][:],
                                        lhsT=wo16h[h][:, dc * 128:(dc + 1) * 128],
                                        rhs=oT[h][:], start=(h == 0),
                                        stop=(h == NHEAD - 1))
                    mmw.then_inc(fP.sem, 1)
                    fP.n += 1
                    rec(wp + f"wo{dc}", fP)
            rec(wp + "pe_end", fP)

            @block.scalar
            def _(s):
                for k in range(len(qku) + 2):
                    if k < 8:
                        wge(s, fP, wp + f"qk{k}")
                        if k >= 2:
                            wge(s, fP, wp + f"n2{k-2}")
                        s.activation(out=sq_sb[k % 2][:], in_=qkbank[k % 3],
                                     func=AF.Square).then_inc(fS.sem, 1)
                        fS.n += 1
                        rec(wp + f"sq{k}", fS)
                    if 0 <= k - 1 < 8:
                        kk = k - 1
                        wge(s, fV, wp + f"nmax{kk}")
                        s.activation(out=n2sb[:], in_=n2sb[:], func=AF.Ln
                                     ).then_inc(fS.sem, 1)
                        fS.n += 1
                        s.wait_ge(fS.sem, fS.n)
                        if kk >= 2:
                            wge(s, fP, wp + f"rbb{kk-2}")
                        s.activation(out=rstd2[kk % 2][:], in_=n2sb[:], func=AF.Exp,
                                     scale=-0.5).then_inc(fS.sem, 1)
                        fS.n += 1
                        rec(wp + f"rstd{kk}", fS)
                    if 0 <= k - 2 < 8:
                        kk = k - 2
                        wge(s, fP, wp + f"rbb{kk}")
                        if kk >= 2:
                            wge(s, fV, wp + f"qn{kk-2}")
                        s.copy(out=rb_sb[kk % 2][:], in_=pool[3][:]).then_inc(fS.sem, 1)
                        fS.n += 1
                        rec(wp + f"rbc{kk}", fS)
                wge(s, fP, wp + "nmT_pe")
                if gwp is not None:
                    wge(s, fG, gwp + "mrow")
                p2bf = pool[2][:].bitcast(BF16)
                s.copy(out=nmT_sb[:], in_=p2bf[0:32, 0:128]).then_inc(fS.sem, 1)
                fS.n += 1
                rec(wp + "nmT", fS)
                for h in range(NHEAD):
                    for ki, (src_, j) in enumerate(kts):
                        kt_idx = j if src_ == "p" else 4 + j
                        r0, r1 = (0, (j + 1) * 128) if src_ == "p" else (j * 128, 512)
                        pp = pool[ki % 2]
                        wge(s, fP, wp + f"st{h}_{ki}")
                        if h > 0:
                            wge(s, fP, wp + f"pv{h-1}")
                        s.activation(out=pT[kt_idx][:, r0:r1], in_=pp[:, r0:r1],
                                     func=AF.Exp).then_inc(fS.sem, 1)
                        fS.n += 1
                        rec(wp + f"exp{h}_{ki}", fS)
                    wge(s, fP, wp + f"pv{h}")
                    s.activation(out=rdenb[64:65, :], in_=pool[5][64:65, :],
                                 func=AF.Ln).then_inc(fS.sem, 1)
                    fS.n += 1
                    s.wait_ge(fS.sem, fS.n)
                    s.activation(out=rdenb[64:65, :], in_=rdenb[64:65, :],
                                 func=AF.Exp, scale=-1.0).then_inc(fS.sem, 1)
                    fS.n += 1
                    rec(wp + f"rden{h}", fS)
                    wge(s, fP, wp + f"rdb{h}")
                    s.copy(out=rd_sb[:], in_=pool[3][0:64, :]).then_inc(fS.sem, 1)
                    fS.n += 1
                    rec(wp + f"rdc{h}", fS)
                for dc in range(DC):
                    wge(s, fV, wp + f"strm{dc}")
                    s.dma_start(out=strm_out[b, dc, :, t0:t0 + 512],
                                in_=strt[dc % 2][:]).then_inc(fWs[dc % 2].sem, 16)
                    fWs[dc % 2].n += 16
                    rec(wp + f"sp{dc}", fWs[dc % 2])
            rec(wp + "s_end", fS)

            @block.vector
            def _(v):
                if w == 0:
                    wge(v, fL[idx % 2], wp + "loads")
                    if gwp is not None:
                        wge(v, fP, gwp + "pe_end")
                    for j in range(4):
                        v.memset(hsh[j][:, 0:1], 0.0).then_inc(fV.sem, 1)
                        fV.n += 1
                    rec(wp + "hshz", fV)
                for k in range(len(qku) + 3):
                    if 0 <= k - 1 < 8:
                        kk = k - 1
                        wge(v, fP, wp + f"n2{kk}")
                        if kk >= 1:
                            wge(v, fS, wp + f"rstd{kk-1}")
                        v.tensor_scalar(out=n2sb[:], in0=pool[2][0:2, :],
                                        scalar1=cv[0:2, 2:3], scalar2=None,
                                        op0=AluOpType.max).then_inc(fV.sem, 1)
                        fV.n += 1
                        rec(wp + f"nmax{kk}", fV)
                    if 0 <= k - 2 < 8:
                        kk = k - 2
                        wge(v, fS, wp + f"rbc{kk}")
                        v.wait_ge(fV.sem, fV.n)
                        v.tensor_tensor(out=qn_sb[kk % 2][:], in0=qkbank[kk % 3],
                                        in1=rb_sb[kk % 2][:], op=AluOpType.mult
                                        ).then_inc(fV.sem, 1)
                        fV.n += 1
                        rec(wp + f"qn{kk}", fV)
                    if 0 <= k - 3 < 8:
                        kk = k - 3
                        qx, hp = qku[kk]
                        ctab = csqf if qx == "q" else cskf
                        stab = ssq16 if qx == "q" else ssk16
                        dstT = qT if qx == "q" else kTc
                        v.wait_ge(fV.sem, fV.n)
                        v.tensor_tensor(out=t1_sb[kk % 2][:], in0=qn_sb[kk % 2][:],
                                        in1=ctab[:, t0:t0 + 512],
                                        op=AluOpType.mult).then_inc(fV.sem, 1)
                        fV.n += 1
                        wge(v, fP, wp + f"rot{kk}")
                        v.wait_ge(fV.sem, fV.n)
                        v.tensor_tensor(out=qn_sb[kk % 2][:], in0=pool[4][:],
                                        in1=stab[:, t0:t0 + 512],
                                        op=AluOpType.mult).then_inc(fV.sem, 1)
                        fV.n += 1
                        rec(wp + f"t2{kk}", fV)
                        v.wait_ge(fV.sem, fV.n)
                        if g2p is not None and qx == "k":
                            wge(v, fP, g2p + "pe_end")
                        v.tensor_tensor(out=dstT[hp][:], in0=t1_sb[kk % 2][:],
                                        in1=qn_sb[kk % 2][:], op=AluOpType.add
                                        ).then_inc(fV.sem, 1)
                        fV.n += 1
                        rec(wp + "qT" + str((qx, hp)), fV)
                for ts in range(4):
                    wge(v, fP, wp + f"vp{ts}")
                    if g2p is not None:
                        wge(v, fP, g2p + "pe_end")
                    v.tensor_copy(
                        out=vtc[ts][:, :].rearrange("p (h c) -> p h c", h=8)[:, :, 0:64],
                        in_=pool[5][:, :].rearrange("p (h c) -> p h c", h=8)
                        ).then_inc(fV.sem, 1)
                    fV.n += 1
                    rec(wp + f"vt{ts}", fV)
                for h in range(NHEAD):
                    for qc in range(4):
                        w1_ = 512 - qc * 128
                        w2_ = qc * 128 + 128
                        wge(v, fP, wp + f"ss{h}_{qc}")
                        if w > 0:
                            v.tensor_tensor(out=pf_ss[:, 0:128], in0=pf_ss[:, 0:128],
                                            in1=trim_f[:, 0:128], op=AluOpType.add
                                            ).then_inc(fV.sem, 1)
                            fV.n += 1
                            v.tensor_tensor(out=pf_ss[:, 512 + w2_ - 128:512 + w2_],
                                            in0=pf_ss[:, 512 + w2_ - 128:512 + w2_],
                                            in1=trim_f[:, 128:256], op=AluOpType.add
                                            ).then_inc(fV.sem, 1)
                            fV.n += 1
                            v.wait_ge(fV.sem, fV.n)
                            v.tensor_reduce(out=nm32[:, 32:33], in_=pf_ss[:, 0:w1_],
                                            axis=AX.X, op=AluOpType.max).then_inc(fV.sem, 1)
                            fV.n += 1
                            v.tensor_reduce(out=nm32[:, 33:34], in_=pf_ss[:, 512:512 + w2_],
                                            axis=AX.X, op=AluOpType.max).then_inc(fV.sem, 1)
                            fV.n += 1
                            v.wait_ge(fV.sem, fV.n)
                            v.tensor_tensor(out=nm32[:, 32:33], in0=nm32[:, 32:33],
                                            in1=nm32[:, 33:34], op=AluOpType.max
                                            ).then_inc(fV.sem, 1)
                            fV.n += 1
                        else:
                            v.tensor_tensor(out=pf_ss[:, w2_ - 128:w2_],
                                            in0=pf_ss[:, w2_ - 128:w2_],
                                            in1=trim_f[:, 128:256], op=AluOpType.add
                                            ).then_inc(fV.sem, 1)
                            fV.n += 1
                            v.wait_ge(fV.sem, fV.n)
                            v.tensor_reduce(out=nm32[:, 32:33], in_=pf_ss[:, 0:w2_],
                                            axis=AX.X, op=AluOpType.max).then_inc(fV.sem, 1)
                            fV.n += 1
                        v.wait_ge(fV.sem, fV.n)
                        v.tensor_scalar(out=nm32[:, h * 4 + qc:h * 4 + qc + 1],
                                        in0=nm32[:, 32:33], scalar1=-1.0, scalar2=None,
                                        op0=AluOpType.mult).then_inc(fV.sem, 1)
                        fV.n += 1
                        rec(wp + f"stats{h}_{qc}", fV)
                v.wait_ge(fV.sem, fV.n)
                v.tensor_copy(out=nm32b[:], in_=nm32[:, 0:32]).then_inc(fV.sem, 1)
                fV.n += 1
                rec(wp + "nm32b", fV)
                for h in range(NHEAD):
                    for ki, (src_, j) in enumerate(kts):
                        kt_idx = j if src_ == "p" else 4 + j
                        dg = j * 128
                        msk = trilo16 if src_ == "p" else triup16
                        wge(v, fS, wp + f"exp{h}_{ki}")
                        v.tensor_tensor(out=pT[kt_idx][:, dg:dg + 128],
                                        in0=pT[kt_idx][:, dg:dg + 128],
                                        in1=msk[:], op=AluOpType.mult).then_inc(fV.sem, 1)
                        fV.n += 1
                        rec(wp + f"pm{h}_{ki}", fV)
                    wge(v, fS, wp + f"rdc{h}")
                    v.tensor_tensor(out=oT[h][:], in0=pool[5][0:64, :],
                                    in1=rd_sb[:], op=AluOpType.mult).then_inc(fV.sem, 1)
                    fV.n += 1
                    rec(wp + f"oT{h}", fV)
                    if h == NHEAD - 1:
                        rec(wp + "oT7", fV)
                for dc in range(DC):
                    wge(v, fP, wp + f"wo{dc}")
                    if dc >= 2:
                        wge(v, fWs[dc % 2], wp + f"sp{dc-2}")
                    elif gwp is not None:
                        wge(v, fWs[dc % 2], gwp + f"sp{dc+6}")
                    v.tensor_tensor(out=strt[dc % 2][:], in0=resid[dc][:],
                                    in1=pool[dc % 2][:], op=AluOpType.add
                                    ).then_inc(fV.sem, 1)
                    fV.n += 1
                    rec(wp + f"strm{dc}", fV)
                    if dc >= 4 and w < NW - 1:
                        sci = (b * NW + (w + 1)) * 4 + (dc - 4)
                        v.wait_ge(fV.sem, fV.n)
                        v.tensor_copy(out=shcolb[:, sci:sci + 1],
                                      in_=strt[dc % 2][:, 511:512]).then_inc(fV.sem, 1)
                        fV.n += 1
            rec(wp + "v_end", fV)

        def _emit_ffnmid_unit(l, b, u, strm):
            t0 = u * 512
            up = f"F{l}B{b}_{u}_"
            NU = T // 512
            idx = b * NU + u
            pup = None
            if idx > 0:
                pb, pu = divmod(idx - 1, NU)
                pup = f"F{l}B{pb}_{pu}_"

            @block.sync
            def _(sync):
                if pup is None:
                    wge(sync, fWs[0], f"F{l}_A_done_W0")
                    wge(sync, fWs[1], f"F{l}_A_done_W1")
                    wge(sync, fV, f"F{l}A{NB-1}_{NW-1}_v_end")
                else:
                    wge(sync, fV, pup + "v_end")
                    wge(sync, fP, pup + "pe_end")
                fl = fL[idx % 2]
                for c in range(DC):
                    sync.dma_start(out=res2[c][:],
                                   in_=strm[b, c, :, t0:t0 + 512]).then_inc(fl.sem, 16)
                    fl.n += 16
                for c in range(4, DC):
                    sync.dma_start(out=sh2[c][:, 1:512],
                                   in_=strm[b, c, :, t0:t0 + 511]).then_inc(fl.sem, 16)
                    fl.n += 16
            rec(up + "loads", fL[idx % 2])

            shf = [res2[0], res2[1], res2[2], res2[3], sh2[4], sh2[5], sh2[6], sh2[7]]

            def w1banka(ja, aw):
                return (pool[4][0:aw, :] if ja % 2 == 0 else pf_ss[0:aw, 0:512])

            def w1bankg(ja, aw):
                return (pool[5][0:aw, :] if ja % 2 == 0 else pf_ss[0:aw, 512:1024])

            @block.gpsimd
            def _(g):
                for ja in range(22):
                    aw = 128 if ja < 21 else HID - 21 * 128
                    pg = ja % 2
                    gja = idx * 22 + ja
                    if gja >= 2:
                        pg_ = gja - 2
                        pidx2, pja = divmod(pg_, 22)
                        pb2, pu2 = divmod(pidx2, NU)
                        wge(g, fP, f"F{l}B{pb2}_{pu2}_w1_{pja}")
                    elif l > 0 and ja < 2:
                        wge(g, fP, f"F{l-1}_B_pe_end")
                    gw = fGw1[pg]
                    g.dma_start(
                        out=w1sl[pg][:, 0:1024].rearrange("p (k j) -> p k j", k=8)[:, :, 0:aw],
                        in_=w1_in[l, :, ja * 128:ja * 128 + aw]
                            .rearrange("(k p) j -> p k j", p=128)).then_inc(gw.sem, 16)
                    gw.n += 16
                    g.dma_start(
                        out=w1sl[pg][:, 1024:2048].rearrange("p (k j) -> p k j", k=8)[:, :, 0:aw],
                        in_=w1_in[l, :, HID + ja * 128:HID + ja * 128 + aw]
                            .rearrange("(k p) j -> p k j", p=128)).then_inc(gw.sem, 16)
                    gw.n += 16
                    rec(up + f"gw1_{ja}", gw)
                for grp in range(2):
                    for jc in range(22):
                        aw = 128 if jc < 21 else HID - 21 * 128
                        pg = jc % 2
                        kidx = idx * 44 + grp * 22 + jc
                        if kidx >= 2:
                            pg_ = kidx - 2
                            pidx2, prem = divmod(pg_, 44)
                            pgrp, pjc = divmod(prem, 22)
                            pb2, pu2 = divmod(pidx2, NU)
                            wge(g, fP, f"F{l}B{pb2}_{pu2}_w2_{pgrp}_{pjc}")
                        gw = fGw2[pg]
                        g.dma_start(out=w2sl[pg][0:aw, :],
                                    in_=w2_in[l, jc * 128:jc * 128 + aw, :]).then_inc(gw.sem, 16)
                        gw.n += 16
                        rec(up + f"gw2_{grp}_{jc}", gw)

            @block.tensor
            def _(te):
                wge(te, fS, up + "sqs")
                wge(te, fL[idx % 2], up + "loads")
                if pup is not None:
                    wge(te, fV, pup + "out_done")
                for c in range(DC):
                    mms = te.matmul(out=pool[0][0:1, :], lhsT=ones16[:, 0:1],
                                    rhs=shf[c][:], start=(c == 0), stop=(c == DC - 1))
                mms.then_inc(fP.sem, 1)
                fP.n += 1
                for c in range(DC):
                    mms2 = te.matmul(out=pool[1][0:1, :], lhsT=ones16[:, 0:1],
                                     rhs=h2[c][:], start=(c == 0), stop=(c == DC - 1))
                mms2.then_inc(fP.sem, 1)
                fP.n += 1
                rec(up + "lnsums", fP)
                wge(te, fS, up + "rsr")
                te.matmul(out=pool[2][:], lhsT=ones16[0:1, :], rhs=murow[:],
                          start=True, stop=True).then_inc(fP.sem, 1)
                fP.n += 1
                te.matmul(out=pool[3][:], lhsT=ones16[0:1, :], rhs=rstdrow[:],
                          start=True, stop=True).then_inc(fP.sem, 1)
                fP.n += 1
                rec(up + "lnbcast", fP)
                for ja in range(22):
                    aw = 128 if ja < 21 else HID - 21 * 128
                    pg = ja % 2
                    wge(te, fGw1[pg], up + f"gw1_{ja}")
                    if ja == 0:
                        wge(te, fV, up + "h2_7")
                    if ja >= 2:
                        wge(te, fV, up + f"hid{ja-2}")
                    elif pup is not None:
                        wge(te, fV, pup + "hid21")
                    for kc in range(DC):
                        mma = te.matmul(out=w1banka(ja, aw),
                                        lhsT=w1sl[pg][:, kc * 128:kc * 128 + aw],
                                        rhs=h2[kc][:], start=(kc == 0),
                                        stop=(kc == DC - 1))
                    mma.then_inc(fP.sem, 1)
                    fP.n += 1
                    for kc in range(DC):
                        mmg = te.matmul(out=w1bankg(ja, aw),
                                        lhsT=w1sl[pg][:, 1024 + kc * 128:1024 + kc * 128 + aw],
                                        rhs=h2[kc][:], start=(kc == 0),
                                        stop=(kc == DC - 1))
                    mmg.then_inc(fP.sem, 1)
                    fP.n += 1
                    rec(up + f"w1_{ja}", fP)
                for grp in range(2):
                    for jc in range(22):
                        aw = 128 if jc < 21 else HID - 21 * 128
                        pg = jc % 2
                        wge(te, fGw2[pg], up + f"gw2_{grp}_{jc}")
                        wge(te, fV, up + f"hid{jc}")
                        if jc == 0:
                            if grp == 0:
                                wge(te, fV, up + "h2_7")
                            else:
                                for ci in range(4):
                                    wge(te, fV, up + f"out{ci}")
                        for ci in range(4):
                            dc = grp * 4 + ci
                            te.matmul(out=pool[ci][:],
                                      lhsT=w2sl[pg][0:aw, dc * 128:(dc + 1) * 128],
                                      rhs=hidst[jc][0:aw, :],
                                      start=(jc == 0), stop=(jc == 21)
                                      ).then_inc(fP.sem, 1)
                            fP.n += 1
                        rec(up + f"w2_{grp}_{jc}", fP)
                    rec(up + f"w2g{grp}", fP)
            rec(up + "pe_end", fP)

            @block.scalar
            def _(s):
                wge(s, fL[idx % 2], up + "loads")
                wge(s, fV, up + "shz")
                if pup is not None:
                    wge(s, fP, pup + "lnsums")
                    wge(s, fV, pup + "out_done")
                elif l > 0:
                    wge(s, fP, f"F{l-1}_B_pe_end")
                for c in range(DC):
                    s.activation(out=h2[c][:], in_=shf[c][:], func=AF.Square
                                 ).then_inc(fS.sem, 1)
                    fS.n += 1
                rec(up + "sqs", fS)
                wge(s, fV, up + "mu")
                s.activation(out=m2row[:], in_=murow[:], func=AF.Square).then_inc(fS.sem, 1)
                fS.n += 1
                rec(up + "m2", fS)
                wge(s, fV, up + "var")
                s.activation(out=varrow[:], in_=varrow[:], func=AF.Ln,
                             bias=cv[0:1, 3:4]).then_inc(fS.sem, 1)
                fS.n += 1
                s.wait_ge(fS.sem, fS.n)
                s.activation(out=rstdrow[:], in_=varrow[:], func=AF.Exp,
                             scale=-0.5).then_inc(fS.sem, 1)
                fS.n += 1
                rec(up + "rsr", fS)
                for ja in range(22):
                    aw = 128 if ja < 21 else HID - 21 * 128
                    wge(s, fP, up + f"w1_{ja}")
                    if ja >= 2:
                        wge(s, fV, up + f"hid{ja-2}")
                    s.activation(out=gel_sb[ja % 2][0:aw, :], in_=w1bankg(ja, aw),
                                 func=AF.Gelu, bias=b1gc[0:aw, ja:ja + 1]
                                 ).then_inc(fS.sem, 1)
                    fS.n += 1
                    rec(up + f"gel{ja}", fS)
                for grp in range(2):
                    for ci in range(4):
                        dc = grp * 4 + ci
                        wge(s, fV, up + f"out{dc}")
                        s.dma_start(out=strm[b, dc, :, t0:t0 + 512],
                                    in_=strt[dc % 2][:]).then_inc(fWs[dc % 2].sem, 16)
                        fWs[dc % 2].n += 16
                        rec(up + f"bsp{dc}", fWs[dc % 2])

            @block.vector
            def _(v):
                wge(v, fL[idx % 2], up + "loads")
                if pup is not None:
                    wge(v, fP, pup + "pe_end")
                elif l > 0:
                    wge(v, fP, f"F{l-1}_B_pe_end")
                if t0 == 0:
                    for c in range(4, DC):
                        v.memset(sh2[c][:, 0:1], 0.0).then_inc(fV.sem, 1)
                        fV.n += 1
                else:
                    for c in range(4, DC):
                        sci = (b * NU + u) * 4 + (c - 4)
                        v.tensor_copy(out=sh2[c][:, 0:1],
                                      in_=shcolb[:, sci:sci + 1]).then_inc(fV.sem, 1)
                        fV.n += 1
                rec(up + "shz", fV)
                wge(v, fP, up + "lnsums")
                v.tensor_scalar(out=murow[:], in0=pool[0][0:1, :], scalar1=1.0 / D,
                                scalar2=None, op0=AluOpType.mult).then_inc(fV.sem, 1)
                fV.n += 1
                rec(up + "mu", fV)
                wge(v, fS, up + "m2")
                v.scalar_tensor_tensor(out=varrow[:], in0=pool[1][0:1, :],
                                       scalar=1.0 / D, in1=m2row[:],
                                       op0=AluOpType.mult,
                                       op1=AluOpType.subtract).then_inc(fV.sem, 1)
                fV.n += 1
                rec(up + "var", fV)
                wge(v, fP, up + "lnbcast")
                for c in range(DC):
                    v.tensor_tensor(out=lntmp_ap, in0=shf[c][:], in1=pool[2][:],
                                    op=AluOpType.subtract).then_inc(fV.sem, 1)
                    fV.n += 1
                    v.wait_ge(fV.sem, fV.n)
                    v.tensor_tensor(out=lntmp2_ap, in0=lntmp_ap, in1=pool[3][:],
                                    op=AluOpType.mult).then_inc(fV.sem, 1)
                    fV.n += 1
                    v.wait_ge(fV.sem, fV.n)
                    v.tensor_scalar(out=h2[c][:], in0=lntmp2_ap,
                                    scalar1=lngc[:, c:c + 1], scalar2=lnbc[:, c:c + 1],
                                    op0=AluOpType.mult, op1=AluOpType.add
                                    ).then_inc(fV.sem, 1)
                    fV.n += 1
                    rec(up + f"h2_{c}", fV)
                rec(up + "h2_7", fV)
                for ja in range(22):
                    aw = 128 if ja < 21 else HID - 21 * 128
                    wge(v, fS, up + f"gel{ja}")
                    v.scalar_tensor_tensor(out=hidst[ja][0:aw, :], in0=w1banka(ja, aw),
                                           scalar=b1ac[0:aw, ja:ja + 1],
                                           in1=gel_sb[ja % 2][0:aw, :],
                                           op0=AluOpType.add, op1=AluOpType.mult
                                           ).then_inc(fV.sem, 1)
                    fV.n += 1
                    rec(up + f"hid{ja}", fV)
                for grp in range(2):
                    wge(v, fP, up + f"w2g{grp}")
                    for ci in range(4):
                        dc = grp * 4 + ci
                        if dc >= 2:
                            wge(v, fWs[dc % 2], up + f"bsp{dc-2}")
                        elif pup is not None:
                            wge(v, fWs[dc % 2], pup + f"bsp{dc+6}")
                        v.scalar_tensor_tensor(out=strt[dc % 2][:], in0=pool[ci][:],
                                               scalar=b2c[:, dc:dc + 1],
                                               in1=res2[dc][:], op0=AluOpType.add,
                                               op1=AluOpType.add).then_inc(fV.sem, 1)
                        fV.n += 1
                        rec(up + f"out{dc}", fV)
                rec(up + "out_done", fV)
            rec(up + "v_end", fV)

        def emit_layer(l, strm_in, strm_out, with_B):
            lp = f"F{l}_"

            @block.gpsimd
            def _(g):
                g.wait_ge(bar.sem, BAR0)
                if l > 0:
                    wge(g, fP, f"F{l-1}_A_pe_end")
                    wge(g, fV, f"F{l-1}_tabs_done")
                    wge(g, fS, f"F{l-1}_B_s_end")
                    wge(g, fV, f"F{l-1}_B_v_end")
                for c in range(DC):
                    g.dma_start(out=wq16[c][:],
                                in_=wqkv_in[l, c * 128:(c + 1) * 128, :]).then_inc(fG.sem, 16)
                    fG.n += 16
                for h in range(NHEAD):
                    g.dma_start(out=wo16h[h][:],
                                in_=wo_in[l, h * 64:(h + 1) * 64, :]).then_inc(fG.sem, 16)
                    fG.n += 16
                for dst, src_ in [(qscol[:], qs_t[l]), (kscol[:], ks_t[l]),
                                  (lngc[:], lng_t[l]), (lnbc[:], lnb_t[l]),
                                  (b1ac[:], b1a_in[l]), (b1gc[:], b1g_in[l]),
                                  (b2c[:], b2_t[l]), (csqf[:], csq_in),
                                  (cskf[:], csk_in)]:
                    g.dma_start(out=dst, in_=src_).then_inc(fG.sem, 16)
                    fG.n += 16
            rec(lp + "wl", fG)

            @block.vector
            def _(v):
                wge(v, fG, lp + "wl")
                v.wait_ge(fGm.sem, FGM_ALL)
                if l > 0:
                    wge(v, fP, f"F{l-1}_A_pe_end")
                v.tensor_scalar(out=csqf[:], in0=csqf[:], scalar1=qscol[:, 0:1],
                                scalar2=None, op0=AluOpType.mult).then_inc(fV.sem, 1)
                fV.n += 1
                v.tensor_scalar(out=cskf[:], in0=cskf[:], scalar1=kscol[:, 0:1],
                                scalar2=None, op0=AluOpType.mult).then_inc(fV.sem, 1)
                fV.n += 1
                v.tensor_scalar(out=rq16[:], in0=rpermq16[:], scalar1=qscol[:, 0:1],
                                scalar2=None, op0=AluOpType.mult).then_inc(fV.sem, 1)
                fV.n += 1
                v.tensor_scalar(out=rk16[:], in0=rperm16[:], scalar1=kscol[:, 0:1],
                                scalar2=None, op0=AluOpType.mult).then_inc(fV.sem, 1)
                fV.n += 1
            rec(lp + "tabs_done", fV)

            for b in range(NB):
                for w in range(NW):
                    _emit_attn_window(l, b, w, strm_in, strm_out)
            rec(lp + "A_pe_end", fP)
            rec(lp + "A_done_W0", fWs[0])
            rec(lp + "A_done_W1", fWs[1])

            if with_B:
                for b in range(NB):
                    for u in range(T // 512):
                        _emit_ffnmid_unit(l, b, u, strm_out)
            rec(lp + "B_done_W0", fWs[0])
            rec(lp + "B_done_W1", fWs[1])
            rec(lp + "B_v_end", fV)
            rec(lp + "B_pe_end", fP)
            rec(lp + "B_s_end", fS)

        emit_layer(0, xn16, strmA, with_B=(stage >= 4))
        if stage >= 5:
            emit_layer(1, strmA, strmB, with_B=True)

        if stage >= 6:
            fOut = Ch(nc, "f_out")
            ostage = w1sl[0][:, 0:1024]
            p2bf3 = pool[2][:].bitcast(BF16)
            NU = T // 512
            for b in range(NB):
                for u in range(NU):
                    t0 = u * 512
                    cp = f"P3_{b}_{u}_"
                    pidx = b * NU + u
                    ppp = None
                    if pidx > 0:
                        qb, qu = divmod(pidx - 1, NU)
                        ppp = f"P3_{qb}_{qu}_"

                    @block.sync
                    def _(sync):
                        if ppp is None:
                            wge(sync, fWs[0], "F1_B_done_W0")
                            wge(sync, fWs[1], "F1_B_done_W1")
                            wge(sync, fV, "F1_B_v_end")
                            wge(sync, fP, "F1_B_pe_end")
                            if stage >= 2:
                                sync.wait_ge(cWs[0].sem, MAMBA_W0)
                                sync.wait_ge(cWs[1].sem, MAMBA_W1)
                        else:
                            wge(sync, fV, ppp + "sg_done")
                        for dc in range(DC):
                            fl = fL[dc % 2]
                            sti = sq_sb[0] if dc % 2 == 0 else qn_sb[0]
                            fti = t1_sb[0] if dc % 2 == 0 else rb_sb[0]
                            if dc >= 2:
                                wge(sync, fV, cp + f"sg{dc-2}")
                            sync.dma_start(out=sti[:], in_=s16[b, dc, :, t0:t0 + 512]
                                           ).then_inc(fl.sem, 16)
                            fl.n += 16
                            sync.dma_start(out=fti[:], in_=strmB[b, dc, :, t0:t0 + 512]
                                           ).then_inc(fl.sem, 16)
                            fl.n += 16
                            rec(cp + f"ld{dc}", fl)
                        for ts in range(4):
                            wge(sync, fV, cp + f"ob{ts}")
                            sync.dma_start(
                                out=out[b, t0 + ts * 128:t0 + (ts + 1) * 128, :],
                                in_=obuf[:]).then_inc(fOut.sem, 16)
                            fOut.n += 16
                            rec(cp + f"st{ts}", fOut)

                    @block.gpsimd
                    def _(g):
                        for ts in range(4):
                            if ppp is not None and ts == 0:
                                wge(g, fV, ppp + "ob3")
                            elif ts > 0:
                                wge(g, fV, cp + f"ob{ts-1}")
                            g.dma_start(out=xrl[:],
                                        in_=x_in[b, t0 + ts * 128:t0 + (ts + 1) * 128, :]
                                        ).then_inc(fG.sem, 16)
                            fG.n += 16
                            rec(cp + f"x{ts}", fG)

                    @block.scalar
                    def _(s):
                        for dc in range(DC):
                            fti = t1_sb[0] if dc % 2 == 0 else rb_sb[0]
                            wge(s, fL[dc % 2], cp + f"ld{dc}")
                            if dc >= 2:
                                wge(s, fV, cp + f"sg{dc-2}")
                            s.activation(out=gel_sb[dc % 2][:], in_=fti[:],
                                         func=AF.Silu).then_inc(fS.sem, 1)
                            fS.n += 1
                            rec(cp + f"sil{dc}", fS)
                        for ts in range(4):
                            wge(s, fP, cp + f"tr{ts}")
                            if ts >= 1:
                                wge(s, fV, cp + f"ob{ts-1}")
                            elif ppp is not None:
                                wge(s, fV, ppp + "ob3")
                            s.copy(out=ostage, in_=p2bf3[:, 0:1024]).then_inc(fS.sem, 1)
                            fS.n += 1
                            rec(cp + f"ost{ts}", fS)

                    @block.vector
                    def _(v):
                        for dc in range(DC):
                            sti = sq_sb[0] if dc % 2 == 0 else qn_sb[0]
                            wge(v, fS, cp + f"sil{dc}")
                            if dc == 0:
                                if ppp is not None:
                                    wge(v, fP, ppp + "tr3")
                                elif stage >= 5:
                                    wge(v, fP, "F1_B_pe_end")
                            v.tensor_tensor(out=h2[dc][:], in0=sti[:],
                                            in1=gel_sb[dc % 2][:], op=AluOpType.mult
                                            ).then_inc(fV.sem, 1)
                            fV.n += 1
                            rec(cp + f"sg{dc}", fV)
                        for ts in range(4):
                            wge(v, fG, cp + f"x{ts}")
                            wge(v, fS, cp + f"ost{ts}")
                            if ts >= 1 or ppp is not None:
                                wge(v, fOut, cp + f"st{ts-1}" if ts >= 1
                                    else ppp + "st3")
                            v.tensor_tensor(out=obuf[:], in0=xrl[:], in1=ostage,
                                            op=AluOpType.add).then_inc(fV.sem, 1)
                            fV.n += 1
                            rec(cp + f"ob{ts}", fV)
                            if ts == 3:
                                rec(cp + "ob3", fV)
                        rec(cp + "sg_done", fV)

                    @block.tensor
                    def _(te):
                        for ts in range(4):
                            for dc in range(DC):
                                wge(te, fV, cp + f"sg{dc}")
                                if ts > 0 and dc == 0:
                                    wge(te, fS, cp + f"ost{ts-1}")
                                elif ts == 0 and dc == 0 and ppp is not None:
                                    wge(te, fS, ppp + "ost3")
                                te.transpose(out=p2bf3[:, dc * 128:(dc + 1) * 128],
                                             in_=h2[dc][:, ts * 128:(ts + 1) * 128],
                                             identity=ident16[:]).then_inc(fP.sem, 1)
                                fP.n += 1
                            rec(cp + f"tr{ts}", fP)
                        rec(cp + "tr3", fP)

    @block.sync
    def _(sync):
        for c in [ch_hwl, ch_hwl2, ch_swl, ch_sp]:
            if c.n:
                sync.wait_ge(c.sem, c.n)
        if stage >= 2:
            for c in [cL, cG, cWs[0], cWs[1]]:
                if c.n:
                    sync.wait_ge(c.sem, c.n)
        if stage >= 3:
            for c in [fL[0], fL[1], fG, fGw1[0], fGw1[1], fGw2[0], fGw2[1],
                      fWs[0], fWs[1]]:
                if c.n:
                    sync.wait_ge(c.sem, c.n)
        if stage >= 6:
            sync.wait_ge(fOut.sem, fOut.n)

    blk.__exit__(None, None, None)
    if stage >= 3:
        for t_ in reversed(fctxs):
            t_.__exit__(None, None, None)
    else:
        if stage >= 2:
            for t_ in reversed(mamba_ctxs):
                t_.__exit__(None, None, None)
        else:
            for t_ in ps_tt_ctx:
                t_.__exit__(None, None, None)
        for t_ in reversed(p0_ctxs):
            t_.__exit__(None, None, None)
    for c in reversed(ctxs):
        c.__exit__(None, None, None)
    return nc, EV


def build2(T=2048, stage=10, ths_override=None, race=False, dbg=False, mstop=99):
    _, ev = build(T, stage, ths_override, evmap=None, dbg=dbg, mstop=mstop)
    nc, _ = build(T, stage, ths_override, evmap=ev, race=race, dbg=dbg, mstop=mstop)
    return nc



def _hd2_const():
    m = np.zeros((128, 2), np.float32)
    m[0:64, 0] = 1.0; m[64:128, 1] = 1.0
    return m


def _hd2T_const():
    m = np.zeros((2, 128), np.float32)
    m[0, 0:64] = 1.0; m[1, 64:128] = 1.0
    return m


def _rperm_const():
    m = np.zeros((128, 128), np.float32)
    for blkd in range(2):
        o = blkd * 64
        for i in range(32):
            m[o + i + 32, o + i] = -1.0
            m[o + i, o + i + 32] = 1.0
    return m


def _trimadd_const():
    m = np.zeros((128, 256), np.float32)
    for r in range(128):
        m[r, 0:r] = -1e30
        m[r, 128 + r + 1:256] = -1e30
    return m


def _rot_tables(T):
    half = np.arange(0, DH, 2, dtype=np.float32)
    inv_freq = 1.0 / 10000.0 ** (half / DH)
    t = np.arange(T, dtype=np.float32)
    fr = t[:, None] * inv_freq[None, :]
    fr = np.concatenate([fr, fr], axis=-1)
    base = (half + 0.4 * DH) / (1.4 * DH)
    power = (t - T // 2) / float(WIN // 2)
    sc = base[None, :] ** power[:, None]
    sc = np.concatenate([sc, sc], axis=-1)
    cos, sin = np.cos(fr), np.sin(fr)
    csq = (cos * sc) * 8.0; ssq = (sin * sc)
    csk = (cos / sc); ssk = (sin / sc)
    def lay(m):
        r = np.transpose(m)
        return np.concatenate([r, r], axis=0).astype(np.float32)
    return lay(csq), lay(ssq), lay(csk), lay(ssk)


def _selc_const():
    m = np.zeros((32, 32 * 128), np.float32)
    for s in range(32):
        m[s, s * 128:(s + 1) * 128] = 1.0
    return m


def _prep_inputs(inputs):
    T = np.asarray(inputs["L0_attn_x"]).shape[1]
    f = lambda a: np.ascontiguousarray(a, dtype=np.float32)
    x = np.asarray(inputs["L0_attn_x"])
    W_dt = np.asarray(inputs["L0_attn_W_dt"])
    b_dt = np.asarray(inputs["L0_attn_b_dt"])
    A_log = np.asarray(inputs["L0_attn_A_log"])
    Dv = np.asarray(inputs["L0_attn_D"])

    W_dtb = np.concatenate([W_dt, b_dt[None, :]], axis=0)
    A_log_t = np.transpose(A_log.reshape(DC, 128, DSTATE), (1, 0, 2)).reshape(128, DC * DSTATE)
    D_t = np.transpose(Dv.reshape(DC, 128), (1, 0))

    qs = np.asarray(inputs["L0_ffn_q_scale"])
    ks = np.asarray(inputs["L0_ffn_k_scale"])
    qs_t = np.tile(qs, (1, 2)).reshape(2, 128, 1)
    ks_t = np.tile(ks, (1, 2)).reshape(2, 128, 1)
    lng = np.asarray(inputs["L0_ffn_ln_g"])
    lnb = np.asarray(inputs["L0_ffn_ln_b"])
    lng_t = np.transpose(lng.reshape(2, DC, 128), (0, 2, 1))
    lnb_t = np.transpose(lnb.reshape(2, DC, 128), (0, 2, 1))
    b1 = np.asarray(inputs["L0_ffn_b1"])
    b1a = b1[:, :HID]
    b1g = b1[:, HID:]
    pad = np.zeros((2, 22 * 128), np.float32)
    b1a_p = pad.copy(); b1a_p[:, :HID] = b1a
    b1g_p = pad.copy(); b1g_p[:, :HID] = b1g
    b1a_t = np.transpose(b1a_p.reshape(2, 22, 128), (0, 2, 1))
    b1g_t = np.transpose(b1g_p.reshape(2, 22, 128), (0, 2, 1))
    b2 = np.asarray(inputs["L0_ffn_b2"])
    b2_t = np.transpose(b2.reshape(2, DC, 128), (0, 2, 1))

    _RT = _rot_tables(T)
    common = {
        "W_x": f(inputs["L0_attn_W_x"]),
        "W_dtb": f(W_dtb),
        "A_log_t": f(A_log_t),
        "D_t": f(D_t),
        "wqkv": f(inputs["L0_ffn_wqkv"]),
        "qs_t": f(qs_t), "ks_t": f(ks_t),
        "wo": f(inputs["L0_ffn_wo"]),
        "lng_t": f(lng_t), "lnb_t": f(lnb_t),
        "w1": f(inputs["L0_ffn_w1"]), "b1a_t": f(b1a_t), "b1g_t": f(b1g_t),
        "w2": f(inputs["L0_ffn_w2"]), "b2_t": f(b2_t),
        "ident": np.eye(128, dtype=np.float32),
        "cvec": np.tile(np.array([LN_EPS_D2, LN_LOGD, 1e-24, 1e-5, 0, 0, 0, 0],
                                 np.float32), (128, 1)),
        "selc": _selc_const(),
        "hd2": _hd2_const(), "hd2T": _hd2T_const(),
        "onesf": np.ones((128, 128), np.float32),
        "rperm": _rperm_const(), "rpermq": _rperm_const() * 8.0,
        "trimadd": _trimadd_const(),
        "trilo": np.tril(np.ones((128, 128), np.float32)),
        "triup": np.triu(np.ones((128, 128), np.float32)),
        "csq": _RT[0], "ssq": _RT[1], "csk": _RT[2], "ssk": _RT[3],
    }
    in_maps = []
    for c in range(NCORES):
        m = dict(common)
        m["x"] = f(x[c * NB:(c + 1) * NB])
        in_maps.append(m)
    return in_maps


def kernel(**inputs):
    T = np.asarray(inputs["L0_attn_x"]).shape[1]
    in_maps = _prep_inputs(inputs)
    nc = build2(T=T, stage=6)
    res = run_bass_kernel_spmd(nc, in_maps, core_ids=list(range(NCORES)))
    outs = [np.asarray(r["out"], np.float32) for r in res.results]
    return np.concatenate(outs, axis=0)


# version 04
# speedup vs baseline: 1.8310x; 1.0945x over previous
import numpy as np
import ml_dtypes
import concourse.bass as bass
import concourse.mybir as mybir
from concourse.bass_utils import run_bass_kernel_spmd
from concourse.alu_op_type import AluOpType

F32 = mybir.dt.float32
BF16 = mybir.dt.bfloat16
AF = mybir.ActivationFunctionType
AX = mybir.AxisListType

NCORES = 8
D = 1024
DC = 8
NB = 2
DSTATE = 16
WIN = 512
NHEAD = 8
DH = 64
INNER = 512
HID = 2730
H1W = 5460
LN_EPS_D2 = 1e-5 * D * D
LN_LOGD = float(np.log(D))


class Ch:
    def __init__(self, nc, name):
        self._cm = nc.semaphore(name)
        self.sem = self._cm.__enter__()
        self.n = 0


def build(T=2048, stage=10, ths_override=None, evmap=None, race=False, dbg=False, mstop=99):
    recording = evmap is None
    EV = {} if recording else evmap

    def rec(name, ch):
        if recording:
            EV[name] = ch.n

    def wge(eng, ch, name):
        eng.wait_ge(ch.sem, 0 if recording else EV[name])

    nc = bass.Bass("TRN2", target_bir_lowering=False)
    nc.detect_race_conditions = race and not recording
    NTT = T // 128

    x_in = nc.dram_tensor("x", [NB, T, D], F32, kind="ExternalInput").ap()
    out = nc.dram_tensor("out", [NB, T, D], F32, kind="ExternalOutput").ap()

    w_x = nc.dram_tensor("W_x", [D, 96], F32, kind="ExternalInput").ap()
    w_dtb = nc.dram_tensor("W_dtb", [65, D], F32, kind="ExternalInput").ap()
    a_log_t = nc.dram_tensor("A_log_t", [128, DC * DSTATE], F32, kind="ExternalInput").ap()
    d_t = nc.dram_tensor("D_t", [128, DC], F32, kind="ExternalInput").ap()

    wqkv_in = nc.dram_tensor("wqkv", [2, D, 3 * INNER], F32, kind="ExternalInput").ap()
    qs_t = nc.dram_tensor("qs_t", [2, 128, 1], F32, kind="ExternalInput").ap()
    ks_t = nc.dram_tensor("ks_t", [2, 128, 1], F32, kind="ExternalInput").ap()
    wo_in = nc.dram_tensor("wo", [2, INNER, D], F32, kind="ExternalInput").ap()
    lng_t = nc.dram_tensor("lng_t", [2, 128, DC], F32, kind="ExternalInput").ap()
    lnb_t = nc.dram_tensor("lnb_t", [2, 128, DC], F32, kind="ExternalInput").ap()
    w1_in = nc.dram_tensor("w1", [2, D, H1W], F32, kind="ExternalInput").ap()
    b1a_in = nc.dram_tensor("b1a_t", [2, 128, 22], F32, kind="ExternalInput").ap()
    b1g_in = nc.dram_tensor("b1g_t", [2, 128, 22], F32, kind="ExternalInput").ap()
    w2_in = nc.dram_tensor("w2", [2, HID, D], F32, kind="ExternalInput").ap()
    b2_t = nc.dram_tensor("b2_t", [2, 128, DC], F32, kind="ExternalInput").ap()

    ident = nc.dram_tensor("ident", [128, 128], F32, kind="ExternalInput").ap()
    cvec = nc.dram_tensor("cvec", [128, 8], F32, kind="ExternalInput").ap()
    selc_in = nc.dram_tensor("selc", [32, 32 * 128], F32, kind="ExternalInput").ap()
    hd2_in = nc.dram_tensor("hd2", [128, 2], F32, kind="ExternalInput").ap()
    hd2T_in = nc.dram_tensor("hd2T", [2, 128], F32, kind="ExternalInput").ap()
    onesf_in = nc.dram_tensor("onesf", [128, 128], F32, kind="ExternalInput").ap()
    rperm_in = nc.dram_tensor("rperm", [128, 128], F32, kind="ExternalInput").ap()
    rpermq_in = nc.dram_tensor("rpermq", [128, 128], F32, kind="ExternalInput").ap()
    trimadd_in = nc.dram_tensor("trimadd", [128, 256], F32, kind="ExternalInput").ap()
    trilo_in = nc.dram_tensor("trilo", [128, 128], F32, kind="ExternalInput").ap()
    triup_in = nc.dram_tensor("triup", [128, 128], F32, kind="ExternalInput").ap()
    csq_in = nc.dram_tensor("csq", [128, T], F32, kind="ExternalInput").ap()
    ssq_in = nc.dram_tensor("ssq", [128, T], F32, kind="ExternalInput").ap()
    csk_in = nc.dram_tensor("csk", [128, T], F32, kind="ExternalInput").ap()
    ssk_in = nc.dram_tensor("ssk", [128, T], F32, kind="ExternalInput").ap()

    _ik = "ExternalOutput" if dbg else "Internal"
    xn16 = nc.dram_tensor("xn16", [NB, DC, 128, T], BF16, kind=_ik).ap()
    s16 = nc.dram_tensor("s16", [NB, DC, 128, T], BF16, kind=_ik).ap()
    strmA = nc.dram_tensor("strmA", [NB, DC, 128, T], BF16, kind=_ik).ap()
    strmB = nc.dram_tensor("strmB", [NB, DC, 128, T], BF16, kind=_ik).ap()
    NW = T // WIN
    mrow_dram = nc.dram_tensor("mrow_dram", [NB, NW, 4096], BF16, kind="Internal").ap()

    ctxs = []
    def sbuf(name, shape, dt):
        t = nc.sbuf_tensor(name, shape, dt)
        h = t.__enter__(); ctxs.append(t); return h

    ident16 = sbuf("ident16", [128, 128], BF16)
    cv = sbuf("cv", [128, 8], F32)

    p0_ctxs = []
    def sbuf_p0(name, shape, dt):
        t = nc.sbuf_tensor(name, shape, dt)
        h = t.__enter__(); p0_ctxs.append(t); return h
    xb = [sbuf_p0(f"xb{i}", [128, D], F32) for i in range(2)]
    sqb = [sbuf_p0(f"sqb{i}", [128, D], F32) for i in range(2)]
    stat = sbuf_p0("stat", [128, 16], F32)
    xnb = [sbuf_p0(f"xnb{i}", [128, D], BF16) for i in range(2)]
    xt_stage = [sbuf_p0(f"xt_stage{i}", [128, 256], BF16) for i in range(DC)]
    NPS_T = 16
    ps_tt_ctx = [nc.psum_tensor(f"ps_tt{i}", [128, 1024], BF16) for i in range(2)]
    ps_tt = [t.__enter__() for t in ps_tt_ctx]
    ps_tt_ctx = list(reversed(ps_tt_ctx))
    def ps_t_slot(k):
        sl = k % NPS_T
        return ps_tt[sl // 8][:, (sl % 8) * 128:(sl % 8) * 128 + 128]

    ch_hwl = Ch(nc, "hwl")
    ch_hwl2 = Ch(nc, "hwl2")
    ch_swl = Ch(nc, "swl")
    ch_sq = Ch(nc, "c_sq")
    ch_st = Ch(nc, "c_st")
    ch_pe = Ch(nc, "c_pe")
    ch_ev = Ch(nc, "c_ev")
    ch_sp = Ch(nc, "c_sp")

    blk = nc.Block()
    block = blk.__enter__()

    tiles = [(b, tt) for b in range(NB) for tt in range(NTT)]
    NTILE = len(tiles)
    VP, SP = 4, 4

    @block.sync
    def _(sync):
        for i, (b, tt) in enumerate(tiles):
            if i >= 2:
                sync.wait_ge(ch_st.sem, (i - 1) * VP)
            _c = ch_hwl if i % 2 == 0 else ch_hwl2
            sync.dma_start(out=xb[i % 2][:],
                           in_=x_in[b, tt * 128:(tt + 1) * 128, :]).then_inc(_c.sem, 16)
            _c.n += 16

    @block.gpsimd
    def _(g):
        g.dma_start(out=ident16[:], in_=ident).then_inc(ch_swl.sem, 16)
        ch_swl.n += 16
        g.dma_start(out=cv[:], in_=cvec).then_inc(ch_swl.sem, 16)
        ch_swl.n += 16

    @block.scalar
    def _(s):
        for i, (b, tt) in enumerate(tiles):
            _c = ch_hwl if i % 2 == 0 else ch_hwl2
            s.wait_ge(_c.sem, 16 * (i // 2 + 1))
            if i == 0:
                s.wait_ge(ch_swl.sem, 32)
            if i >= 2:
                s.wait_ge(ch_st.sem, (i - 1) * VP)
            s.activation(out=sqb[i % 2][:], in_=xb[i % 2][:],
                         func=AF.Square).then_inc(ch_sq.sem, 1)
            s.wait_ge(ch_st.sem, i * VP + 1)
            s.activation(out=stat[:, 2:3], in_=stat[:, 0:1],
                         func=AF.Identity, scale=1.0 / D)
            s.activation(out=stat[:, 6:7], in_=stat[:, 0:1],
                         func=AF.Square).then_inc(ch_sq.sem, 1)
            s.wait_ge(ch_st.sem, i * VP + 3)
            s.activation(out=stat[:, 8:9], in_=stat[:, 3:4], func=AF.Ln,
                         bias=cv[:, 0:1]).then_inc(ch_sq.sem, 1)
            s.wait_ge(ch_sq.sem, i * SP + 3)
            s.activation(out=stat[:, 4:5], in_=stat[:, 8:9], func=AF.Exp,
                         scale=-0.5, bias=cv[:, 1:2]).then_inc(ch_sq.sem, 1)
            ch_sq.n = (i + 1) * SP

            def _evac_tile(j):
                bj, ttj = tiles[j]
                s.wait_ge(ch_pe.sem, (j + 1) * DC)
                if j >= 2:
                    s.wait_ge(ch_sp.sem, 128 * (j // 2))
                for dc in range(DC):
                    k = j * DC + dc
                    s.copy(out=xt_stage[dc][:, (ttj % 2) * 128:(ttj % 2) * 128 + 128],
                           in_=ps_t_slot(k)).then_inc(ch_ev.sem, 1)
                    ch_ev.n += 1
                if ttj % 2 == 1:
                    s.wait_ge(ch_ev.sem, ch_ev.n)
                    for dc in range(DC):
                        s.dma_start(out=xn16[bj, dc, :, (ttj - 1) * 128:(ttj + 1) * 128],
                                    in_=xt_stage[dc][:]).then_inc(ch_sp.sem, 16)
                        ch_sp.n += 16
            if i >= 1:
                _evac_tile(i - 1)
            if i == NTILE - 1:
                _evac_tile(i)

    @block.vector
    def _(v):
        for i, (b, tt) in enumerate(tiles):
            _c = ch_hwl if i % 2 == 0 else ch_hwl2
            v.wait_ge(_c.sem, 16 * (i // 2 + 1))
            if i >= 1:
                v.wait_ge(ch_sq.sem, i * SP)
            v.tensor_reduce(out=stat[:, 0:1], in_=xb[i % 2][:], axis=AX.X,
                            op=AluOpType.add).then_inc(ch_st.sem, 1)
            v.wait_ge(ch_sq.sem, i * SP + 1)
            v.tensor_reduce(out=stat[:, 1:2], in_=sqb[i % 2][:], axis=AX.X,
                            op=AluOpType.add).then_inc(ch_st.sem, 1)
            v.wait_ge(ch_st.sem, i * VP + 2)
            v.wait_ge(ch_sq.sem, i * SP + 2)
            v.scalar_tensor_tensor(out=stat[:, 3:4], in0=stat[:, 1:2],
                                   scalar=float(D), in1=stat[:, 6:7],
                                   op0=AluOpType.mult,
                                   op1=AluOpType.subtract).then_inc(ch_st.sem, 1)
            v.wait_ge(ch_sq.sem, i * SP + 4)
            if i >= 2:
                v.wait_ge(ch_pe.sem, (i - 1) * DC)
            v.tensor_scalar(out=xnb[i % 2][:], in0=xb[i % 2][:],
                            scalar1=stat[:, 2:3], scalar2=stat[:, 4:5],
                            op0=AluOpType.subtract,
                            op1=AluOpType.mult).then_inc(ch_st.sem, 1)
            ch_st.n = (i + 1) * VP

    @block.tensor
    def _(te):
        te.wait_ge(ch_swl.sem, 32)
        for i, (b, tt) in enumerate(tiles):
            te.wait_ge(ch_st.sem, i * VP + 4)
            for dc in range(DC):
                k = i * DC + dc
                if k >= NPS_T:
                    te.wait_ge(ch_ev.sem, k - (NPS_T - 1))
                te.transpose(out=ps_t_slot(k),
                             in_=xnb[i % 2][:, dc * 128:(dc + 1) * 128],
                             identity=ident16[:]).then_inc(ch_pe.sem, 1)
                ch_pe.n += 1

    P0_SP_ALL = NTILE // 2 * DC * 16

    if stage >= 2:
        THS = ths_override or min(1024, T)
        NTH = T // THS
        NTC = THS // 512

        for t_ in ps_tt_ctx:
            t_.__exit__(None, None, None)
        ps_tt_freed = True

        mamba_ctxs = []
        def sbuf_m(name, shape, dt):
            t = nc.sbuf_tensor(name, shape, dt)
            h = t.__enter__(); mamba_ctxs.append(t); return h
        def psum_m(name, shape, dt):
            t = nc.psum_tensor(name, shape, dt)
            h = t.__enter__(); mamba_ctxs.append(t); return h

        wx16 = sbuf_m("wx16", [128, DC * 96], BF16)
        wdtb16 = sbuf_m("wdtb16", [65, D], BF16)
        selc16 = sbuf_m("selc16", [32, 32 * 128], BF16)
        alog_sb = sbuf_m("alog_sb", [128, 128], F32)
        a_sb = sbuf_m("a_sb", [128, 128], F32)
        dvec = sbuf_m("dvec", [128, DC], F32)
        xpa = sbuf_m("xpa", [65, THS], BF16)
        bc16 = sbuf_m("bc16", [32, THS], BF16)
        xnT = [sbuf_m(f"m_xnT{c}", [128, THS], BF16) for c in range(DC)]
        dtT = [sbuf_m(f"m_dtT{c}", [128, THS], F32) for c in range(DC)]
        dtx = [sbuf_m(f"m_dtx{c}", [128, THS], BF16) for c in range(DC)]
        ybuf = [sbuf_m(f"m_y{c}", [128, THS], F32) for c in range(DC)]
        sp_t = [sbuf_m(f"m_spt{i}", [128, 512], F32) for i in range(6)]
        abar = [sbuf_m(f"m_ab{i}", [128, THS], F32) for i in range(4)]
        ubuf = [sbuf_m(f"m_u{i}", [128, THS], BF16) for i in range(2)]
        hbuf = [sbuf_m(f"m_h{i}", [128, THS], BF16) for i in range(2)]
        tmpb = [sbuf_m(f"m_tmp{i}", [128, THS], BF16) for i in range(2)]
        bb_sb = [sbuf_m(f"m_bb{i}", [128, THS], BF16) for i in range(2)]
        cb_sb = [sbuf_m(f"m_cb{i}", [128, THS], BF16) for i in range(2)]
        hcar = sbuf_m("m_hcar", [128, 128], F32)
        soutb = [sbuf_m(f"m_sout{i}", [128, THS], BF16) for i in range(2)]

        pm_dt = psum_m("pm_dt", [64, 512], F32)
        pm_bc = psum_m("pm_bc", [32, 512], F32)
        pm_dtT = [psum_m(f"pm_dtT{i}", [128, 512], F32) for i in range(2)]
        pm_bb = psum_m("pm_bb", [128, THS], F32)
        pm_cb = psum_m("pm_cb", [128, THS], F32)

        cL = Ch(nc, "m_L"); cG = Ch(nc, "m_G"); cP = Ch(nc, "m_P")
        cS = Ch(nc, "m_S"); cV = Ch(nc, "m_V")
        cWs = [Ch(nc, "m_W0"), Ch(nc, "m_W1")]
        cGm = Ch(nc, "m_Gm")

        @block.gpsimd
        def _(g):
            for c in range(DC):
                g.dma_start(out=wx16[:, c * 96:(c + 1) * 96],
                            in_=w_x[c * 128:(c + 1) * 128, :]).then_inc(cG.sem, 16)
                cG.n += 16
            for ap_dst, ap_src in [(wdtb16[:], w_dtb), (selc16[:], selc_in),
                                   (alog_sb[:], a_log_t), (dvec[:], d_t)]:
                g.dma_start(out=ap_dst, in_=ap_src).then_inc(cG.sem, 16)
                cG.n += 16
            g.memset(xpa[64:65, :], 1.0).then_inc(cGm.sem, 1)
            cGm.n += 1
        G_W = cG.n

        @block.scalar
        def _(s):
            s.wait_ge(cG.sem, G_W)
            s.activation(out=a_sb[:], in_=alog_sb[:], func=AF.Exp).then_inc(cS.sem, 1)
        cS.n += 1
        S_EXPA = cS.n

        @block.vector
        def _(v):
            v.wait_ge(cS.sem, S_EXPA)
            v.tensor_scalar(out=a_sb[:], in0=a_sb[:], scalar1=-1.0, scalar2=None,
                            op0=AluOpType.mult).then_inc(cV.sem, 1)
        cV.n += 1
        V_NEGA = cV.n

        blocks2 = [(b, th) for b in range(NB) for th in range(NTH)]
        NKK = NTC * DC
        for m, (b, th) in enumerate(blocks2):
            t0 = th * THS
            mm_ = f"M{m}_"
            pm_ = f"M{m-1}_" if m > 0 else None

            @block.sync
            def _(sync):
                if m == 0:
                    sync.wait_ge(ch_sp.sem, P0_SP_ALL)
                else:
                    wge(sync, cV, pm_ + "vend")
                    wge(sync, cP, pm_ + "pend")
                for c in range(DC):
                    sync.dma_start(out=xnT[c][:], in_=xn16[b, c, :, t0:t0 + THS]
                                   ).then_inc(cL.sem, 16)
                    cL.n += 16
            rec(mm_ + "loads", cL)

            @block.tensor
            def _(te):
                wge(te, cL, mm_ + "loads")
                if m == 0:
                    te.wait_ge(cV.sem, V_NEGA)
                    te.wait_ge(cGm.sem, 1)
                    te.wait_ge(ch_ev.sem, NTILE * DC)
                for tc in range(NTC):
                    if m > 0 or tc > 0:
                        wge(te, cS, (mm_ + f"xw{tc-1}") if tc > 0 else (pm_ + f"xw{NTC-1}"))
                    for c in range(DC):
                        mm1 = te.matmul(out=pm_dt[:], lhsT=wx16[:, c * 96:c * 96 + 64],
                                        rhs=xnT[c][:, tc * 512:(tc + 1) * 512],
                                        start=(c == 0), stop=(c == DC - 1))
                    mm1.then_inc(cP.sem, 1)
                    cP.n += 1
                    for c in range(DC):
                        mm2 = te.matmul(out=pm_bc[:], lhsT=wx16[:, c * 96 + 64:c * 96 + 96],
                                        rhs=xnT[c][:, tc * 512:(tc + 1) * 512],
                                        start=(c == 0), stop=(c == DC - 1))
                    mm2.then_inc(cP.sem, 1)
                    cP.n += 1
                    rec(mm_ + f"xproj{tc}", cP)

            for tc in range(NTC if mstop >= 21 else 0):
                for c in range(DC):
                    @block.tensor
                    def _(te, tc=tc, c=c):
                        kk = tc * DC + c
                        pi = kk % 2
                        gk = m * NKK + kk
                        if c == 0:
                            wge(te, cS, mm_ + f"xpa{tc}")
                        if gk >= 2:
                            pgk = gk - 2
                            pmm = f"M{pgk // NKK}_"
                            ptc, pc = divmod(pgk % NKK, DC)
                            if mstop >= 23:
                                wge(te, cV, pmm + f"spw{ptc}_{pc}")
                            if mstop >= 22:
                                wge(te, cS, pmm + f"spl{ptc}_{pc}")
                        te.matmul(out=pm_dtT[pi][:], lhsT=wdtb16[:, c * 128:(c + 1) * 128],
                                  rhs=xpa[:, tc * 512:(tc + 1) * 512],
                                  start=True, stop=True).then_inc(cP.sem, 1)
                        cP.n += 1
                        rec(mm_ + f"dtT{tc}_{c}", cP)

            for si in range(DSTATE if mstop >= 30 else 0):
                @block.tensor
                def _(te, si=si):
                    gsi = m * DSTATE + si
                    wge(te, cV, mm_ + "dtxall")
                    if gsi >= 1:
                        pgs = gsi - 1
                        pmm = f"M{pgs // DSTATE}_"
                        psi = pgs % DSTATE
                        wge(te, cS, pmm + f"bbev{psi}")
                    for h2_ in range(NTC):
                        mmb = te.matmul(out=pm_bb[:, h2_ * 512:(h2_ + 1) * 512],
                                        lhsT=selc16[:, si * 128:(si + 1) * 128],
                                        rhs=bc16[:, h2_ * 512:(h2_ + 1) * 512],
                                        start=True, stop=True)
                    mmb.then_inc(cP.sem, 1)
                    cP.n += 1
                    for h2_ in range(NTC):
                        mmc = te.matmul(out=pm_cb[:, h2_ * 512:(h2_ + 1) * 512],
                                        lhsT=selc16[:, (16 + si) * 128:(17 + si) * 128],
                                        rhs=bc16[:, h2_ * 512:(h2_ + 1) * 512],
                                        start=True, stop=True)
                    mmc.then_inc(cP.sem, 1)
                    cP.n += 1
                    rec(mm_ + f"bb{si}", cP)
            rec(mm_ + "pend", cP)

            @block.scalar
            def _(s):
                for tc in range(NTC):
                    wge(s, cP, mm_ + f"xproj{tc}")
                    s.copy(out=xpa[0:64, tc * 512:(tc + 1) * 512],
                           in_=pm_dt[:]).then_inc(cS.sem, 1)
                    cS.n += 1
                    s.copy(out=bc16[:, tc * 512:(tc + 1) * 512],
                           in_=pm_bc[:]).then_inc(cS.sem, 1)
                    cS.n += 1
                    rec(mm_ + f"xw{tc}", cS)
                    rec(mm_ + f"xpa{tc}", cS)
                    for c in range(DC if mstop >= 22 else 0):
                        kk = tc * DC + c
                        pi = kk % 2
                        gk = m * NKK + kk
                        sb3 = 3 * (kk % 2)
                        wge(s, cP, mm_ + f"dtT{tc}_{c}")
                        if gk >= 2 and mstop >= 23:
                            pgk = gk - 2
                            pmm = f"M{pgk // NKK}_"
                            ptc, pc = divmod(pgk % NKK, DC)
                            wge(s, cV, pmm + f"spw{ptc}_{pc}")
                        s.activation(out=sp_t[sb3][:], in_=pm_dtT[pi][:],
                                     func=AF.Abs).then_inc(cS.sem, 1)
                        cS.n += 1
                        s.wait_ge(cS.sem, cS.n)
                        s.activation(out=sp_t[sb3 + 1][:], in_=sp_t[sb3][:],
                                     func=AF.Exp, scale=-1.0).then_inc(cS.sem, 1)
                        cS.n += 1
                        s.wait_ge(cS.sem, cS.n)
                        s.activation(out=sp_t[sb3 + 2][:], in_=sp_t[sb3 + 1][:],
                                     func=AF.Ln, bias=1.0).then_inc(cS.sem, 1)
                        cS.n += 1
                        rec(mm_ + f"spl{tc}_{c}", cS)
                for si in range(DSTATE if mstop >= 30 else 0):
                    pr = si % 2
                    gsi = m * DSTATE + si
                    wge(s, cP, mm_ + f"bb{si}")
                    if gsi >= 2 and mstop >= 40:
                        pgs = gsi - 2
                        pmm = f"M{pgs // DSTATE}_"
                        wge(s, cV, pmm + f"bbuse{pgs % DSTATE}")
                    s.copy(out=bb_sb[pr][:], in_=pm_bb[:]).then_inc(cS.sem, 1)
                    cS.n += 1
                    s.copy(out=cb_sb[pr][:], in_=pm_cb[:]).then_inc(cS.sem, 1)
                    cS.n += 1
                    rec(mm_ + f"bbev{si}", cS)
                    for c in range(DC):
                        gr = (m * DSTATE + si) * DC + c
                        rot4 = gr % 4
                        if gr >= 4 and mstop >= 40:
                            pgr = gr - 4
                            pmm = f"M{pgr // (DSTATE * DC)}_"
                            r_ = pgr % (DSTATE * DC)
                            wge(s, cV, pmm + f"abw{r_ // DC}_{r_ % DC}")
                        s.activation(out=abar[rot4][:], in_=dtT[c][:], func=AF.Exp,
                                     scale=a_sb[:, c * 16 + si:c * 16 + si + 1]
                                     ).then_inc(cS.sem, 1)
                        cS.n += 1
                        rec(mm_ + f"ab{si}_{c}", cS)
            rec(mm_ + "send", cS)

            @block.vector
            def _(v):
                for tc in range(NTC if mstop >= 23 else 0):
                    for c in range(DC):
                        pi = (tc * DC + c) % 2
                        wge(v, cP, mm_ + f"dtT{tc}_{c}")
                        wge(v, cS, mm_ + f"spl{tc}_{c}")
                        v.tensor_scalar(out=dtT[c][:, tc * 512:(tc + 1) * 512],
                                        in0=pm_dtT[pi][:], scalar1=0.0, scalar2=None,
                                        op0=AluOpType.max).then_inc(cV.sem, 1)
                        cV.n += 1
                        rec(mm_ + f"dtw{tc}_{c}", cV)
                        v.wait_ge(cV.sem, cV.n)
                        v.tensor_tensor(out=dtT[c][:, tc * 512:(tc + 1) * 512],
                                        in0=dtT[c][:, tc * 512:(tc + 1) * 512],
                                        in1=sp_t[3 * ((tc * DC + c) % 2) + 2][:],
                                        op=AluOpType.add
                                        ).then_inc(cV.sem, 1)
                        cV.n += 1
                        rec(mm_ + f"spw{tc}_{c}", cV)
                for c in range(DC if mstop >= 24 else 0):
                    v.wait_ge(cV.sem, cV.n)
                    v.tensor_tensor(out=dtx[c][:], in0=dtT[c][:], in1=xnT[c][:],
                                    op=AluOpType.mult).then_inc(cV.sem, 1)
                    cV.n += 1
                rec(mm_ + "dtxall", cV)

            for si in range(DSTATE if mstop >= 40 else 0):
                @block.vector
                def _(v, si=si):
                    pr = si % 2
                    for c in range(DC):
                        gr = (m * DSTATE + si) * DC + c
                        rot2 = gr % 2
                        rot4 = gr % 4
                        wge(v, cS, mm_ + f"bbev{si}")
                        v.wait_ge(cV.sem, cV.n)
                        v.tensor_tensor(out=ubuf[rot2][:], in0=dtx[c][:],
                                        in1=bb_sb[pr][:], op=AluOpType.mult
                                        ).then_inc(cV.sem, 1)
                        cV.n += 1
                        wge(v, cS, mm_ + f"ab{si}_{c}")
                        v.wait_ge(cV.sem, cV.n)
                        ini = 0.0 if th == 0 else hcar[:, c * 16 + si:c * 16 + si + 1]
                        v.tensor_tensor_scan(out=hbuf[rot2][:], data0=abar[rot4][:],
                                             data1=ubuf[rot2][:], initial=ini,
                                             op0=AluOpType.mult,
                                             op1=AluOpType.add).then_inc(cV.sem, 1)
                        cV.n += 1
                        rec(mm_ + f"abw{si}_{c}", cV)
                        v.wait_ge(cV.sem, cV.n)
                        v.tensor_tensor(out=tmpb[rot2][:], in0=hbuf[rot2][:],
                                        in1=cb_sb[pr][:], op=AluOpType.mult
                                        ).then_inc(cV.sem, 1)
                        cV.n += 1
                        v.wait_ge(cV.sem, cV.n)
                        if si == 0:
                            v.tensor_copy(out=ybuf[c][:], in_=tmpb[rot2][:]
                                          ).then_inc(cV.sem, 1)
                        else:
                            v.tensor_tensor(out=ybuf[c][:], in0=ybuf[c][:],
                                            in1=tmpb[rot2][:], op=AluOpType.add
                                            ).then_inc(cV.sem, 1)
                        cV.n += 1
                        if th < NTH - 1:
                            v.wait_ge(cV.sem, cV.n)
                            v.tensor_copy(out=hcar[:, c * 16 + si:c * 16 + si + 1],
                                          in_=hbuf[rot2][:, THS - 1:THS]
                                          ).then_inc(cV.sem, 1)
                            cV.n += 1
                    rec(mm_ + f"bbuse{si}", cV)

            @block.vector
            def _(v):
                for c in range(DC if mstop >= 50 else 0):
                    v.wait_ge(cV.sem, cV.n)
                    gsc = m * DC + c
                    if gsc >= 2:
                        pgs = gsc - 2
                        wge(v, cWs[c % 2], f"M{pgs // DC}_spw{pgs % DC}w")
                    v.scalar_tensor_tensor(out=soutb[c % 2][:], in0=xnT[c][:],
                                           scalar=dvec[:, c:c + 1], in1=ybuf[c][:],
                                           op0=AluOpType.mult,
                                           op1=AluOpType.add).then_inc(cV.sem, 1)
                    cV.n += 1
                    rec(mm_ + f"sout{c}", cV)
            rec(mm_ + "vend", cV)

            @block.scalar
            def _(s):
                for c in range(DC if mstop >= 50 else 0):
                    wge(s, cV, mm_ + f"sout{c}")
                    s.dma_start(out=s16[b, c, :, t0:t0 + THS],
                                in_=soutb[c % 2][:]).then_inc(cWs[c % 2].sem, 16)
                    cWs[c % 2].n += 16
                    rec(mm_ + f"spw{c}w", cWs[c % 2])
        MAMBA_W0, MAMBA_W1 = cWs[0].n, cWs[1].n

    if stage >= 3:
        for t_ in reversed(mamba_ctxs):
            t_.__exit__(None, None, None)
        for t_ in reversed(p0_ctxs):
            t_.__exit__(None, None, None)

        fctxs = []
        def fsb(name, shape, dt):
            t = nc.sbuf_tensor(name, shape, dt)
            h = t.__enter__(); fctxs.append(t); return h
        def fps(name, shape, dt):
            t = nc.psum_tensor(name, shape, dt)
            h = t.__enter__(); fctxs.append(t); return h

        wq16 = [fsb(f"f_wq{c}", [128, 1536], BF16) for c in range(DC)]
        wo16h = [fsb(f"f_wo{h}", [64, 1024], BF16) for h in range(NHEAD)]
        ssq16 = fsb("f_ssq", [128, T], BF16); ssk16 = fsb("f_ssk", [128, T], BF16)
        csqf = fsb("f_csqf", [128, T], BF16); cskf = fsb("f_cskf", [128, T], BF16)
        rperm16 = fsb("f_rp", [128, 128], BF16)
        rpermq16 = fsb("f_rpq", [128, 128], BF16)
        qscol = fsb("f_qs", [128, 1], F32); kscol = fsb("f_ks", [128, 1], F32)
        hd2_16 = fsb("f_hd2", [128, 2], BF16); hd2T16 = fsb("f_hd2T", [2, 128], BF16)
        ones16 = fsb("f_ones", [128, 128], BF16)
        onesf32 = fsb("f_onesf32", [65, 128], F32)
        trim_f = fsb("f_trim", [128, 256], BF16)
        trilo16 = fsb("f_trilo", [128, 128], BF16); triup16 = fsb("f_triup", [128, 128], BF16)
        lngc = fsb("f_lng", [128, DC], F32); lnbc = fsb("f_lnb", [128, DC], F32)
        b1ac = fsb("f_b1a", [128, 22], F32); b1gc = fsb("f_b1g", [128, 22], F32)
        b2c = fsb("f_b2", [128, DC], F32)

        residb = [[fsb(f"f_res{p}_{c}", [128, 512], BF16) for c in range(DC)]
                  for p in range(2)]
        hshb = [[fsb(f"f_hsh{p}_{j}", [128, 512], BF16) for j in range(4)]
                for p in range(2)]
        sq_sb = [fsb(f"f_sq{i}", [128, 512], BF16) for i in range(2)]
        n2sb = fsb("f_n2", [2, 512], F32)
        rstd2 = [fsb(f"f_rstd{i}", [2, 512], BF16) for i in range(2)]
        rb_sb = [fsb(f"f_rb{i}", [128, 512], BF16) for i in range(2)]
        qn_sb = [fsb(f"f_qn{i}", [128, 512], BF16) for i in range(2)]
        t1_sb = [fsb(f"f_t1{i}", [128, 512], BF16) for i in range(2)]
        qT = [fsb(f"f_qT{p}", [128, 512], BF16) for p in range(4)]
        kTA = [fsb(f"f_kTA{p}", [128, 512], BF16) for p in range(4)]
        kTB = [fsb(f"f_kTB{p}", [128, 512], BF16) for p in range(4)]
        vtA = [fsb(f"f_vtA{t}", [128, 8 * 65], BF16) for t in range(4)]
        vtB = [fsb(f"f_vtB{t}", [128, 8 * 65], BF16) for t in range(4)]
        nm32 = fsb("f_nm32", [128, 34], F32)
        nm32b = fsb("f_nm32b", [128, 32], BF16)
        nmT_sb = fsb("f_nmT", [32, 128], BF16)
        mrowcat = fsb("f_mrow", [65, 2048], BF16)
        pT = [fsb(f"f_pT{k}", [128, 512], BF16) for k in range(8)]
        rdenb = fsb("f_rden", [128, 512], F32)
        rd_sb = fsb("f_rd", [64, 512], F32)
        oT = [fsb(f"f_oT{h}", [64, 512], BF16) for h in range(NHEAD)]
        strt = [fsb(f"f_strt{i}", [128, 512], BF16) for i in range(2)]
        sh2 = {c: fsb(f"f_sh2{c}", [128, 512], BF16) for c in range(4, DC)}
        shcolb = fsb("f_shcol", [128, 32], BF16)
        res2 = [fsb(f"f_res2{c}", [128, 512], BF16) for c in range(DC)]
        lnrow_f = fsb("f_lnrf", [1, 1024], F32)
        lnrow_b = fsb("f_lnrb", [1, 1024], BF16)
        h2 = [fsb(f"f_h2{c}", [128, 512], BF16) for c in range(DC)]
        w1sl = [fsb(f"f_w1s{p}", [128, 2048], BF16) for p in range(2)]
        gel_sb = [fsb(f"f_gel{i}", [128, 512], BF16) for i in range(2)]
        hidst = [fsb(f"f_hid{j}", [128, 512], BF16) for j in range(22)]
        w2sl = [fsb(f"f_w2s{p}", [128, 1024], BF16) for p in range(2)]
        murow = lnrow_b[:, 0:512]
        rstdrow = lnrow_b[:, 512:1024]
        m2row = lnrow_f[:, 0:512]
        varrow = lnrow_f[:, 512:1024]
        if T >= 2048:
            xrl = csqf[:].bitcast(F32)[:, 0:1024]
            obuf = cskf[:].bitcast(F32)[:, 0:1024]
        else:
            xrl = fsb("f_xrl", [128, 1024], F32)[:]
            obuf = fsb("f_obuf", [128, 1024], F32)[:]
        lntmp_ap = rdenb[:]
        lntmp2_ap = obuf[:, 0:512]

        pool = [fps(f"pf_p{i}", [128, 512], F32) for i in range(6)]
        pf_ss = fps("pf_ss", [128, 1024], F32)

        fL = [Ch(nc, "f_L0"), Ch(nc, "f_L1")]
        fG = Ch(nc, "f_G")
        fGw1 = [Ch(nc, "f_Gw1a"), Ch(nc, "f_Gw1b")]
        fGw2 = [Ch(nc, "f_Gw2a"), Ch(nc, "f_Gw2b")]
        fP = Ch(nc, "f_P"); fS = Ch(nc, "f_S"); fV = Ch(nc, "f_V")
        fWs = [Ch(nc, "f_W0"), Ch(nc, "f_W1")]
        fGm = Ch(nc, "f_Gm")
        bar = Ch(nc, "f_bar")

        @block.gpsimd
        def _(g):
            for c in [cL, cG, cP, cS, cV, cWs[0], cWs[1]]:
                g.wait_ge(c.sem, c.n)
            g.wait_ge(ch_sp.sem, ch_sp.n)
            g.memset(nm32[0:1, 0:1], 0.0).then_inc(bar.sem, 1)
            bar.n += 1
        BAR0 = bar.n

        @block.gpsimd
        def _(g):
            g.wait_ge(bar.sem, BAR0)
            for dst, src_ in [(ssq16[:], ssq_in), (ssk16[:], ssk_in),
                              (hd2_16[:], hd2_in), (hd2T16[:], hd2T_in),
                              (ones16[:], onesf_in), (onesf32[:], onesf_in[0:65, :]),
                              (trim_f[:], trimadd_in), (trilo16[:], trilo_in),
                              (triup16[:], triup_in)]:
                g.dma_start(out=dst, in_=src_).then_inc(fG.sem, 16)
                fG.n += 16
            for t_ in range(4):
                g.memset(vtA[t_][:, :].rearrange("p (h c) -> p h c", h=8)[:, :, 64:65],
                         1.0).then_inc(fGm.sem, 1)
                fGm.n += 1
                g.memset(vtB[t_][:, :].rearrange("p (h c) -> p h c", h=8)[:, :, 64:65],
                         1.0).then_inc(fGm.sem, 1)
                fGm.n += 1
            for j in range(4):
                if j < 3:
                    g.memset(pT[j][:, (j + 1) * 128:512], 0.0).then_inc(fGm.sem, 1)
                    fGm.n += 1
                if j > 0:
                    g.memset(pT[4 + j][:, 0:j * 128], 0.0).then_inc(fGm.sem, 1)
                    fGm.n += 1
        FGM_ALL = fGm.n

        def _emit_attn_window(l, b, w, strm_in, strm_out):
            t0 = w * WIN
            kTc = kTA if w % 2 == 0 else kTB
            kTp = kTB if w % 2 == 0 else kTA
            vtc = vtA if w % 2 == 0 else vtB
            vtp = vtB if w % 2 == 0 else vtA
            wp = f"F{l}A{b}_{w}_"
            idx = b * NW + w
            resid = residb[idx % 2]
            hsh = hshb[idx % 2]
            gwp = None
            if idx >= 1:
                b1_, w1_ = divmod(idx - 1, NW)
                gwp = f"F{l}A{b1_}_{w1_}_"
            g2p = None
            if idx >= 2:
                b2_, w2_ = divmod(idx - 2, NW)
                g2p = f"F{l}A{b2_}_{w2_}_"
            qku = [("q", hp) for hp in range(4)] + [("k", hp) for hp in range(4)]
            kts = ([("p", j) for j in range(4)] if w > 0 else []) + \
                  [("c", j) for j in range(4)]

            @block.sync
            def _(sync):
                if g2p is not None:
                    wge(sync, fP, g2p + "pe_end")
                    wge(sync, fV, g2p + "v_end")
                if gwp is not None:
                    pass
                elif l > 0:
                    wge(sync, fWs[0], f"F{l-1}_B_done_W0")
                    wge(sync, fWs[1], f"F{l-1}_B_done_W1")
                    wge(sync, fV, f"F{l-1}_B_v_end")
                    wge(sync, fP, f"F{l-1}_B_pe_end")
                else:
                    sync.wait_ge(bar.sem, BAR0)
                fl = fL[idx % 2]
                for c in range(DC):
                    sync.dma_start(out=resid[c][:],
                                   in_=strm_in[b, c, :, t0:t0 + 512]).then_inc(fl.sem, 16)
                    fl.n += 16
                for j in range(4):
                    if w > 0:
                        sync.dma_start(out=hsh[j][:],
                                       in_=strm_in[b, 4 + j, :, t0 - 1:t0 + 511]
                                       ).then_inc(fl.sem, 16)
                    else:
                        sync.dma_start(out=hsh[j][:, 1:512],
                                       in_=strm_in[b, 4 + j, :, 0:511]).then_inc(fl.sem, 16)
                    fl.n += 16
            rec(wp + "loads", fL[idx % 2])

            def srhs(kc):
                return resid[kc][:] if kc < 4 else hsh[kc - 4][:]

            @block.gpsimd
            def _(g):
                wge(g, fS, wp + "nmT")
                if gwp is not None:
                    wge(g, fP, gwp + "st_all")
                g.dma_start(out=mrow_dram[b, w, :].rearrange("(a c) -> a c", a=32),
                            in_=nmT_sb[:]).then_inc(fG.sem, 16)
                fG.n += 16
                g.wait_ge(fG.sem, fG.n)
                g.dma_start(out=mrowcat[0:1, :],
                            in_=mrow_dram[b, w, 0:2048][None, :]).then_inc(fG.sem, 16)
                fG.n += 16
                g.dma_start(out=mrowcat[64:65, :],
                            in_=mrow_dram[b, w, 2048:4096][None, :]).then_inc(fG.sem, 16)
                fG.n += 16
            rec(wp + "mrow", fG)

            qkbank = [pool[0][:], pool[1][:], pf_ss[:, 0:512]]

            @block.tensor
            def _(te):
                wge(te, fL[idx % 2], wp + "loads")
                if idx == 0:
                    te.wait_ge(fGm.sem, FGM_ALL)
                    wge(te, fG, f"F{l}_wl")
                    wge(te, fV, f"F{l}_tabs_done")
                if w == 0:
                    wge(te, fV, wp + "hshz")
                for k in range(len(qku) + 3):
                    if k < 8:
                        qx, hp = qku[k]
                        if k >= 3:
                            wge(te, fV, wp + f"qn{k-3}")
                        elif gwp is not None:
                            wge(te, fV, gwp + ("strm6" if k == 0 else
                                               "strm7" if k == 1 else "nm32b"))
                        off = 0 if qx == "q" else 512
                        for kc in range(DC):
                            mmq = te.matmul(out=qkbank[k % 3],
                                            lhsT=wq16[kc][:, off + hp * 128:off + (hp + 1) * 128],
                                            rhs=srhs(kc),
                                            start=(kc == 0), stop=(kc == DC - 1))
                        mmq.then_inc(fP.sem, 1)
                        fP.n += 1
                        rec(wp + f"qk{k}", fP)
                    if 0 <= k - 1 < 8:
                        kk = k - 1
                        wge(te, fS, wp + f"sq{kk}")
                        if kk >= 1:
                            wge(te, fV, wp + f"nmax{kk-1}")
                        elif gwp is not None:
                            wge(te, fV, gwp + "nmax7")
                        te.matmul(out=pool[2][0:2, :], lhsT=hd2_16[:],
                                  rhs=sq_sb[kk % 2][:], start=True,
                                  stop=True).then_inc(fP.sem, 1)
                        fP.n += 1
                        rec(wp + f"n2{kk}", fP)
                    if 0 <= k - 2 < 8:
                        kk = k - 2
                        wge(te, fS, wp + f"rstd{kk}")
                        if kk >= 1:
                            wge(te, fS, wp + f"rbc{kk-1}")
                        te.matmul(out=pool[3][:], lhsT=hd2T16[:],
                                  rhs=rstd2[kk % 2][:], start=True,
                                  stop=True).then_inc(fP.sem, 1)
                        fP.n += 1
                        rec(wp + f"rbb{kk}", fP)
                    if 0 <= k - 3 < 8:
                        kk = k - 3
                        qx, hp = qku[kk]
                        wge(te, fV, wp + f"qn{kk}")
                        if kk >= 1:
                            wge(te, fV, wp + f"t2{kk-1}")
                        te.matmul(out=pool[4][:], lhsT=(rpermq16 if qx == "q" else rperm16)[:],
                                  rhs=qn_sb[kk % 2][:], start=True,
                                  stop=True).then_inc(fP.sem, 1)
                        fP.n += 1
                        rec(wp + f"rot{kk}", fP)
                for ts in range(4):
                    if ts >= 1:
                        wge(te, fV, wp + f"vt{ts-1}")
                    for kc in range(DC):
                        mmv = te.matmul(out=pool[5][:],
                                        lhsT=srhs(kc)[:, ts * 128:(ts + 1) * 128],
                                        rhs=wq16[kc][:, 1024:1536],
                                        start=(kc == 0), stop=(kc == DC - 1))
                    mmv.then_inc(fP.sem, 1)
                    fP.n += 1
                    rec(wp + f"vp{ts}", fP)
                for h in range(NHEAD):
                    hp, hi = h // 2, h % 2
                    base = hi * 64
                    wge(te, fV, wp + "qT" + str(("q", hp)))
                    wge(te, fV, wp + "qT" + str(("k", hp)))
                    for qc in range(4):
                        w1_ = 512 - qc * 128
                        w2_ = qc * 128 + 128
                        if (h, qc) != (0, 0):
                            wge(te, fV, wp + (f"stats{h}_{qc-1}" if qc > 0
                                              else f"stats{h-1}_3"))
                        else:
                            wge(te, fV, wp + "qn7")
                        if w > 0:
                            te.matmul(out=pf_ss[:, 0:w1_],
                                      lhsT=qT[hp][base:base + 64, qc * 128:(qc + 1) * 128],
                                      rhs=kTp[hp][base:base + 64, qc * 128:512],
                                      start=True, stop=True)
                            te.matmul(out=pf_ss[:, 512:512 + w2_],
                                      lhsT=qT[hp][base:base + 64, qc * 128:(qc + 1) * 128],
                                      rhs=kTc[hp][base:base + 64, 0:w2_],
                                      start=True, stop=True).then_inc(fP.sem, 1)
                        else:
                            te.matmul(out=pf_ss[:, 0:w2_],
                                      lhsT=qT[hp][base:base + 64, qc * 128:(qc + 1) * 128],
                                      rhs=kTc[hp][base:base + 64, 0:w2_],
                                      start=True, stop=True).then_inc(fP.sem, 1)
                        fP.n += 1
                        rec(wp + f"ss{h}_{qc}", fP)
                p2bf = pool[2][:].bitcast(BF16)
                wge(te, fV, wp + "nm32b")
                te.transpose(out=p2bf[0:32, 0:128], in_=nm32b[:],
                             identity=ident16[:]).then_inc(fP.sem, 1)
                fP.n += 1
                rec(wp + "nmT_pe", fP)
                for h in range(NHEAD):
                    hp, hi = h // 2, h % 2
                    base = hi * 64
                    for ki, (src_, j) in enumerate(kts):
                        if src_ == "p":
                            r0, r1 = 0, (j + 1) * 128
                            ktsrc = kTp
                        else:
                            r0, r1 = j * 128, 512
                            ktsrc = kTc
                        pp = pool[ki % 2]
                        if h > 0 or ki >= 2:
                            ph, pki = (h, ki - 2) if ki >= 2 else (h - 1, len(kts) - 2 + ki)
                            wge(te, fS, wp + f"exp{ph}_{pki}")
                        if h == 0 and ki < 2:
                            wge(te, fG, wp + "mrow")
                        te.matmul(out=pp[:, r0:r1],
                                  lhsT=ktsrc[hp][base:base + 64, j * 128:(j + 1) * 128],
                                  rhs=qT[hp][base:base + 64, r0:r1],
                                  start=True, stop=False)
                        mseg = (h // 2) * 512
                        te.matmul(out=pp[:, r0:r1],
                                  lhsT=ones16[base:base + 1, 0:128],
                                  rhs=mrowcat[base:base + 1, mseg + r0:mseg + r1],
                                  start=False, stop=True).then_inc(fP.sem, 1)
                        fP.n += 1
                        rec(wp + f"st{h}_{ki}", fP)
                    if h == 0:
                        wge(te, fV, wp + "vt3")
                    else:
                        wge(te, fV, wp + f"oT{h-1}")
                    first = True
                    for ki, (src_, j) in enumerate(kts):
                        kt_idx = j if src_ == "p" else 4 + j
                        vt_ = vtp if src_ == "p" else vtc
                        wge(te, fV, wp + f"pm{h}_{ki}")
                        mmp = te.matmul(out=pool[5][0:65, :],
                                        lhsT=vt_[j][:, h * 65:(h + 1) * 65],
                                        rhs=pT[kt_idx][:, 0:512],
                                        start=first, stop=(ki == len(kts) - 1))
                        first = False
                    mmp.then_inc(fP.sem, 1)
                    fP.n += 1
                    rec(wp + f"pv{h}", fP)
                    wge(te, fS, wp + f"rden{h}")
                    te.matmul(out=pool[3][0:64, :], lhsT=onesf32[64:65, 0:64],
                              rhs=rdenb[64:65, :], start=True,
                              stop=True).then_inc(fP.sem, 1)
                    fP.n += 1
                    rec(wp + f"rdb{h}", fP)
                rec(wp + "st_all", fP)
                for dc in range(DC):
                    wge(te, fV, wp + "oT7")
                    if dc >= 2:
                        wge(te, fV, wp + f"strm{dc-2}")
                    for h in range(NHEAD):
                        mmw = te.matmul(out=pool[dc % 2][:],
                                        lhsT=wo16h[h][:, dc * 128:(dc + 1) * 128],
                                        rhs=oT[h][:], start=(h == 0),
                                        stop=(h == NHEAD - 1))
                    mmw.then_inc(fP.sem, 1)
                    fP.n += 1
                    rec(wp + f"wo{dc}", fP)
            rec(wp + "pe_end", fP)

            @block.scalar
            def _(s):
                for k in range(len(qku) + 2):
                    if k < 8:
                        wge(s, fP, wp + f"qk{k}")
                        if k >= 2:
                            wge(s, fP, wp + f"n2{k-2}")
                        s.activation(out=sq_sb[k % 2][:], in_=qkbank[k % 3],
                                     func=AF.Square).then_inc(fS.sem, 1)
                        fS.n += 1
                        rec(wp + f"sq{k}", fS)
                    if 0 <= k - 1 < 8:
                        kk = k - 1
                        wge(s, fV, wp + f"nmax{kk}")
                        s.activation(out=n2sb[:], in_=n2sb[:], func=AF.Ln
                                     ).then_inc(fS.sem, 1)
                        fS.n += 1
                        s.wait_ge(fS.sem, fS.n)
                        if kk >= 2:
                            wge(s, fP, wp + f"rbb{kk-2}")
                        s.activation(out=rstd2[kk % 2][:], in_=n2sb[:], func=AF.Exp,
                                     scale=-0.5).then_inc(fS.sem, 1)
                        fS.n += 1
                        rec(wp + f"rstd{kk}", fS)
                    if 0 <= k - 2 < 8:
                        kk = k - 2
                        wge(s, fP, wp + f"rbb{kk}")
                        if kk >= 2:
                            wge(s, fV, wp + f"qn{kk-2}")
                        s.copy(out=rb_sb[kk % 2][:], in_=pool[3][:]).then_inc(fS.sem, 1)
                        fS.n += 1
                        rec(wp + f"rbc{kk}", fS)
                wge(s, fP, wp + "nmT_pe")
                if gwp is not None:
                    wge(s, fG, gwp + "mrow")
                p2bf = pool[2][:].bitcast(BF16)
                s.copy(out=nmT_sb[:], in_=p2bf[0:32, 0:128]).then_inc(fS.sem, 1)
                fS.n += 1
                rec(wp + "nmT", fS)
                for h in range(NHEAD):
                    for ki, (src_, j) in enumerate(kts):
                        kt_idx = j if src_ == "p" else 4 + j
                        r0, r1 = (0, (j + 1) * 128) if src_ == "p" else (j * 128, 512)
                        pp = pool[ki % 2]
                        wge(s, fP, wp + f"st{h}_{ki}")
                        if h > 0:
                            wge(s, fP, wp + f"pv{h-1}")
                        s.activation(out=pT[kt_idx][:, r0:r1], in_=pp[:, r0:r1],
                                     func=AF.Exp).then_inc(fS.sem, 1)
                        fS.n += 1
                        rec(wp + f"exp{h}_{ki}", fS)
                    wge(s, fP, wp + f"pv{h}")
                    s.activation(out=rdenb[64:65, :], in_=pool[5][64:65, :],
                                 func=AF.Ln).then_inc(fS.sem, 1)
                    fS.n += 1
                    s.wait_ge(fS.sem, fS.n)
                    s.activation(out=rdenb[64:65, :], in_=rdenb[64:65, :],
                                 func=AF.Exp, scale=-1.0).then_inc(fS.sem, 1)
                    fS.n += 1
                    rec(wp + f"rden{h}", fS)
                    wge(s, fP, wp + f"rdb{h}")
                    s.copy(out=rd_sb[:], in_=pool[3][0:64, :]).then_inc(fS.sem, 1)
                    fS.n += 1
                    rec(wp + f"rdc{h}", fS)
                for dc in range(DC):
                    wge(s, fV, wp + f"strm{dc}")
                    s.dma_start(out=strm_out[b, dc, :, t0:t0 + 512],
                                in_=strt[dc % 2][:]).then_inc(fWs[dc % 2].sem, 16)
                    fWs[dc % 2].n += 16
                    rec(wp + f"sp{dc}", fWs[dc % 2])
            rec(wp + "s_end", fS)

            @block.vector
            def _(v):
                if w == 0:
                    wge(v, fL[idx % 2], wp + "loads")
                    if g2p is not None:
                        wge(v, fP, g2p + "pe_end")
                    for j in range(4):
                        v.memset(hsh[j][:, 0:1], 0.0).then_inc(fV.sem, 1)
                        fV.n += 1
                    rec(wp + "hshz", fV)
                for k in range(len(qku) + 3):
                    if 0 <= k - 1 < 8:
                        kk = k - 1
                        wge(v, fP, wp + f"n2{kk}")
                        if kk >= 1:
                            wge(v, fS, wp + f"rstd{kk-1}")
                        v.tensor_scalar(out=n2sb[:], in0=pool[2][0:2, :],
                                        scalar1=cv[0:2, 2:3], scalar2=None,
                                        op0=AluOpType.max).then_inc(fV.sem, 1)
                        fV.n += 1
                        rec(wp + f"nmax{kk}", fV)
                    if 0 <= k - 2 < 8:
                        kk = k - 2
                        wge(v, fS, wp + f"rbc{kk}")
                        v.wait_ge(fV.sem, fV.n)
                        v.tensor_tensor(out=qn_sb[kk % 2][:], in0=qkbank[kk % 3],
                                        in1=rb_sb[kk % 2][:], op=AluOpType.mult
                                        ).then_inc(fV.sem, 1)
                        fV.n += 1
                        rec(wp + f"qn{kk}", fV)
                    if 0 <= k - 3 < 8:
                        kk = k - 3
                        qx, hp = qku[kk]
                        ctab = csqf if qx == "q" else cskf
                        stab = ssq16 if qx == "q" else ssk16
                        dstT = qT if qx == "q" else kTc
                        v.wait_ge(fV.sem, fV.n)
                        v.tensor_tensor(out=t1_sb[kk % 2][:], in0=qn_sb[kk % 2][:],
                                        in1=ctab[:, t0:t0 + 512],
                                        op=AluOpType.mult).then_inc(fV.sem, 1)
                        fV.n += 1
                        wge(v, fP, wp + f"rot{kk}")
                        v.wait_ge(fV.sem, fV.n)
                        v.tensor_tensor(out=qn_sb[kk % 2][:], in0=pool[4][:],
                                        in1=stab[:, t0:t0 + 512],
                                        op=AluOpType.mult).then_inc(fV.sem, 1)
                        fV.n += 1
                        rec(wp + f"t2{kk}", fV)
                        v.wait_ge(fV.sem, fV.n)
                        if g2p is not None and qx == "k":
                            wge(v, fP, g2p + "pe_end")
                        v.tensor_tensor(out=dstT[hp][:], in0=t1_sb[kk % 2][:],
                                        in1=qn_sb[kk % 2][:], op=AluOpType.add
                                        ).then_inc(fV.sem, 1)
                        fV.n += 1
                        rec(wp + "qT" + str((qx, hp)), fV)
                for ts in range(4):
                    wge(v, fP, wp + f"vp{ts}")
                    if g2p is not None:
                        wge(v, fP, g2p + "pe_end")
                    v.tensor_copy(
                        out=vtc[ts][:, :].rearrange("p (h c) -> p h c", h=8)[:, :, 0:64],
                        in_=pool[5][:, :].rearrange("p (h c) -> p h c", h=8)
                        ).then_inc(fV.sem, 1)
                    fV.n += 1
                    rec(wp + f"vt{ts}", fV)
                for h in range(NHEAD):
                    for qc in range(4):
                        w1_ = 512 - qc * 128
                        w2_ = qc * 128 + 128
                        wge(v, fP, wp + f"ss{h}_{qc}")
                        if w > 0:
                            v.tensor_tensor(out=pf_ss[:, 0:128], in0=pf_ss[:, 0:128],
                                            in1=trim_f[:, 0:128], op=AluOpType.add
                                            ).then_inc(fV.sem, 1)
                            fV.n += 1
                            v.tensor_tensor(out=pf_ss[:, 512 + w2_ - 128:512 + w2_],
                                            in0=pf_ss[:, 512 + w2_ - 128:512 + w2_],
                                            in1=trim_f[:, 128:256], op=AluOpType.add
                                            ).then_inc(fV.sem, 1)
                            fV.n += 1
                            v.wait_ge(fV.sem, fV.n)
                            v.tensor_reduce(out=nm32[:, 32:33], in_=pf_ss[:, 0:w1_],
                                            axis=AX.X, op=AluOpType.max).then_inc(fV.sem, 1)
                            fV.n += 1
                            v.tensor_reduce(out=nm32[:, 33:34], in_=pf_ss[:, 512:512 + w2_],
                                            axis=AX.X, op=AluOpType.max).then_inc(fV.sem, 1)
                            fV.n += 1
                            v.wait_ge(fV.sem, fV.n)
                            v.tensor_tensor(out=nm32[:, 32:33], in0=nm32[:, 32:33],
                                            in1=nm32[:, 33:34], op=AluOpType.max
                                            ).then_inc(fV.sem, 1)
                            fV.n += 1
                        else:
                            v.tensor_tensor(out=pf_ss[:, w2_ - 128:w2_],
                                            in0=pf_ss[:, w2_ - 128:w2_],
                                            in1=trim_f[:, 128:256], op=AluOpType.add
                                            ).then_inc(fV.sem, 1)
                            fV.n += 1
                            v.wait_ge(fV.sem, fV.n)
                            v.tensor_reduce(out=nm32[:, 32:33], in_=pf_ss[:, 0:w2_],
                                            axis=AX.X, op=AluOpType.max).then_inc(fV.sem, 1)
                            fV.n += 1
                        v.wait_ge(fV.sem, fV.n)
                        nmc = (h % 2) * 16 + (h // 2) * 4 + qc
                        v.tensor_scalar(out=nm32[:, nmc:nmc + 1],
                                        in0=nm32[:, 32:33], scalar1=-1.0, scalar2=None,
                                        op0=AluOpType.mult).then_inc(fV.sem, 1)
                        fV.n += 1
                        rec(wp + f"stats{h}_{qc}", fV)
                v.wait_ge(fV.sem, fV.n)
                v.tensor_copy(out=nm32b[:], in_=nm32[:, 0:32]).then_inc(fV.sem, 1)
                fV.n += 1
                rec(wp + "nm32b", fV)
                for h in range(NHEAD):
                    for ki, (src_, j) in enumerate(kts):
                        kt_idx = j if src_ == "p" else 4 + j
                        dg = j * 128
                        msk = trilo16 if src_ == "p" else triup16
                        wge(v, fS, wp + f"exp{h}_{ki}")
                        v.tensor_tensor(out=pT[kt_idx][:, dg:dg + 128],
                                        in0=pT[kt_idx][:, dg:dg + 128],
                                        in1=msk[:], op=AluOpType.mult).then_inc(fV.sem, 1)
                        fV.n += 1
                        rec(wp + f"pm{h}_{ki}", fV)
                    wge(v, fS, wp + f"rdc{h}")
                    v.tensor_tensor(out=oT[h][:], in0=pool[5][0:64, :],
                                    in1=rd_sb[:], op=AluOpType.mult).then_inc(fV.sem, 1)
                    fV.n += 1
                    rec(wp + f"oT{h}", fV)
                    if h == NHEAD - 1:
                        rec(wp + "oT7", fV)
                for dc in range(DC):
                    wge(v, fP, wp + f"wo{dc}")
                    if dc >= 2:
                        wge(v, fWs[dc % 2], wp + f"sp{dc-2}")
                    elif gwp is not None:
                        wge(v, fWs[dc % 2], gwp + f"sp{dc+6}")
                    v.tensor_tensor(out=strt[dc % 2][:], in0=resid[dc][:],
                                    in1=pool[dc % 2][:], op=AluOpType.add
                                    ).then_inc(fV.sem, 1)
                    fV.n += 1
                    rec(wp + f"strm{dc}", fV)
                    if dc >= 4 and w < NW - 1:
                        sci = (b * NW + (w + 1)) * 4 + (dc - 4)
                        v.wait_ge(fV.sem, fV.n)
                        v.tensor_copy(out=shcolb[:, sci:sci + 1],
                                      in_=strt[dc % 2][:, 511:512]).then_inc(fV.sem, 1)
                        fV.n += 1
            rec(wp + "v_end", fV)

        def _emit_ffnmid_unit(l, b, u, strm):
            t0 = u * 512
            up = f"F{l}B{b}_{u}_"
            NU = T // 512
            idx = b * NU + u
            pup = None
            if idx > 0:
                pb, pu = divmod(idx - 1, NU)
                pup = f"F{l}B{pb}_{pu}_"

            @block.sync
            def _(sync):
                if pup is None:
                    wge(sync, fWs[0], f"F{l}_A_done_W0")
                    wge(sync, fWs[1], f"F{l}_A_done_W1")
                    wge(sync, fV, f"F{l}A{NB-1}_{NW-1}_v_end")
                else:
                    wge(sync, fV, pup + "v_end")
                    wge(sync, fP, pup + "pe_end")
                fl = fL[idx % 2]
                for c in range(DC):
                    sync.dma_start(out=res2[c][:],
                                   in_=strm[b, c, :, t0:t0 + 512]).then_inc(fl.sem, 16)
                    fl.n += 16
                for c in range(4, DC):
                    sync.dma_start(out=sh2[c][:, 1:512],
                                   in_=strm[b, c, :, t0:t0 + 511]).then_inc(fl.sem, 16)
                    fl.n += 16
            rec(up + "loads", fL[idx % 2])

            shf = [res2[0], res2[1], res2[2], res2[3], sh2[4], sh2[5], sh2[6], sh2[7]]

            def w1banka(ja, aw):
                return (pool[4][0:aw, :] if ja % 2 == 0 else pf_ss[0:aw, 0:512])

            def w1bankg(ja, aw):
                return (pool[5][0:aw, :] if ja % 2 == 0 else pf_ss[0:aw, 512:1024])

            @block.gpsimd
            def _(g):
                for ja in range(22):
                    aw = 128 if ja < 21 else HID - 21 * 128
                    pg = ja % 2
                    gja = idx * 22 + ja
                    if gja >= 2:
                        pg_ = gja - 2
                        pidx2, pja = divmod(pg_, 22)
                        pb2, pu2 = divmod(pidx2, NU)
                        wge(g, fP, f"F{l}B{pb2}_{pu2}_w1_{pja}")
                    elif l > 0 and ja < 2:
                        wge(g, fP, f"F{l-1}_B_pe_end")
                    gw = fGw1[pg]
                    g.dma_start(
                        out=w1sl[pg][:, 0:1024].rearrange("p (k j) -> p k j", k=8)[:, :, 0:aw],
                        in_=w1_in[l, :, ja * 128:ja * 128 + aw]
                            .rearrange("(k p) j -> p k j", p=128)).then_inc(gw.sem, 16)
                    gw.n += 16
                    g.dma_start(
                        out=w1sl[pg][:, 1024:2048].rearrange("p (k j) -> p k j", k=8)[:, :, 0:aw],
                        in_=w1_in[l, :, HID + ja * 128:HID + ja * 128 + aw]
                            .rearrange("(k p) j -> p k j", p=128)).then_inc(gw.sem, 16)
                    gw.n += 16
                    rec(up + f"gw1_{ja}", gw)
                for jc in range(22):
                    aw = 128 if jc < 21 else HID - 21 * 128
                    pg = jc % 2
                    kidx = idx * 22 + jc
                    if kidx >= 2:
                        pg_ = kidx - 2
                        pidx2, pjc = divmod(pg_, 22)
                        pb2, pu2 = divmod(pidx2, NU)
                        wge(g, fP, f"F{l}B{pb2}_{pu2}_w2_{pjc}")
                    gw = fGw2[pg]
                    g.dma_start(out=w2sl[pg][0:aw, :],
                                in_=w2_in[l, jc * 128:jc * 128 + aw, :]).then_inc(gw.sem, 16)
                    gw.n += 16
                    rec(up + f"gw2_{jc}", gw)

            @block.tensor
            def _(te):
                wge(te, fS, up + "sqs")
                wge(te, fL[idx % 2], up + "loads")
                if pup is not None:
                    wge(te, fV, pup + "out_done")
                for c in range(DC):
                    mms = te.matmul(out=pool[0][0:1, :], lhsT=ones16[:, 0:1],
                                    rhs=shf[c][:], start=(c == 0), stop=(c == DC - 1))
                mms.then_inc(fP.sem, 1)
                fP.n += 1
                for c in range(DC):
                    mms2 = te.matmul(out=pool[1][0:1, :], lhsT=ones16[:, 0:1],
                                     rhs=h2[c][:], start=(c == 0), stop=(c == DC - 1))
                mms2.then_inc(fP.sem, 1)
                fP.n += 1
                rec(up + "lnsums", fP)
                wge(te, fS, up + "rsr")
                te.matmul(out=pool[2][:], lhsT=ones16[0:1, :], rhs=murow,
                          start=True, stop=True).then_inc(fP.sem, 1)
                fP.n += 1
                te.matmul(out=pool[3][:], lhsT=ones16[0:1, :], rhs=rstdrow,
                          start=True, stop=True).then_inc(fP.sem, 1)
                fP.n += 1
                rec(up + "lnbcast", fP)
                for ja in range(22):
                    aw = 128 if ja < 21 else HID - 21 * 128
                    pg = ja % 2
                    wge(te, fGw1[pg], up + f"gw1_{ja}")
                    if ja == 0:
                        wge(te, fV, up + "h2_7")
                    if ja >= 2:
                        wge(te, fV, up + f"hid{ja-2}")
                    elif pup is not None:
                        wge(te, fV, pup + "out_done")
                    for kc in range(DC):
                        mma = te.matmul(out=w1banka(ja, aw),
                                        lhsT=w1sl[pg][:, kc * 128:kc * 128 + aw],
                                        rhs=h2[kc][:], start=(kc == 0),
                                        stop=(kc == DC - 1))
                    mma.then_inc(fP.sem, 1)
                    fP.n += 1
                    for kc in range(DC):
                        mmg = te.matmul(out=w1bankg(ja, aw),
                                        lhsT=w1sl[pg][:, 1024 + kc * 128:1024 + kc * 128 + aw],
                                        rhs=h2[kc][:], start=(kc == 0),
                                        stop=(kc == DC - 1))
                    mmg.then_inc(fP.sem, 1)
                    fP.n += 1
                    rec(up + f"w1_{ja}", fP)
                def w2bank(dc):
                    if dc < 6:
                        return pool[dc][:]
                    return pf_ss[:, 0:512] if dc == 6 else pf_ss[:, 512:1024]
                for jc in range(22):
                    aw = 128 if jc < 21 else HID - 21 * 128
                    pg = jc % 2
                    wge(te, fGw2[pg], up + f"gw2_{jc}")
                    wge(te, fV, up + f"hid{jc}")
                    if jc == 0:
                        wge(te, fV, up + "h2_7")
                        wge(te, fV, up + "hid21")
                    for dc in range(DC):
                        te.matmul(out=w2bank(dc),
                                  lhsT=w2sl[pg][0:aw, dc * 128:(dc + 1) * 128],
                                  rhs=hidst[jc][0:aw, :],
                                  start=(jc == 0), stop=(jc == 21)
                                  ).then_inc(fP.sem, 1)
                        fP.n += 1
                        if jc == 21:
                            rec(up + f"w2f{dc}", fP)
                    rec(up + f"w2_{jc}", fP)
            rec(up + "pe_end", fP)

            @block.scalar
            def _(s):
                wge(s, fL[idx % 2], up + "loads")
                wge(s, fV, up + "shz")
                if pup is not None:
                    wge(s, fP, pup + "lnsums")
                    wge(s, fV, pup + "out_done")
                elif l > 0:
                    wge(s, fP, f"F{l-1}_B_pe_end")
                for c in range(DC):
                    s.activation(out=h2[c][:], in_=shf[c][:], func=AF.Square
                                 ).then_inc(fS.sem, 1)
                    fS.n += 1
                rec(up + "sqs", fS)
                wge(s, fV, up + "mu")
                s.activation(out=m2row, in_=murow, func=AF.Square).then_inc(fS.sem, 1)
                fS.n += 1
                rec(up + "m2", fS)
                wge(s, fV, up + "var")
                s.activation(out=varrow, in_=varrow, func=AF.Ln,
                             bias=cv[0:1, 3:4]).then_inc(fS.sem, 1)
                fS.n += 1
                s.wait_ge(fS.sem, fS.n)
                s.activation(out=rstdrow, in_=varrow, func=AF.Exp,
                             scale=-0.5).then_inc(fS.sem, 1)
                fS.n += 1
                rec(up + "rsr", fS)
                for ja in range(22):
                    aw = 128 if ja < 21 else HID - 21 * 128
                    wge(s, fP, up + f"w1_{ja}")
                    if ja >= 2:
                        wge(s, fV, up + f"hid{ja-2}")
                    s.activation(out=gel_sb[ja % 2][0:aw, :], in_=w1bankg(ja, aw),
                                 func=AF.Gelu, bias=b1gc[0:aw, ja:ja + 1]
                                 ).then_inc(fS.sem, 1)
                    fS.n += 1
                    rec(up + f"gel{ja}", fS)
                for dc in range(DC):
                    wge(s, fV, up + f"out{dc}")
                    s.dma_start(out=strm[b, dc, :, t0:t0 + 512],
                                in_=strt[dc % 2][:]).then_inc(fWs[dc % 2].sem, 16)
                    fWs[dc % 2].n += 16
                    rec(up + f"bsp{dc}", fWs[dc % 2])

            @block.vector
            def _(v):
                wge(v, fL[idx % 2], up + "loads")
                if pup is not None:
                    wge(v, fP, pup + "pe_end")
                elif l > 0:
                    wge(v, fP, f"F{l-1}_B_pe_end")
                if t0 == 0:
                    for c in range(4, DC):
                        v.memset(sh2[c][:, 0:1], 0.0).then_inc(fV.sem, 1)
                        fV.n += 1
                else:
                    for c in range(4, DC):
                        sci = (b * NU + u) * 4 + (c - 4)
                        v.tensor_copy(out=sh2[c][:, 0:1],
                                      in_=shcolb[:, sci:sci + 1]).then_inc(fV.sem, 1)
                        fV.n += 1
                rec(up + "shz", fV)
                wge(v, fP, up + "lnsums")
                v.tensor_scalar(out=murow, in0=pool[0][0:1, :], scalar1=1.0 / D,
                                scalar2=None, op0=AluOpType.mult).then_inc(fV.sem, 1)
                fV.n += 1
                rec(up + "mu", fV)
                wge(v, fS, up + "m2")
                v.scalar_tensor_tensor(out=varrow, in0=pool[1][0:1, :],
                                       scalar=1.0 / D, in1=m2row,
                                       op0=AluOpType.mult,
                                       op1=AluOpType.subtract).then_inc(fV.sem, 1)
                fV.n += 1
                rec(up + "var", fV)
                wge(v, fP, up + "lnbcast")
                for c in range(DC):
                    v.tensor_tensor(out=lntmp_ap, in0=shf[c][:], in1=pool[2][:],
                                    op=AluOpType.subtract).then_inc(fV.sem, 1)
                    fV.n += 1
                    v.wait_ge(fV.sem, fV.n)
                    v.tensor_tensor(out=lntmp2_ap, in0=lntmp_ap, in1=pool[3][:],
                                    op=AluOpType.mult).then_inc(fV.sem, 1)
                    fV.n += 1
                    v.wait_ge(fV.sem, fV.n)
                    v.tensor_scalar(out=h2[c][:], in0=lntmp2_ap,
                                    scalar1=lngc[:, c:c + 1], scalar2=lnbc[:, c:c + 1],
                                    op0=AluOpType.mult, op1=AluOpType.add
                                    ).then_inc(fV.sem, 1)
                    fV.n += 1
                    rec(up + f"h2_{c}", fV)
                rec(up + "h2_7", fV)
                for ja in range(22):
                    aw = 128 if ja < 21 else HID - 21 * 128
                    wge(v, fS, up + f"gel{ja}")
                    v.scalar_tensor_tensor(out=hidst[ja][0:aw, :], in0=w1banka(ja, aw),
                                           scalar=b1ac[0:aw, ja:ja + 1],
                                           in1=gel_sb[ja % 2][0:aw, :],
                                           op0=AluOpType.add, op1=AluOpType.mult
                                           ).then_inc(fV.sem, 1)
                    fV.n += 1
                    rec(up + f"hid{ja}", fV)
                def w2bank_v(dc):
                    if dc < 6:
                        return pool[dc][:]
                    return pf_ss[:, 0:512] if dc == 6 else pf_ss[:, 512:1024]
                if True:
                    for dc in range(DC):
                        wge(v, fP, up + f"w2f{dc}")
                        if dc >= 2:
                            wge(v, fWs[dc % 2], up + f"bsp{dc-2}")
                        elif pup is not None:
                            wge(v, fWs[dc % 2], pup + f"bsp{dc+6}")
                        v.scalar_tensor_tensor(out=strt[dc % 2][:], in0=w2bank_v(dc),
                                               scalar=b2c[:, dc:dc + 1],
                                               in1=res2[dc][:], op0=AluOpType.add,
                                               op1=AluOpType.add).then_inc(fV.sem, 1)
                        fV.n += 1
                        rec(up + f"out{dc}", fV)
                rec(up + "out_done", fV)
            rec(up + "v_end", fV)

        def emit_layer(l, strm_in, strm_out, with_B):
            lp = f"F{l}_"

            @block.gpsimd
            def _(g):
                g.wait_ge(bar.sem, BAR0)
                if l > 0:
                    wge(g, fP, f"F{l-1}_A_pe_end")
                    wge(g, fV, f"F{l-1}_tabs_done")
                    wge(g, fS, f"F{l-1}_B_s_end")
                    wge(g, fV, f"F{l-1}_B_v_end")
                for c in range(DC):
                    g.dma_start(out=wq16[c][:],
                                in_=wqkv_in[l, c * 128:(c + 1) * 128, :]).then_inc(fG.sem, 16)
                    fG.n += 16
                for h in range(NHEAD):
                    g.dma_start(out=wo16h[h][:],
                                in_=wo_in[l, h * 64:(h + 1) * 64, :]).then_inc(fG.sem, 16)
                    fG.n += 16
                for dst, src_ in [(qscol[:], qs_t[l]), (kscol[:], ks_t[l]),
                                  (lngc[:], lng_t[l]), (lnbc[:], lnb_t[l]),
                                  (b1ac[:], b1a_in[l]), (b1gc[:], b1g_in[l]),
                                  (b2c[:], b2_t[l]), (csqf[:], csq_in),
                                  (cskf[:], csk_in), (rperm16[:], rperm_in),
                                  (rpermq16[:], rpermq_in)]:
                    g.dma_start(out=dst, in_=src_).then_inc(fG.sem, 16)
                    fG.n += 16
            rec(lp + "wl", fG)

            @block.vector
            def _(v):
                wge(v, fG, lp + "wl")
                v.wait_ge(fGm.sem, FGM_ALL)
                if l > 0:
                    wge(v, fP, f"F{l-1}_A_pe_end")
                v.tensor_scalar(out=csqf[:], in0=csqf[:], scalar1=qscol[:, 0:1],
                                scalar2=None, op0=AluOpType.mult).then_inc(fV.sem, 1)
                fV.n += 1
                v.tensor_scalar(out=cskf[:], in0=cskf[:], scalar1=kscol[:, 0:1],
                                scalar2=None, op0=AluOpType.mult).then_inc(fV.sem, 1)
                fV.n += 1
                v.tensor_scalar(out=rpermq16[:], in0=rpermq16[:], scalar1=qscol[:, 0:1],
                                scalar2=None, op0=AluOpType.mult).then_inc(fV.sem, 1)
                fV.n += 1
                v.tensor_scalar(out=rperm16[:], in0=rperm16[:], scalar1=kscol[:, 0:1],
                                scalar2=None, op0=AluOpType.mult).then_inc(fV.sem, 1)
                fV.n += 1
            rec(lp + "tabs_done", fV)

            for b in range(NB):
                for w in range(NW):
                    _emit_attn_window(l, b, w, strm_in, strm_out)
            rec(lp + "A_pe_end", fP)
            rec(lp + "A_done_W0", fWs[0])
            rec(lp + "A_done_W1", fWs[1])

            if with_B:
                for b in range(NB):
                    for u in range(T // 512):
                        _emit_ffnmid_unit(l, b, u, strm_out)
            rec(lp + "B_done_W0", fWs[0])
            rec(lp + "B_done_W1", fWs[1])
            rec(lp + "B_v_end", fV)
            rec(lp + "B_pe_end", fP)
            rec(lp + "B_s_end", fS)

        emit_layer(0, xn16, strmA, with_B=(stage >= 4))
        if stage >= 5:
            emit_layer(1, strmA, strmB, with_B=True)

        if stage >= 6:
            fOut = Ch(nc, "f_out")
            ostage = w1sl[0][:, 0:1024]
            p2bf3 = pool[2][:].bitcast(BF16)
            NU = T // 512
            for b in range(NB):
                for u in range(NU):
                    t0 = u * 512
                    cp = f"P3_{b}_{u}_"
                    pidx = b * NU + u
                    ppp = None
                    if pidx > 0:
                        qb, qu = divmod(pidx - 1, NU)
                        ppp = f"P3_{qb}_{qu}_"

                    @block.sync
                    def _(sync):
                        if ppp is None:
                            wge(sync, fWs[0], "F1_B_done_W0")
                            wge(sync, fWs[1], "F1_B_done_W1")
                            wge(sync, fV, "F1_B_v_end")
                            wge(sync, fP, "F1_B_pe_end")
                            if stage >= 2:
                                sync.wait_ge(cWs[0].sem, MAMBA_W0)
                                sync.wait_ge(cWs[1].sem, MAMBA_W1)
                        else:
                            wge(sync, fV, ppp + "sg_done")
                        for dc in range(DC):
                            fl = fL[dc % 2]
                            sti = sq_sb[0] if dc % 2 == 0 else qn_sb[0]
                            fti = t1_sb[0] if dc % 2 == 0 else rb_sb[0]
                            if dc >= 2:
                                wge(sync, fV, cp + f"sg{dc-2}")
                            sync.dma_start(out=sti[:], in_=s16[b, dc, :, t0:t0 + 512]
                                           ).then_inc(fl.sem, 16)
                            fl.n += 16
                            sync.dma_start(out=fti[:], in_=strmB[b, dc, :, t0:t0 + 512]
                                           ).then_inc(fl.sem, 16)
                            fl.n += 16
                            rec(cp + f"ld{dc}", fl)
                        for ts in range(4):
                            wge(sync, fV, cp + f"ob{ts}")
                            sync.dma_start(
                                out=out[b, t0 + ts * 128:t0 + (ts + 1) * 128, :],
                                in_=obuf).then_inc(fOut.sem, 16)
                            fOut.n += 16
                            rec(cp + f"st{ts}", fOut)

                    @block.gpsimd
                    def _(g):
                        for ts in range(4):
                            if ppp is not None and ts == 0:
                                wge(g, fV, ppp + "ob3")
                            elif ts > 0:
                                wge(g, fV, cp + f"ob{ts-1}")
                            g.dma_start(out=xrl,
                                        in_=x_in[b, t0 + ts * 128:t0 + (ts + 1) * 128, :]
                                        ).then_inc(fG.sem, 16)
                            fG.n += 16
                            rec(cp + f"x{ts}", fG)

                    @block.scalar
                    def _(s):
                        for dc in range(DC):
                            fti = t1_sb[0] if dc % 2 == 0 else rb_sb[0]
                            wge(s, fL[dc % 2], cp + f"ld{dc}")
                            if dc >= 2:
                                wge(s, fV, cp + f"sg{dc-2}")
                            s.activation(out=gel_sb[dc % 2][:], in_=fti[:],
                                         func=AF.Silu).then_inc(fS.sem, 1)
                            fS.n += 1
                            rec(cp + f"sil{dc}", fS)
                        for ts in range(4):
                            wge(s, fP, cp + f"tr{ts}")
                            if ts >= 1:
                                wge(s, fV, cp + f"ob{ts-1}")
                            elif ppp is not None:
                                wge(s, fV, ppp + "ob3")
                            s.copy(out=ostage, in_=p2bf3[:, 0:1024]).then_inc(fS.sem, 1)
                            fS.n += 1
                            rec(cp + f"ost{ts}", fS)

                    @block.vector
                    def _(v):
                        for dc in range(DC):
                            sti = sq_sb[0] if dc % 2 == 0 else qn_sb[0]
                            wge(v, fS, cp + f"sil{dc}")
                            if dc == 0:
                                if ppp is not None:
                                    wge(v, fP, ppp + "tr3")
                                elif stage >= 5:
                                    wge(v, fP, "F1_B_pe_end")
                            v.tensor_tensor(out=h2[dc][:], in0=sti[:],
                                            in1=gel_sb[dc % 2][:], op=AluOpType.mult
                                            ).then_inc(fV.sem, 1)
                            fV.n += 1
                            rec(cp + f"sg{dc}", fV)
                        for ts in range(4):
                            wge(v, fG, cp + f"x{ts}")
                            wge(v, fS, cp + f"ost{ts}")
                            if ts >= 1 or ppp is not None:
                                wge(v, fOut, cp + f"st{ts-1}" if ts >= 1
                                    else ppp + "st3")
                            v.tensor_tensor(out=obuf, in0=xrl, in1=ostage,
                                            op=AluOpType.add).then_inc(fV.sem, 1)
                            fV.n += 1
                            rec(cp + f"ob{ts}", fV)
                            if ts == 3:
                                rec(cp + "ob3", fV)
                        rec(cp + "sg_done", fV)

                    @block.tensor
                    def _(te):
                        for ts in range(4):
                            for dc in range(DC):
                                wge(te, fV, cp + f"sg{dc}")
                                if ts > 0 and dc == 0:
                                    wge(te, fS, cp + f"ost{ts-1}")
                                elif ts == 0 and dc == 0 and ppp is not None:
                                    wge(te, fS, ppp + "ost3")
                                te.transpose(out=p2bf3[:, dc * 128:(dc + 1) * 128],
                                             in_=h2[dc][:, ts * 128:(ts + 1) * 128],
                                             identity=ident16[:]).then_inc(fP.sem, 1)
                                fP.n += 1
                            rec(cp + f"tr{ts}", fP)
                        rec(cp + "tr3", fP)

    @block.sync
    def _(sync):
        for c in [ch_hwl, ch_hwl2, ch_swl, ch_sp]:
            if c.n:
                sync.wait_ge(c.sem, c.n)
        if stage >= 2:
            for c in [cL, cG, cWs[0], cWs[1]]:
                if c.n:
                    sync.wait_ge(c.sem, c.n)
        if stage >= 3:
            for c in [fL[0], fL[1], fG, fGw1[0], fGw1[1], fGw2[0], fGw2[1],
                      fWs[0], fWs[1]]:
                if c.n:
                    sync.wait_ge(c.sem, c.n)
        if stage >= 6:
            sync.wait_ge(fOut.sem, fOut.n)

    blk.__exit__(None, None, None)
    if stage >= 3:
        for t_ in reversed(fctxs):
            t_.__exit__(None, None, None)
    else:
        if stage >= 2:
            for t_ in reversed(mamba_ctxs):
                t_.__exit__(None, None, None)
        else:
            for t_ in ps_tt_ctx:
                t_.__exit__(None, None, None)
        for t_ in reversed(p0_ctxs):
            t_.__exit__(None, None, None)
    for c in reversed(ctxs):
        c.__exit__(None, None, None)
    return nc, EV


def build2(T=2048, stage=10, ths_override=None, race=False, dbg=False, mstop=99):
    _, ev = build(T, stage, ths_override, evmap=None, dbg=dbg, mstop=mstop)
    nc, _ = build(T, stage, ths_override, evmap=ev, race=race, dbg=dbg, mstop=mstop)
    return nc



def _hd2_const():
    m = np.zeros((128, 2), np.float32)
    m[0:64, 0] = 1.0; m[64:128, 1] = 1.0
    return m


def _hd2T_const():
    m = np.zeros((2, 128), np.float32)
    m[0, 0:64] = 1.0; m[1, 64:128] = 1.0
    return m


def _rperm_const():
    m = np.zeros((128, 128), np.float32)
    for blkd in range(2):
        o = blkd * 64
        for i in range(32):
            m[o + i + 32, o + i] = -1.0
            m[o + i, o + i + 32] = 1.0
    return m


def _trimadd_const():
    m = np.zeros((128, 256), np.float32)
    for r in range(128):
        m[r, 0:r] = -1e30
        m[r, 128 + r + 1:256] = -1e30
    return m


def _rot_tables(T):
    half = np.arange(0, DH, 2, dtype=np.float32)
    inv_freq = 1.0 / 10000.0 ** (half / DH)
    t = np.arange(T, dtype=np.float32)
    fr = t[:, None] * inv_freq[None, :]
    fr = np.concatenate([fr, fr], axis=-1)
    base = (half + 0.4 * DH) / (1.4 * DH)
    power = (t - T // 2) / float(WIN // 2)
    sc = base[None, :] ** power[:, None]
    sc = np.concatenate([sc, sc], axis=-1)
    cos, sin = np.cos(fr), np.sin(fr)
    csq = (cos * sc) * 8.0; ssq = (sin * sc)
    csk = (cos / sc); ssk = (sin / sc)
    def lay(m):
        r = np.transpose(m)
        return np.concatenate([r, r], axis=0).astype(np.float32)
    return lay(csq), lay(ssq), lay(csk), lay(ssk)


def _selc_const():
    m = np.zeros((32, 32 * 128), np.float32)
    for s in range(32):
        m[s, s * 128:(s + 1) * 128] = 1.0
    return m


def _prep_inputs(inputs):
    T = np.asarray(inputs["L0_attn_x"]).shape[1]
    f = lambda a: np.ascontiguousarray(a, dtype=np.float32)
    x = np.asarray(inputs["L0_attn_x"])
    W_dt = np.asarray(inputs["L0_attn_W_dt"])
    b_dt = np.asarray(inputs["L0_attn_b_dt"])
    A_log = np.asarray(inputs["L0_attn_A_log"])
    Dv = np.asarray(inputs["L0_attn_D"])

    W_dtb = np.concatenate([W_dt, b_dt[None, :]], axis=0)
    A_log_t = np.transpose(A_log.reshape(DC, 128, DSTATE), (1, 0, 2)).reshape(128, DC * DSTATE)
    D_t = np.transpose(Dv.reshape(DC, 128), (1, 0))

    qs = np.asarray(inputs["L0_ffn_q_scale"])
    ks = np.asarray(inputs["L0_ffn_k_scale"])
    qs_t = np.tile(qs, (1, 2)).reshape(2, 128, 1)
    ks_t = np.tile(ks, (1, 2)).reshape(2, 128, 1)
    lng = np.asarray(inputs["L0_ffn_ln_g"])
    lnb = np.asarray(inputs["L0_ffn_ln_b"])
    lng_t = np.transpose(lng.reshape(2, DC, 128), (0, 2, 1))
    lnb_t = np.transpose(lnb.reshape(2, DC, 128), (0, 2, 1))
    b1 = np.asarray(inputs["L0_ffn_b1"])
    b1a = b1[:, :HID]
    b1g = b1[:, HID:]
    pad = np.zeros((2, 22 * 128), np.float32)
    b1a_p = pad.copy(); b1a_p[:, :HID] = b1a
    b1g_p = pad.copy(); b1g_p[:, :HID] = b1g
    b1a_t = np.transpose(b1a_p.reshape(2, 22, 128), (0, 2, 1))
    b1g_t = np.transpose(b1g_p.reshape(2, 22, 128), (0, 2, 1))
    b2 = np.asarray(inputs["L0_ffn_b2"])
    b2_t = np.transpose(b2.reshape(2, DC, 128), (0, 2, 1))

    _RT = _rot_tables(T)
    common = {
        "W_x": f(inputs["L0_attn_W_x"]),
        "W_dtb": f(W_dtb),
        "A_log_t": f(A_log_t),
        "D_t": f(D_t),
        "wqkv": f(inputs["L0_ffn_wqkv"]),
        "qs_t": f(qs_t), "ks_t": f(ks_t),
        "wo": f(inputs["L0_ffn_wo"]),
        "lng_t": f(lng_t), "lnb_t": f(lnb_t),
        "w1": f(inputs["L0_ffn_w1"]), "b1a_t": f(b1a_t), "b1g_t": f(b1g_t),
        "w2": f(inputs["L0_ffn_w2"]), "b2_t": f(b2_t),
        "ident": np.eye(128, dtype=np.float32),
        "cvec": np.tile(np.array([LN_EPS_D2, LN_LOGD, 1e-24, 1e-5, 0, 0, 0, 0],
                                 np.float32), (128, 1)),
        "selc": _selc_const(),
        "hd2": _hd2_const(), "hd2T": _hd2T_const(),
        "onesf": np.ones((128, 128), np.float32),
        "rperm": _rperm_const(), "rpermq": _rperm_const() * 8.0,
        "trimadd": _trimadd_const(),
        "trilo": np.tril(np.ones((128, 128), np.float32)),
        "triup": np.triu(np.ones((128, 128), np.float32)),
        "csq": _RT[0], "ssq": _RT[1], "csk": _RT[2], "ssk": _RT[3],
    }
    in_maps = []
    for c in range(NCORES):
        m = dict(common)
        m["x"] = f(x[c * NB:(c + 1) * NB])
        in_maps.append(m)
    return in_maps


def kernel(**inputs):
    T = np.asarray(inputs["L0_attn_x"]).shape[1]
    in_maps = _prep_inputs(inputs)
    nc = build2(T=T, stage=6)
    res = run_bass_kernel_spmd(nc, in_maps, core_ids=list(range(NCORES)))
    outs = [np.asarray(r["out"], np.float32) for r in res.results]
    return np.concatenate(outs, axis=0)


# version 05
# speedup vs baseline: 1.9717x; 1.0768x over previous
import numpy as np
import ml_dtypes
import concourse.bass as bass
import concourse.mybir as mybir
from concourse.bass_utils import run_bass_kernel_spmd
from concourse.alu_op_type import AluOpType

F32 = mybir.dt.float32
BF16 = mybir.dt.bfloat16
AF = mybir.ActivationFunctionType
AX = mybir.AxisListType

NCORES = 8
D = 1024
DC = 8
NB = 2
DSTATE = 16
WIN = 512
NHEAD = 8
DH = 64
INNER = 512
HID = 2730
H1W = 5460
LN_EPS_D2 = 1e-5 * D * D
LN_LOGD = float(np.log(D))


class Ch:
    def __init__(self, nc, name):
        self._cm = nc.semaphore(name)
        self.sem = self._cm.__enter__()
        self.n = 0


def build(T=2048, stage=10, ths_override=None, evmap=None, race=False, dbg=False, mstop=99):
    recording = evmap is None
    EV = {} if recording else evmap

    def rec(name, ch):
        if recording:
            EV[name] = ch.n

    def wge(eng, ch, name):
        eng.wait_ge(ch.sem, 0 if recording else EV[name])

    nc = bass.Bass("TRN2", target_bir_lowering=False)
    nc.detect_race_conditions = race and not recording
    NTT = T // 128

    x_in = nc.dram_tensor("x", [NB, T, D], F32, kind="ExternalInput").ap()
    out = nc.dram_tensor("out", [NB, T, D], F32, kind="ExternalOutput").ap()

    w_x = nc.dram_tensor("W_x", [D, 96], F32, kind="ExternalInput").ap()
    w_dtb = nc.dram_tensor("W_dtb", [65, D], F32, kind="ExternalInput").ap()
    a_log_t = nc.dram_tensor("A_log_t", [128, DC * DSTATE], F32, kind="ExternalInput").ap()
    d_t = nc.dram_tensor("D_t", [128, DC], F32, kind="ExternalInput").ap()

    wqkv_in = nc.dram_tensor("wqkv", [2, D, 3 * INNER], F32, kind="ExternalInput").ap()
    qs_t = nc.dram_tensor("qs_t", [2, 128, 1], F32, kind="ExternalInput").ap()
    ks_t = nc.dram_tensor("ks_t", [2, 128, 1], F32, kind="ExternalInput").ap()
    wo_in = nc.dram_tensor("wo", [2, INNER, D], F32, kind="ExternalInput").ap()
    lng_t = nc.dram_tensor("lng_t", [2, 128, DC], F32, kind="ExternalInput").ap()
    lnb_t = nc.dram_tensor("lnb_t", [2, 128, DC], F32, kind="ExternalInput").ap()
    w1_in = nc.dram_tensor("w1", [2, D, H1W], F32, kind="ExternalInput").ap()
    b1a_in = nc.dram_tensor("b1a_t", [2, 128, 22], F32, kind="ExternalInput").ap()
    b1g_in = nc.dram_tensor("b1g_t", [2, 128, 22], F32, kind="ExternalInput").ap()
    w2_in = nc.dram_tensor("w2", [2, HID, D], F32, kind="ExternalInput").ap()
    b2_t = nc.dram_tensor("b2_t", [2, 128, DC], F32, kind="ExternalInput").ap()

    ident = nc.dram_tensor("ident", [128, 128], F32, kind="ExternalInput").ap()
    cvec = nc.dram_tensor("cvec", [128, 8], F32, kind="ExternalInput").ap()
    selc_in = nc.dram_tensor("selc", [32, 32 * 128], F32, kind="ExternalInput").ap()
    hd2_in = nc.dram_tensor("hd2", [128, 2], F32, kind="ExternalInput").ap()
    hd2T_in = nc.dram_tensor("hd2T", [2, 128], F32, kind="ExternalInput").ap()
    onesf_in = nc.dram_tensor("onesf", [128, 128], F32, kind="ExternalInput").ap()
    rperm_in = nc.dram_tensor("rperm", [128, 128], F32, kind="ExternalInput").ap()
    rpermq_in = nc.dram_tensor("rpermq", [128, 128], F32, kind="ExternalInput").ap()
    trimadd_in = nc.dram_tensor("trimadd", [128, 256], F32, kind="ExternalInput").ap()
    trilo_in = nc.dram_tensor("trilo", [128, 128], F32, kind="ExternalInput").ap()
    triup_in = nc.dram_tensor("triup", [128, 128], F32, kind="ExternalInput").ap()
    csq_in = nc.dram_tensor("csq", [128, T], F32, kind="ExternalInput").ap()
    ssq_in = nc.dram_tensor("ssq", [128, T], F32, kind="ExternalInput").ap()
    csk_in = nc.dram_tensor("csk", [128, T], F32, kind="ExternalInput").ap()
    ssk_in = nc.dram_tensor("ssk", [128, T], F32, kind="ExternalInput").ap()

    _ik = "ExternalOutput" if dbg else "Internal"
    xn16 = nc.dram_tensor("xn16", [NB, DC, 128, T], BF16, kind=_ik).ap()
    s16 = nc.dram_tensor("s16", [NB, DC, 128, T], BF16, kind=_ik).ap()
    strmA = nc.dram_tensor("strmA", [NB, DC, 128, T], BF16, kind=_ik).ap()
    strmB = nc.dram_tensor("strmB", [NB, DC, 128, T], BF16, kind=_ik).ap()
    NW = T // WIN
    mrow_dram = nc.dram_tensor("mrow_dram", [NB, NW, 4096], BF16, kind="Internal").ap()
    w1b = nc.dram_tensor("w1b", [2, 22, 128, 2048], BF16, kind="Internal").ap()
    w2b = nc.dram_tensor("w2b", [2, 22, 128, 1024], BF16, kind="Internal").ap()

    ctxs = []
    def sbuf(name, shape, dt):
        t = nc.sbuf_tensor(name, shape, dt)
        h = t.__enter__(); ctxs.append(t); return h

    ident16 = sbuf("ident16", [128, 128], BF16)
    cv = sbuf("cv", [128, 8], F32)

    p0_ctxs = []
    def sbuf_p0(name, shape, dt):
        t = nc.sbuf_tensor(name, shape, dt)
        h = t.__enter__(); p0_ctxs.append(t); return h
    xb = [sbuf_p0(f"xb{i}", [128, D], F32) for i in range(2)]
    sqb = [sbuf_p0(f"sqb{i}", [128, D], F32) for i in range(2)]
    stat = sbuf_p0("stat", [128, 16], F32)
    xnb = [sbuf_p0(f"xnb{i}", [128, D], BF16) for i in range(2)]
    xt_stage = [sbuf_p0(f"xt_stage{i}", [128, 256], BF16) for i in range(DC)]
    NPS_T = 16
    ps_tt_ctx = [nc.psum_tensor(f"ps_tt{i}", [128, 1024], BF16) for i in range(2)]
    ps_tt = [t.__enter__() for t in ps_tt_ctx]
    ps_tt_ctx = list(reversed(ps_tt_ctx))
    def ps_t_slot(k):
        sl = k % NPS_T
        return ps_tt[sl // 8][:, (sl % 8) * 128:(sl % 8) * 128 + 128]

    ch_hwl = Ch(nc, "hwl")
    ch_hwl2 = Ch(nc, "hwl2")
    ch_swl = Ch(nc, "swl")
    ch_sq = Ch(nc, "c_sq")
    ch_st = Ch(nc, "c_st")
    ch_pe = Ch(nc, "c_pe")
    ch_ev = Ch(nc, "c_ev")
    ch_sp = Ch(nc, "c_sp")

    blk = nc.Block()
    block = blk.__enter__()

    tiles = [(b, tt) for b in range(NB) for tt in range(NTT)]
    NTILE = len(tiles)
    VP, SP = 4, 4

    @block.sync
    def _(sync):
        for i, (b, tt) in enumerate(tiles):
            if i >= 2:
                sync.wait_ge(ch_st.sem, (i - 1) * VP)
            _c = ch_hwl if i % 2 == 0 else ch_hwl2
            sync.dma_start(out=xb[i % 2][:],
                           in_=x_in[b, tt * 128:(tt + 1) * 128, :]).then_inc(_c.sem, 16)
            _c.n += 16

    @block.gpsimd
    def _(g):
        g.dma_start(out=ident16[:], in_=ident).then_inc(ch_swl.sem, 16)
        ch_swl.n += 16
        g.dma_start(out=cv[:], in_=cvec).then_inc(ch_swl.sem, 16)
        ch_swl.n += 16

    @block.scalar
    def _(s):
        for i, (b, tt) in enumerate(tiles):
            _c = ch_hwl if i % 2 == 0 else ch_hwl2
            s.wait_ge(_c.sem, 16 * (i // 2 + 1))
            if i == 0:
                s.wait_ge(ch_swl.sem, 32)
            if i >= 2:
                s.wait_ge(ch_st.sem, (i - 1) * VP)
            s.activation(out=sqb[i % 2][:], in_=xb[i % 2][:],
                         func=AF.Square).then_inc(ch_sq.sem, 1)
            s.wait_ge(ch_st.sem, i * VP + 1)
            s.activation(out=stat[:, 2:3], in_=stat[:, 0:1],
                         func=AF.Identity, scale=1.0 / D)
            s.activation(out=stat[:, 6:7], in_=stat[:, 0:1],
                         func=AF.Square).then_inc(ch_sq.sem, 1)
            s.wait_ge(ch_st.sem, i * VP + 3)
            s.activation(out=stat[:, 8:9], in_=stat[:, 3:4], func=AF.Ln,
                         bias=cv[:, 0:1]).then_inc(ch_sq.sem, 1)
            s.wait_ge(ch_sq.sem, i * SP + 3)
            s.activation(out=stat[:, 4:5], in_=stat[:, 8:9], func=AF.Exp,
                         scale=-0.5, bias=cv[:, 1:2]).then_inc(ch_sq.sem, 1)
            ch_sq.n = (i + 1) * SP

            def _evac_tile(j):
                bj, ttj = tiles[j]
                s.wait_ge(ch_pe.sem, (j + 1) * DC)
                if j >= 2:
                    s.wait_ge(ch_sp.sem, 128 * (j // 2))
                for dc in range(DC):
                    k = j * DC + dc
                    s.copy(out=xt_stage[dc][:, (ttj % 2) * 128:(ttj % 2) * 128 + 128],
                           in_=ps_t_slot(k)).then_inc(ch_ev.sem, 1)
                    ch_ev.n += 1
                if ttj % 2 == 1:
                    s.wait_ge(ch_ev.sem, ch_ev.n)
                    for dc in range(DC):
                        s.dma_start(out=xn16[bj, dc, :, (ttj - 1) * 128:(ttj + 1) * 128],
                                    in_=xt_stage[dc][:]).then_inc(ch_sp.sem, 16)
                        ch_sp.n += 16
            if i >= 1:
                _evac_tile(i - 1)
            if i == NTILE - 1:
                _evac_tile(i)

    @block.vector
    def _(v):
        for i, (b, tt) in enumerate(tiles):
            _c = ch_hwl if i % 2 == 0 else ch_hwl2
            v.wait_ge(_c.sem, 16 * (i // 2 + 1))
            if i >= 1:
                v.wait_ge(ch_sq.sem, i * SP)
            v.tensor_reduce(out=stat[:, 0:1], in_=xb[i % 2][:], axis=AX.X,
                            op=AluOpType.add).then_inc(ch_st.sem, 1)
            v.wait_ge(ch_sq.sem, i * SP + 1)
            v.tensor_reduce(out=stat[:, 1:2], in_=sqb[i % 2][:], axis=AX.X,
                            op=AluOpType.add).then_inc(ch_st.sem, 1)
            v.wait_ge(ch_st.sem, i * VP + 2)
            v.wait_ge(ch_sq.sem, i * SP + 2)
            v.scalar_tensor_tensor(out=stat[:, 3:4], in0=stat[:, 1:2],
                                   scalar=float(D), in1=stat[:, 6:7],
                                   op0=AluOpType.mult,
                                   op1=AluOpType.subtract).then_inc(ch_st.sem, 1)
            v.wait_ge(ch_sq.sem, i * SP + 4)
            if i >= 2:
                v.wait_ge(ch_pe.sem, (i - 1) * DC)
            v.tensor_scalar(out=xnb[i % 2][:], in0=xb[i % 2][:],
                            scalar1=stat[:, 2:3], scalar2=stat[:, 4:5],
                            op0=AluOpType.subtract,
                            op1=AluOpType.mult).then_inc(ch_st.sem, 1)
            ch_st.n = (i + 1) * VP

    @block.tensor
    def _(te):
        te.wait_ge(ch_swl.sem, 32)
        for i, (b, tt) in enumerate(tiles):
            te.wait_ge(ch_st.sem, i * VP + 4)
            for dc in range(DC):
                k = i * DC + dc
                if k >= NPS_T:
                    te.wait_ge(ch_ev.sem, k - (NPS_T - 1))
                te.transpose(out=ps_t_slot(k),
                             in_=xnb[i % 2][:, dc * 128:(dc + 1) * 128],
                             identity=ident16[:]).then_inc(ch_pe.sem, 1)
                ch_pe.n += 1

    P0_SP_ALL = NTILE // 2 * DC * 16

    if stage >= 2:
        THS = ths_override or min(1024, T)
        NTH = T // THS
        NTC = THS // 512

        for t_ in ps_tt_ctx:
            t_.__exit__(None, None, None)
        ps_tt_freed = True

        mamba_ctxs = []
        def sbuf_m(name, shape, dt):
            t = nc.sbuf_tensor(name, shape, dt)
            h = t.__enter__(); mamba_ctxs.append(t); return h
        def psum_m(name, shape, dt):
            t = nc.psum_tensor(name, shape, dt)
            h = t.__enter__(); mamba_ctxs.append(t); return h

        wx16 = sbuf_m("wx16", [128, DC * 96], BF16)
        wdtb16 = sbuf_m("wdtb16", [65, D], BF16)
        selc16 = sbuf_m("selc16", [32, 32 * 128], BF16)
        alog_sb = sbuf_m("alog_sb", [128, 128], F32)
        a_sb = sbuf_m("a_sb", [128, 128], F32)
        dvec = sbuf_m("dvec", [128, DC], F32)
        xpa = sbuf_m("xpa", [65, THS], BF16)
        bc16 = sbuf_m("bc16", [32, THS], BF16)
        xnT = [sbuf_m(f"m_xnT{c}", [128, THS], BF16) for c in range(DC)]
        dtT = [sbuf_m(f"m_dtT{c}", [128, THS], F32) for c in range(DC)]
        dtx = [sbuf_m(f"m_dtx{c}", [128, THS], BF16) for c in range(DC)]
        ybuf = [sbuf_m(f"m_y{c}", [128, THS], F32) for c in range(DC)]
        sp_t = [sbuf_m(f"m_spt{i}", [128, 512], F32) for i in range(6)]
        abar = [sbuf_m(f"m_ab{i}", [128, THS], F32) for i in range(4)]
        ubuf = [sbuf_m(f"m_u{i}", [128, THS], BF16) for i in range(2)]
        hbuf = [sbuf_m(f"m_h{i}", [128, THS], BF16) for i in range(2)]
        tmpb = [sbuf_m(f"m_tmp{i}", [128, THS], BF16) for i in range(2)]
        bb_sb = [sbuf_m(f"m_bb{i}", [128, THS], BF16) for i in range(2)]
        cb_sb = [sbuf_m(f"m_cb{i}", [128, THS], BF16) for i in range(2)]
        hcar = sbuf_m("m_hcar", [128, 128], F32)
        soutb = [sbuf_m(f"m_sout{i}", [128, THS], BF16) for i in range(2)]

        pm_dt = psum_m("pm_dt", [64, 512], F32)
        pm_bc = psum_m("pm_bc", [32, 512], F32)
        pm_dtT = [psum_m(f"pm_dtT{i}", [128, 512], F32) for i in range(2)]
        pm_bb = psum_m("pm_bb", [128, THS], F32)
        pm_cb = psum_m("pm_cb", [128, THS], F32)

        cL = Ch(nc, "m_L"); cG = Ch(nc, "m_G"); cP = Ch(nc, "m_P")
        cS = Ch(nc, "m_S"); cV = Ch(nc, "m_V")
        cWs = [Ch(nc, "m_W0"), Ch(nc, "m_W1")]
        cGm = Ch(nc, "m_Gm")

        @block.gpsimd
        def _(g):
            for c in range(DC):
                g.dma_start(out=wx16[:, c * 96:(c + 1) * 96],
                            in_=w_x[c * 128:(c + 1) * 128, :]).then_inc(cG.sem, 16)
                cG.n += 16
            for ap_dst, ap_src in [(wdtb16[:], w_dtb), (selc16[:], selc_in),
                                   (alog_sb[:], a_log_t), (dvec[:], d_t)]:
                g.dma_start(out=ap_dst, in_=ap_src).then_inc(cG.sem, 16)
                cG.n += 16
            g.memset(xpa[64:65, :], 1.0).then_inc(cGm.sem, 1)
            cGm.n += 1
        G_W = cG.n

        @block.scalar
        def _(s):
            s.wait_ge(cG.sem, G_W)
            s.activation(out=a_sb[:], in_=alog_sb[:], func=AF.Exp).then_inc(cS.sem, 1)
        cS.n += 1
        S_EXPA = cS.n

        @block.vector
        def _(v):
            v.wait_ge(cS.sem, S_EXPA)
            v.tensor_scalar(out=a_sb[:], in0=a_sb[:], scalar1=-1.0, scalar2=None,
                            op0=AluOpType.mult).then_inc(cV.sem, 1)
        cV.n += 1
        V_NEGA = cV.n

        blocks2 = [(b, th) for b in range(NB) for th in range(NTH)]
        NKK = NTC * DC
        for m, (b, th) in enumerate(blocks2):
            t0 = th * THS
            mm_ = f"M{m}_"
            pm_ = f"M{m-1}_" if m > 0 else None

            @block.sync
            def _(sync):
                if m == 0:
                    sync.wait_ge(ch_sp.sem, P0_SP_ALL)
                else:
                    wge(sync, cV, pm_ + "vend")
                    wge(sync, cP, pm_ + "pend")
                for c in range(DC):
                    sync.dma_start(out=xnT[c][:], in_=xn16[b, c, :, t0:t0 + THS]
                                   ).then_inc(cL.sem, 16)
                    cL.n += 16
            rec(mm_ + "loads", cL)

            @block.tensor
            def _(te):
                wge(te, cL, mm_ + "loads")
                if m == 0:
                    te.wait_ge(cV.sem, V_NEGA)
                    te.wait_ge(cGm.sem, 1)
                    te.wait_ge(ch_ev.sem, NTILE * DC)
                for tc in range(NTC):
                    if m > 0 or tc > 0:
                        wge(te, cS, (mm_ + f"xw{tc-1}") if tc > 0 else (pm_ + f"xw{NTC-1}"))
                    for c in range(DC):
                        mm1 = te.matmul(out=pm_dt[:], lhsT=wx16[:, c * 96:c * 96 + 64],
                                        rhs=xnT[c][:, tc * 512:(tc + 1) * 512],
                                        start=(c == 0), stop=(c == DC - 1))
                    mm1.then_inc(cP.sem, 1)
                    cP.n += 1
                    for c in range(DC):
                        mm2 = te.matmul(out=pm_bc[:], lhsT=wx16[:, c * 96 + 64:c * 96 + 96],
                                        rhs=xnT[c][:, tc * 512:(tc + 1) * 512],
                                        start=(c == 0), stop=(c == DC - 1))
                    mm2.then_inc(cP.sem, 1)
                    cP.n += 1
                    rec(mm_ + f"xproj{tc}", cP)

            for tc in range(NTC if mstop >= 21 else 0):
                for c in range(DC):
                    @block.tensor
                    def _(te, tc=tc, c=c):
                        kk = tc * DC + c
                        pi = kk % 2
                        gk = m * NKK + kk
                        if c == 0:
                            wge(te, cS, mm_ + f"xpa{tc}")
                        if gk >= 2:
                            pgk = gk - 2
                            pmm = f"M{pgk // NKK}_"
                            ptc, pc = divmod(pgk % NKK, DC)
                            if mstop >= 23:
                                wge(te, cV, pmm + f"spw{ptc}_{pc}")
                            if mstop >= 22:
                                wge(te, cS, pmm + f"spl{ptc}_{pc}")
                        te.matmul(out=pm_dtT[pi][:], lhsT=wdtb16[:, c * 128:(c + 1) * 128],
                                  rhs=xpa[:, tc * 512:(tc + 1) * 512],
                                  start=True, stop=True).then_inc(cP.sem, 1)
                        cP.n += 1
                        rec(mm_ + f"dtT{tc}_{c}", cP)

            for si in range(DSTATE if mstop >= 30 else 0):
                @block.tensor
                def _(te, si=si):
                    gsi = m * DSTATE + si
                    wge(te, cV, mm_ + "dtxall")
                    if gsi >= 1:
                        pgs = gsi - 1
                        pmm = f"M{pgs // DSTATE}_"
                        psi = pgs % DSTATE
                        wge(te, cS, pmm + f"bbev{psi}")
                    for h2_ in range(NTC):
                        mmb = te.matmul(out=pm_bb[:, h2_ * 512:(h2_ + 1) * 512],
                                        lhsT=selc16[:, si * 128:(si + 1) * 128],
                                        rhs=bc16[:, h2_ * 512:(h2_ + 1) * 512],
                                        start=True, stop=True)
                    mmb.then_inc(cP.sem, 1)
                    cP.n += 1
                    for h2_ in range(NTC):
                        mmc = te.matmul(out=pm_cb[:, h2_ * 512:(h2_ + 1) * 512],
                                        lhsT=selc16[:, (16 + si) * 128:(17 + si) * 128],
                                        rhs=bc16[:, h2_ * 512:(h2_ + 1) * 512],
                                        start=True, stop=True)
                    mmc.then_inc(cP.sem, 1)
                    cP.n += 1
                    rec(mm_ + f"bb{si}", cP)
            rec(mm_ + "pend", cP)

            @block.scalar
            def _(s):
                for tc in range(NTC):
                    wge(s, cP, mm_ + f"xproj{tc}")
                    s.copy(out=xpa[0:64, tc * 512:(tc + 1) * 512],
                           in_=pm_dt[:]).then_inc(cS.sem, 1)
                    cS.n += 1
                    s.copy(out=bc16[:, tc * 512:(tc + 1) * 512],
                           in_=pm_bc[:]).then_inc(cS.sem, 1)
                    cS.n += 1
                    rec(mm_ + f"xw{tc}", cS)
                    rec(mm_ + f"xpa{tc}", cS)
                    for c in range(DC if mstop >= 22 else 0):
                        kk = tc * DC + c
                        pi = kk % 2
                        gk = m * NKK + kk
                        sb3 = 3 * (kk % 2)
                        wge(s, cP, mm_ + f"dtT{tc}_{c}")
                        if gk >= 2 and mstop >= 23:
                            pgk = gk - 2
                            pmm = f"M{pgk // NKK}_"
                            ptc, pc = divmod(pgk % NKK, DC)
                            wge(s, cV, pmm + f"spw{ptc}_{pc}")
                        s.activation(out=sp_t[sb3][:], in_=pm_dtT[pi][:],
                                     func=AF.Abs).then_inc(cS.sem, 1)
                        cS.n += 1
                        s.wait_ge(cS.sem, cS.n)
                        s.activation(out=sp_t[sb3 + 1][:], in_=sp_t[sb3][:],
                                     func=AF.Exp, scale=-1.0).then_inc(cS.sem, 1)
                        cS.n += 1
                        s.wait_ge(cS.sem, cS.n)
                        s.activation(out=sp_t[sb3 + 2][:], in_=sp_t[sb3 + 1][:],
                                     func=AF.Ln, bias=1.0).then_inc(cS.sem, 1)
                        cS.n += 1
                        rec(mm_ + f"spl{tc}_{c}", cS)
                for si in range(DSTATE if mstop >= 30 else 0):
                    pr = si % 2
                    gsi = m * DSTATE + si
                    wge(s, cP, mm_ + f"bb{si}")
                    if gsi >= 2 and mstop >= 40:
                        pgs = gsi - 2
                        pmm = f"M{pgs // DSTATE}_"
                        wge(s, cV, pmm + f"bbuse{pgs % DSTATE}")
                    s.copy(out=bb_sb[pr][:], in_=pm_bb[:]).then_inc(cS.sem, 1)
                    cS.n += 1
                    s.copy(out=cb_sb[pr][:], in_=pm_cb[:]).then_inc(cS.sem, 1)
                    cS.n += 1
                    rec(mm_ + f"bbev{si}", cS)
                    for c in range(DC):
                        gr = (m * DSTATE + si) * DC + c
                        rot4 = gr % 4
                        if gr >= 4 and mstop >= 40:
                            pgr = gr - 4
                            pmm = f"M{pgr // (DSTATE * DC)}_"
                            r_ = pgr % (DSTATE * DC)
                            wge(s, cV, pmm + f"abw{r_ // DC}_{r_ % DC}")
                        s.activation(out=abar[rot4][:], in_=dtT[c][:], func=AF.Exp,
                                     scale=a_sb[:, c * 16 + si:c * 16 + si + 1]
                                     ).then_inc(cS.sem, 1)
                        cS.n += 1
                        rec(mm_ + f"ab{si}_{c}", cS)
            rec(mm_ + "send", cS)

            @block.vector
            def _(v):
                for tc in range(NTC if mstop >= 23 else 0):
                    for c in range(DC):
                        pi = (tc * DC + c) % 2
                        wge(v, cP, mm_ + f"dtT{tc}_{c}")
                        wge(v, cS, mm_ + f"spl{tc}_{c}")
                        v.tensor_scalar(out=dtT[c][:, tc * 512:(tc + 1) * 512],
                                        in0=pm_dtT[pi][:], scalar1=0.0, scalar2=None,
                                        op0=AluOpType.max).then_inc(cV.sem, 1)
                        cV.n += 1
                        rec(mm_ + f"dtw{tc}_{c}", cV)
                        v.wait_ge(cV.sem, cV.n)
                        v.tensor_tensor(out=dtT[c][:, tc * 512:(tc + 1) * 512],
                                        in0=dtT[c][:, tc * 512:(tc + 1) * 512],
                                        in1=sp_t[3 * ((tc * DC + c) % 2) + 2][:],
                                        op=AluOpType.add
                                        ).then_inc(cV.sem, 1)
                        cV.n += 1
                        rec(mm_ + f"spw{tc}_{c}", cV)
                for c in range(DC if mstop >= 24 else 0):
                    v.wait_ge(cV.sem, cV.n)
                    v.tensor_tensor(out=dtx[c][:], in0=dtT[c][:], in1=xnT[c][:],
                                    op=AluOpType.mult).then_inc(cV.sem, 1)
                    cV.n += 1
                rec(mm_ + "dtxall", cV)

            for si in range(DSTATE if mstop >= 40 else 0):
                @block.vector
                def _(v, si=si):
                    pr = si % 2
                    for c in range(DC):
                        gr = (m * DSTATE + si) * DC + c
                        rot2 = gr % 2
                        rot4 = gr % 4
                        wge(v, cS, mm_ + f"bbev{si}")
                        v.wait_ge(cV.sem, cV.n)
                        v.tensor_tensor(out=ubuf[rot2][:], in0=dtx[c][:],
                                        in1=bb_sb[pr][:], op=AluOpType.mult
                                        ).then_inc(cV.sem, 1)
                        cV.n += 1
                        wge(v, cS, mm_ + f"ab{si}_{c}")
                        v.wait_ge(cV.sem, cV.n)
                        ini = 0.0 if th == 0 else hcar[:, c * 16 + si:c * 16 + si + 1]
                        v.tensor_tensor_scan(out=hbuf[rot2][:], data0=abar[rot4][:],
                                             data1=ubuf[rot2][:], initial=ini,
                                             op0=AluOpType.mult,
                                             op1=AluOpType.add).then_inc(cV.sem, 1)
                        cV.n += 1
                        rec(mm_ + f"abw{si}_{c}", cV)
                        v.wait_ge(cV.sem, cV.n)
                        v.tensor_tensor(out=tmpb[rot2][:], in0=hbuf[rot2][:],
                                        in1=cb_sb[pr][:], op=AluOpType.mult
                                        ).then_inc(cV.sem, 1)
                        cV.n += 1
                        v.wait_ge(cV.sem, cV.n)
                        if si == 0:
                            v.tensor_copy(out=ybuf[c][:], in_=tmpb[rot2][:]
                                          ).then_inc(cV.sem, 1)
                        else:
                            v.tensor_tensor(out=ybuf[c][:], in0=ybuf[c][:],
                                            in1=tmpb[rot2][:], op=AluOpType.add
                                            ).then_inc(cV.sem, 1)
                        cV.n += 1
                        if th < NTH - 1:
                            v.wait_ge(cV.sem, cV.n)
                            v.tensor_copy(out=hcar[:, c * 16 + si:c * 16 + si + 1],
                                          in_=hbuf[rot2][:, THS - 1:THS]
                                          ).then_inc(cV.sem, 1)
                            cV.n += 1
                    rec(mm_ + f"bbuse{si}", cV)

            @block.vector
            def _(v):
                for c in range(DC if mstop >= 50 else 0):
                    v.wait_ge(cV.sem, cV.n)
                    gsc = m * DC + c
                    if gsc >= 2:
                        pgs = gsc - 2
                        wge(v, cWs[c % 2], f"M{pgs // DC}_spw{pgs % DC}w")
                    v.scalar_tensor_tensor(out=soutb[c % 2][:], in0=xnT[c][:],
                                           scalar=dvec[:, c:c + 1], in1=ybuf[c][:],
                                           op0=AluOpType.mult,
                                           op1=AluOpType.add).then_inc(cV.sem, 1)
                    cV.n += 1
                    rec(mm_ + f"sout{c}", cV)
            rec(mm_ + "vend", cV)

            @block.scalar
            def _(s):
                for c in range(DC if mstop >= 50 else 0):
                    wge(s, cV, mm_ + f"sout{c}")
                    s.dma_start(out=s16[b, c, :, t0:t0 + THS],
                                in_=soutb[c % 2][:]).then_inc(cWs[c % 2].sem, 16)
                    cWs[c % 2].n += 16
                    rec(mm_ + f"spw{c}w", cWs[c % 2])
        MAMBA_W0, MAMBA_W1 = cWs[0].n, cWs[1].n

    if stage >= 3:
        for t_ in reversed(mamba_ctxs):
            t_.__exit__(None, None, None)
        for t_ in reversed(p0_ctxs):
            t_.__exit__(None, None, None)

        fctxs = []
        def fsb(name, shape, dt):
            t = nc.sbuf_tensor(name, shape, dt)
            h = t.__enter__(); fctxs.append(t); return h
        def fps(name, shape, dt):
            t = nc.psum_tensor(name, shape, dt)
            h = t.__enter__(); fctxs.append(t); return h

        wq16 = [fsb(f"f_wq{c}", [128, 1536], BF16) for c in range(DC)]
        wo16h = [fsb(f"f_wo{h}", [64, 1024], BF16) for h in range(NHEAD)]
        ssq16 = fsb("f_ssq", [128, T], BF16); ssk16 = fsb("f_ssk", [128, T], BF16)
        csqf = fsb("f_csqf", [128, T], BF16); cskf = fsb("f_cskf", [128, T], BF16)
        rperm16 = fsb("f_rp", [128, 128], BF16)
        rpermq16 = fsb("f_rpq", [128, 128], BF16)
        qscol = fsb("f_qs", [128, 1], F32); kscol = fsb("f_ks", [128, 1], F32)
        hd2_16 = fsb("f_hd2", [128, 2], BF16); hd2T16 = fsb("f_hd2T", [2, 128], BF16)
        ones16 = fsb("f_ones", [128, 128], BF16)
        onesf32 = fsb("f_onesf32", [65, 128], F32)
        trim_f = fsb("f_trim", [128, 256], BF16)
        trilo16 = fsb("f_trilo", [128, 128], BF16); triup16 = fsb("f_triup", [128, 128], BF16)
        lngc = fsb("f_lng", [128, DC], F32); lnbc = fsb("f_lnb", [128, DC], F32)
        b1ac = fsb("f_b1a", [128, 22], F32); b1gc = fsb("f_b1g", [128, 22], F32)
        b2c = fsb("f_b2", [128, DC], F32)

        residb = [[fsb(f"f_res{p}_{c}", [128, 512], BF16) for c in range(DC)]
                  for p in range(2)]
        hshb = [[fsb(f"f_hsh{p}_{j}", [128, 512], BF16) for j in range(4)]
                for p in range(2)]
        sq_sb = [fsb(f"f_sq{i}", [128, 512], BF16) for i in range(2)]
        n2sb = fsb("f_n2", [2, 512], F32)
        rstd2 = [fsb(f"f_rstd{i}", [2, 512], BF16) for i in range(2)]
        rb_sb = [fsb(f"f_rb{i}", [128, 512], BF16) for i in range(2)]
        qn_sb = [fsb(f"f_qn{i}", [128, 512], BF16) for i in range(2)]
        t1_sb = [fsb(f"f_t1{i}", [128, 512], BF16) for i in range(2)]
        qT = [fsb(f"f_qT{p}", [128, 512], BF16) for p in range(4)]
        kTA = [fsb(f"f_kTA{p}", [128, 512], BF16) for p in range(4)]
        kTB = [fsb(f"f_kTB{p}", [128, 512], BF16) for p in range(4)]
        vtA = [fsb(f"f_vtA{t}", [128, 8 * 65], BF16) for t in range(4)]
        vtB = [fsb(f"f_vtB{t}", [128, 8 * 65], BF16) for t in range(4)]
        nm32 = fsb("f_nm32", [128, 34], F32)
        nm32b = fsb("f_nm32b", [128, 32], BF16)
        nmT_sb = fsb("f_nmT", [32, 128], BF16)
        mrowcat = fsb("f_mrow", [65, 2048], BF16)
        pT = [fsb(f"f_pT{k}", [128, 512], BF16) for k in range(8)]
        rdenb = fsb("f_rden", [128, 512], F32)
        rd_sb = fsb("f_rd", [64, 512], F32)
        oT = [fsb(f"f_oT{h}", [64, 512], BF16) for h in range(NHEAD)]
        strt = [fsb(f"f_strt{i}", [128, 512], BF16) for i in range(2)]
        sh2 = {c: fsb(f"f_sh2{c}", [128, 512], BF16) for c in range(4, DC)}
        shcolb = fsb("f_shcol", [128, 32], BF16)
        res2 = [fsb(f"f_res2{c}", [128, 512], BF16) for c in range(DC)]
        lnrow_f = fsb("f_lnrf", [1, 1024], F32)
        lnrow_b = fsb("f_lnrb", [1, 1024], BF16)
        h2 = [fsb(f"f_h2{c}", [128, 512], BF16) for c in range(DC)]
        w1sl = [fsb(f"f_w1s{p}", [128, 2048], BF16) for p in range(2)]
        gel_sb = [fsb(f"f_gel{i}", [128, 512], BF16) for i in range(2)]
        hidst = [fsb(f"f_hid{j}", [128, 512], BF16) for j in range(22)]
        w2sl = [fsb(f"f_w2s{p}", [128, 1024], BF16) for p in range(2)]
        murow = lnrow_b[:, 0:512]
        rstdrow = lnrow_b[:, 512:1024]
        m2row = lnrow_f[:, 0:512]
        varrow = lnrow_f[:, 512:1024]
        if T >= 2048:
            xrl = csqf[:].bitcast(F32)[:, 0:1024]
            obuf = cskf[:].bitcast(F32)[:, 0:1024]
        else:
            xrl = fsb("f_xrl", [128, 1024], F32)[:]
            obuf = fsb("f_obuf", [128, 1024], F32)[:]
        lntmp_ap = rdenb[:]
        lntmp2_ap = obuf[:, 0:512]

        pool = [fps(f"pf_p{i}", [128, 512], F32) for i in range(6)]
        pf_ss = fps("pf_ss", [128, 1024], F32)

        fGc = [Ch(nc, "f_Gc0"), Ch(nc, "f_Gc1")]
        WCAST = [0, 0]

        @block.gpsimd
        def _(g):
            for l in range(2):
                for ja in range(22):
                    aw = 128 if ja < 21 else HID - 21 * 128
                    g.dma_start(
                        out=w1b[l, ja, :, 0:1024].rearrange("p (k j) -> p k j", k=8)[:, :, 0:aw],
                        in_=w1_in[l, :, ja * 128:ja * 128 + aw]
                            .rearrange("(k p) j -> p k j", p=128)).then_inc(fGc[l].sem, 16)
                    fGc[l].n += 16
                    g.dma_start(
                        out=w1b[l, ja, :, 1024:2048].rearrange("p (k j) -> p k j", k=8)[:, :, 0:aw],
                        in_=w1_in[l, :, HID + ja * 128:HID + ja * 128 + aw]
                            .rearrange("(k p) j -> p k j", p=128)).then_inc(fGc[l].sem, 16)
                    fGc[l].n += 16
                    g.dma_start(out=w2b[l, ja, 0:aw, :],
                                in_=w2_in[l, ja * 128:ja * 128 + aw, :]).then_inc(fGc[l].sem, 16)
                    fGc[l].n += 16
                WCAST[l] = fGc[l].n

        fL = [Ch(nc, "f_L0"), Ch(nc, "f_L1")]
        fG = Ch(nc, "f_G")
        fGw1 = [Ch(nc, "f_Gw1a"), Ch(nc, "f_Gw1b")]
        fGw2 = [Ch(nc, "f_Gw2a"), Ch(nc, "f_Gw2b")]
        fP = Ch(nc, "f_P"); fS = Ch(nc, "f_S"); fV = Ch(nc, "f_V")
        fWs = [Ch(nc, "f_W0"), Ch(nc, "f_W1")]
        fGm = Ch(nc, "f_Gm")
        bar = Ch(nc, "f_bar")

        @block.gpsimd
        def _(g):
            for c in [cL, cG, cP, cS, cV, cWs[0], cWs[1]]:
                g.wait_ge(c.sem, c.n)
            g.wait_ge(ch_sp.sem, ch_sp.n)
            g.memset(nm32[0:1, 0:1], 0.0).then_inc(bar.sem, 1)
            bar.n += 1
        BAR0 = bar.n

        @block.gpsimd
        def _(g):
            g.wait_ge(bar.sem, BAR0)
            for dst, src_ in [(ssq16[:], ssq_in), (ssk16[:], ssk_in),
                              (hd2_16[:], hd2_in), (hd2T16[:], hd2T_in),
                              (ones16[:], onesf_in), (onesf32[:], onesf_in[0:65, :]),
                              (trim_f[:], trimadd_in), (trilo16[:], trilo_in),
                              (triup16[:], triup_in)]:
                g.dma_start(out=dst, in_=src_).then_inc(fG.sem, 16)
                fG.n += 16
            for t_ in range(4):
                g.memset(vtA[t_][:, :].rearrange("p (h c) -> p h c", h=8)[:, :, 64:65],
                         1.0).then_inc(fGm.sem, 1)
                fGm.n += 1
                g.memset(vtB[t_][:, :].rearrange("p (h c) -> p h c", h=8)[:, :, 64:65],
                         1.0).then_inc(fGm.sem, 1)
                fGm.n += 1
            for j in range(4):
                if j < 3:
                    g.memset(pT[j][:, (j + 1) * 128:512], 0.0).then_inc(fGm.sem, 1)
                    fGm.n += 1
                if j > 0:
                    g.memset(pT[4 + j][:, 0:j * 128], 0.0).then_inc(fGm.sem, 1)
                    fGm.n += 1
        FGM_ALL = fGm.n

        def _emit_attn_window(l, b, w, strm_in, strm_out):
            t0 = w * WIN
            kTc = kTA if w % 2 == 0 else kTB
            kTp = kTB if w % 2 == 0 else kTA
            vtc = vtA if w % 2 == 0 else vtB
            vtp = vtB if w % 2 == 0 else vtA
            wp = f"F{l}A{b}_{w}_"
            idx = b * NW + w
            resid = residb[idx % 2]
            hsh = hshb[idx % 2]
            gwp = None
            if idx >= 1:
                b1_, w1_ = divmod(idx - 1, NW)
                gwp = f"F{l}A{b1_}_{w1_}_"
            g2p = None
            if idx >= 2:
                b2_, w2_ = divmod(idx - 2, NW)
                g2p = f"F{l}A{b2_}_{w2_}_"
            qku = [("q", hp) for hp in range(4)] + [("k", hp) for hp in range(4)]
            kts = ([("p", j) for j in range(4)] if w > 0 else []) + \
                  [("c", j) for j in range(4)]

            @block.sync
            def _(sync):
                if g2p is not None:
                    wge(sync, fP, g2p + "pe_end")
                    wge(sync, fV, g2p + "v_end")
                if gwp is not None:
                    pass
                elif l > 0:
                    wge(sync, fWs[0], f"F{l-1}_B_done_W0")
                    wge(sync, fWs[1], f"F{l-1}_B_done_W1")
                    wge(sync, fV, f"F{l-1}_B_v_end")
                    wge(sync, fP, f"F{l-1}_B_pe_end")
                else:
                    sync.wait_ge(bar.sem, BAR0)
                fl = fL[idx % 2]
                for c in range(DC):
                    sync.dma_start(out=resid[c][:],
                                   in_=strm_in[b, c, :, t0:t0 + 512]).then_inc(fl.sem, 16)
                    fl.n += 16
                for j in range(4):
                    if w > 0:
                        sync.dma_start(out=hsh[j][:],
                                       in_=strm_in[b, 4 + j, :, t0 - 1:t0 + 511]
                                       ).then_inc(fl.sem, 16)
                    else:
                        sync.dma_start(out=hsh[j][:, 1:512],
                                       in_=strm_in[b, 4 + j, :, 0:511]).then_inc(fl.sem, 16)
                    fl.n += 16
            rec(wp + "loads", fL[idx % 2])

            def srhs(kc):
                return resid[kc][:] if kc < 4 else hsh[kc - 4][:]

            @block.gpsimd
            def _(g):
                wge(g, fS, wp + "nmT")
                if gwp is not None:
                    wge(g, fP, gwp + "st_all")
                g.dma_start(out=mrow_dram[b, w, :].rearrange("(a c) -> a c", a=32),
                            in_=nmT_sb[:]).then_inc(fG.sem, 16)
                fG.n += 16
                g.wait_ge(fG.sem, fG.n)
                g.dma_start(out=mrowcat[0:1, :],
                            in_=mrow_dram[b, w, 0:2048][None, :]).then_inc(fG.sem, 16)
                fG.n += 16
                g.dma_start(out=mrowcat[64:65, :],
                            in_=mrow_dram[b, w, 2048:4096][None, :]).then_inc(fG.sem, 16)
                fG.n += 16
            rec(wp + "mrow", fG)

            qkbank = [pool[0][:], pool[1][:], pf_ss[:, 0:512]]

            @block.tensor
            def _(te):
                wge(te, fL[idx % 2], wp + "loads")
                if idx == 0:
                    te.wait_ge(fGm.sem, FGM_ALL)
                    wge(te, fG, f"F{l}_wl")
                    wge(te, fV, f"F{l}_tabs_done")
                if w == 0:
                    wge(te, fV, wp + "hshz")
                for k in range(len(qku) + 3):
                    if k < 8:
                        qx, hp = qku[k]
                        if k >= 3:
                            wge(te, fV, wp + f"qn{k-3}")
                        elif gwp is not None:
                            wge(te, fV, gwp + ("strm6" if k == 0 else
                                               "strm7" if k == 1 else "nm32b"))
                        off = 0 if qx == "q" else 512
                        for kc in range(DC):
                            mmq = te.matmul(out=qkbank[k % 3],
                                            lhsT=wq16[kc][:, off + hp * 128:off + (hp + 1) * 128],
                                            rhs=srhs(kc),
                                            start=(kc == 0), stop=(kc == DC - 1))
                        mmq.then_inc(fP.sem, 1)
                        fP.n += 1
                        rec(wp + f"qk{k}", fP)
                    if 0 <= k - 1 < 8:
                        kk = k - 1
                        wge(te, fS, wp + f"sq{kk}")
                        if kk >= 1:
                            wge(te, fV, wp + f"nmax{kk-1}")
                        elif gwp is not None:
                            wge(te, fV, gwp + "nmax7")
                        te.matmul(out=pool[2][0:2, :], lhsT=hd2_16[:],
                                  rhs=sq_sb[kk % 2][:], start=True,
                                  stop=True).then_inc(fP.sem, 1)
                        fP.n += 1
                        rec(wp + f"n2{kk}", fP)
                    if 0 <= k - 2 < 8:
                        kk = k - 2
                        wge(te, fS, wp + f"rstd{kk}")
                        if kk >= 1:
                            wge(te, fS, wp + f"rbc{kk-1}")
                        te.matmul(out=pool[3][:], lhsT=hd2T16[:],
                                  rhs=rstd2[kk % 2][:], start=True,
                                  stop=True).then_inc(fP.sem, 1)
                        fP.n += 1
                        rec(wp + f"rbb{kk}", fP)
                    if 0 <= k - 3 < 8:
                        kk = k - 3
                        qx, hp = qku[kk]
                        wge(te, fV, wp + f"qn{kk}")
                        if kk >= 1:
                            wge(te, fV, wp + f"t2{kk-1}")
                        te.matmul(out=pool[4][:], lhsT=(rpermq16 if qx == "q" else rperm16)[:],
                                  rhs=qn_sb[kk % 2][:], start=True,
                                  stop=True).then_inc(fP.sem, 1)
                        fP.n += 1
                        rec(wp + f"rot{kk}", fP)
                for ts in range(4):
                    if ts >= 1:
                        wge(te, fV, wp + f"vt{ts-1}")
                    for kc in range(DC):
                        mmv = te.matmul(out=pool[5][:],
                                        lhsT=srhs(kc)[:, ts * 128:(ts + 1) * 128],
                                        rhs=wq16[kc][:, 1024:1536],
                                        start=(kc == 0), stop=(kc == DC - 1))
                    mmv.then_inc(fP.sem, 1)
                    fP.n += 1
                    rec(wp + f"vp{ts}", fP)
                for h in range(NHEAD):
                    hp, hi = h // 2, h % 2
                    base = hi * 64
                    wge(te, fV, wp + "qT" + str(("q", hp)))
                    wge(te, fV, wp + "qT" + str(("k", hp)))
                    for qc in range(4):
                        w1_ = 512 - qc * 128
                        w2_ = qc * 128 + 128
                        if (h, qc) != (0, 0):
                            wge(te, fV, wp + (f"stats{h}_{qc-1}" if qc > 0
                                              else f"stats{h-1}_3"))
                        else:
                            wge(te, fV, wp + "qn7")
                        if w > 0:
                            te.matmul(out=pf_ss[:, 0:w1_],
                                      lhsT=qT[hp][base:base + 64, qc * 128:(qc + 1) * 128],
                                      rhs=kTp[hp][base:base + 64, qc * 128:512],
                                      start=True, stop=True)
                            te.matmul(out=pf_ss[:, 512:512 + w2_],
                                      lhsT=qT[hp][base:base + 64, qc * 128:(qc + 1) * 128],
                                      rhs=kTc[hp][base:base + 64, 0:w2_],
                                      start=True, stop=True).then_inc(fP.sem, 1)
                        else:
                            te.matmul(out=pf_ss[:, 0:w2_],
                                      lhsT=qT[hp][base:base + 64, qc * 128:(qc + 1) * 128],
                                      rhs=kTc[hp][base:base + 64, 0:w2_],
                                      start=True, stop=True).then_inc(fP.sem, 1)
                        fP.n += 1
                        rec(wp + f"ss{h}_{qc}", fP)
                p2bf = pool[2][:].bitcast(BF16)
                wge(te, fV, wp + "nm32b")
                te.transpose(out=p2bf[0:32, 0:128], in_=nm32b[:],
                             identity=ident16[:]).then_inc(fP.sem, 1)
                fP.n += 1
                rec(wp + "nmT_pe", fP)
                for h in range(NHEAD):
                    hp, hi = h // 2, h % 2
                    base = hi * 64
                    for ki, (src_, j) in enumerate(kts):
                        if src_ == "p":
                            r0, r1 = 0, (j + 1) * 128
                            ktsrc = kTp
                        else:
                            r0, r1 = j * 128, 512
                            ktsrc = kTc
                        pp = pool[ki % 2]
                        if h > 0 or ki >= 2:
                            ph, pki = (h, ki - 2) if ki >= 2 else (h - 1, len(kts) - 2 + ki)
                            wge(te, fS, wp + f"exp{ph}_{pki}")
                        if h == 0 and ki < 2:
                            wge(te, fG, wp + "mrow")
                        te.matmul(out=pp[:, r0:r1],
                                  lhsT=ktsrc[hp][base:base + 64, j * 128:(j + 1) * 128],
                                  rhs=qT[hp][base:base + 64, r0:r1],
                                  start=True, stop=False)
                        mseg = (h // 2) * 512
                        te.matmul(out=pp[:, r0:r1],
                                  lhsT=ones16[base:base + 1, 0:128],
                                  rhs=mrowcat[base:base + 1, mseg + r0:mseg + r1],
                                  start=False, stop=True).then_inc(fP.sem, 1)
                        fP.n += 1
                        rec(wp + f"st{h}_{ki}", fP)
                    if h == 0:
                        wge(te, fV, wp + "vt3")
                    else:
                        wge(te, fV, wp + f"oT{h-1}")
                    first = True
                    for ki, (src_, j) in enumerate(kts):
                        kt_idx = j if src_ == "p" else 4 + j
                        vt_ = vtp if src_ == "p" else vtc
                        wge(te, fV, wp + f"pm{h}_{ki}")
                        mmp = te.matmul(out=pool[5][0:65, :],
                                        lhsT=vt_[j][:, h * 65:(h + 1) * 65],
                                        rhs=pT[kt_idx][:, 0:512],
                                        start=first, stop=(ki == len(kts) - 1))
                        first = False
                    mmp.then_inc(fP.sem, 1)
                    fP.n += 1
                    rec(wp + f"pv{h}", fP)
                    wge(te, fS, wp + f"rden{h}")
                    te.matmul(out=pool[3][0:64, :], lhsT=onesf32[64:65, 0:64],
                              rhs=rdenb[64:65, :], start=True,
                              stop=True).then_inc(fP.sem, 1)
                    fP.n += 1
                    rec(wp + f"rdb{h}", fP)
                rec(wp + "st_all", fP)
                for dc in range(DC):
                    wge(te, fV, wp + "oT7")
                    if dc >= 2:
                        wge(te, fV, wp + f"strm{dc-2}")
                    for h in range(NHEAD):
                        mmw = te.matmul(out=pool[dc % 2][:],
                                        lhsT=wo16h[h][:, dc * 128:(dc + 1) * 128],
                                        rhs=oT[h][:], start=(h == 0),
                                        stop=(h == NHEAD - 1))
                    mmw.then_inc(fP.sem, 1)
                    fP.n += 1
                    rec(wp + f"wo{dc}", fP)
            rec(wp + "pe_end", fP)

            @block.scalar
            def _(s):
                for k in range(len(qku) + 2):
                    if k < 8:
                        wge(s, fP, wp + f"qk{k}")
                        if k >= 2:
                            wge(s, fP, wp + f"n2{k-2}")
                        s.activation(out=sq_sb[k % 2][:], in_=qkbank[k % 3],
                                     func=AF.Square).then_inc(fS.sem, 1)
                        fS.n += 1
                        rec(wp + f"sq{k}", fS)
                    if 0 <= k - 1 < 8:
                        kk = k - 1
                        wge(s, fV, wp + f"nmax{kk}")
                        s.activation(out=n2sb[:], in_=n2sb[:], func=AF.Ln
                                     ).then_inc(fS.sem, 1)
                        fS.n += 1
                        s.wait_ge(fS.sem, fS.n)
                        if kk >= 2:
                            wge(s, fP, wp + f"rbb{kk-2}")
                        s.activation(out=rstd2[kk % 2][:], in_=n2sb[:], func=AF.Exp,
                                     scale=-0.5).then_inc(fS.sem, 1)
                        fS.n += 1
                        rec(wp + f"rstd{kk}", fS)
                    if 0 <= k - 2 < 8:
                        kk = k - 2
                        wge(s, fP, wp + f"rbb{kk}")
                        if kk >= 2:
                            wge(s, fV, wp + f"qn{kk-2}")
                        s.copy(out=rb_sb[kk % 2][:], in_=pool[3][:]).then_inc(fS.sem, 1)
                        fS.n += 1
                        rec(wp + f"rbc{kk}", fS)
                wge(s, fP, wp + "nmT_pe")
                if gwp is not None:
                    wge(s, fG, gwp + "mrow")
                p2bf = pool[2][:].bitcast(BF16)
                s.copy(out=nmT_sb[:], in_=p2bf[0:32, 0:128]).then_inc(fS.sem, 1)
                fS.n += 1
                rec(wp + "nmT", fS)
                for h in range(NHEAD):
                    for ki, (src_, j) in enumerate(kts):
                        kt_idx = j if src_ == "p" else 4 + j
                        r0, r1 = (0, (j + 1) * 128) if src_ == "p" else (j * 128, 512)
                        pp = pool[ki % 2]
                        wge(s, fP, wp + f"st{h}_{ki}")
                        if h > 0:
                            wge(s, fP, wp + f"pv{h-1}")
                        s.activation(out=pT[kt_idx][:, r0:r1], in_=pp[:, r0:r1],
                                     func=AF.Exp).then_inc(fS.sem, 1)
                        fS.n += 1
                        rec(wp + f"exp{h}_{ki}", fS)
                    wge(s, fP, wp + f"pv{h}")
                    s.activation(out=rdenb[64:65, :], in_=pool[5][64:65, :],
                                 func=AF.Ln).then_inc(fS.sem, 1)
                    fS.n += 1
                    s.wait_ge(fS.sem, fS.n)
                    s.activation(out=rdenb[64:65, :], in_=rdenb[64:65, :],
                                 func=AF.Exp, scale=-1.0).then_inc(fS.sem, 1)
                    fS.n += 1
                    rec(wp + f"rden{h}", fS)
                    wge(s, fP, wp + f"rdb{h}")
                    s.copy(out=rd_sb[:], in_=pool[3][0:64, :]).then_inc(fS.sem, 1)
                    fS.n += 1
                    rec(wp + f"rdc{h}", fS)
                for dc in range(DC):
                    wge(s, fV, wp + f"strm{dc}")
                    s.dma_start(out=strm_out[b, dc, :, t0:t0 + 512],
                                in_=strt[dc % 2][:]).then_inc(fWs[dc % 2].sem, 16)
                    fWs[dc % 2].n += 16
                    rec(wp + f"sp{dc}", fWs[dc % 2])
            rec(wp + "s_end", fS)

            @block.vector
            def _(v):
                if w == 0:
                    wge(v, fL[idx % 2], wp + "loads")
                    if g2p is not None:
                        wge(v, fP, g2p + "pe_end")
                    for j in range(4):
                        v.memset(hsh[j][:, 0:1], 0.0).then_inc(fV.sem, 1)
                        fV.n += 1
                    rec(wp + "hshz", fV)
                for k in range(len(qku) + 3):
                    if 0 <= k - 1 < 8:
                        kk = k - 1
                        wge(v, fP, wp + f"n2{kk}")
                        if kk >= 1:
                            wge(v, fS, wp + f"rstd{kk-1}")
                        v.tensor_scalar(out=n2sb[:], in0=pool[2][0:2, :],
                                        scalar1=cv[0:2, 2:3], scalar2=None,
                                        op0=AluOpType.max).then_inc(fV.sem, 1)
                        fV.n += 1
                        rec(wp + f"nmax{kk}", fV)
                    if 0 <= k - 2 < 8:
                        kk = k - 2
                        wge(v, fS, wp + f"rbc{kk}")
                        v.wait_ge(fV.sem, fV.n)
                        v.tensor_tensor(out=qn_sb[kk % 2][:], in0=qkbank[kk % 3],
                                        in1=rb_sb[kk % 2][:], op=AluOpType.mult
                                        ).then_inc(fV.sem, 1)
                        fV.n += 1
                        rec(wp + f"qn{kk}", fV)
                    if 0 <= k - 3 < 8:
                        kk = k - 3
                        qx, hp = qku[kk]
                        ctab = csqf if qx == "q" else cskf
                        stab = ssq16 if qx == "q" else ssk16
                        dstT = qT if qx == "q" else kTc
                        v.wait_ge(fV.sem, fV.n)
                        v.tensor_tensor(out=t1_sb[kk % 2][:], in0=qn_sb[kk % 2][:],
                                        in1=ctab[:, t0:t0 + 512],
                                        op=AluOpType.mult).then_inc(fV.sem, 1)
                        fV.n += 1
                        wge(v, fP, wp + f"rot{kk}")
                        v.wait_ge(fV.sem, fV.n)
                        v.tensor_tensor(out=qn_sb[kk % 2][:], in0=pool[4][:],
                                        in1=stab[:, t0:t0 + 512],
                                        op=AluOpType.mult).then_inc(fV.sem, 1)
                        fV.n += 1
                        rec(wp + f"t2{kk}", fV)
                        v.wait_ge(fV.sem, fV.n)
                        if g2p is not None and qx == "k":
                            wge(v, fP, g2p + "pe_end")
                        v.tensor_tensor(out=dstT[hp][:], in0=t1_sb[kk % 2][:],
                                        in1=qn_sb[kk % 2][:], op=AluOpType.add
                                        ).then_inc(fV.sem, 1)
                        fV.n += 1
                        rec(wp + "qT" + str((qx, hp)), fV)
                for ts in range(4):
                    wge(v, fP, wp + f"vp{ts}")
                    if g2p is not None:
                        wge(v, fP, g2p + "pe_end")
                    v.tensor_copy(
                        out=vtc[ts][:, :].rearrange("p (h c) -> p h c", h=8)[:, :, 0:64],
                        in_=pool[5][:, :].rearrange("p (h c) -> p h c", h=8)
                        ).then_inc(fV.sem, 1)
                    fV.n += 1
                    rec(wp + f"vt{ts}", fV)
                for h in range(NHEAD):
                    for qc in range(4):
                        w1_ = 512 - qc * 128
                        w2_ = qc * 128 + 128
                        wge(v, fP, wp + f"ss{h}_{qc}")
                        if w > 0:
                            v.tensor_tensor(out=pf_ss[:, 0:128], in0=pf_ss[:, 0:128],
                                            in1=trim_f[:, 0:128], op=AluOpType.add
                                            ).then_inc(fV.sem, 1)
                            fV.n += 1
                            v.tensor_tensor(out=pf_ss[:, 512 + w2_ - 128:512 + w2_],
                                            in0=pf_ss[:, 512 + w2_ - 128:512 + w2_],
                                            in1=trim_f[:, 128:256], op=AluOpType.add
                                            ).then_inc(fV.sem, 1)
                            fV.n += 1
                            v.wait_ge(fV.sem, fV.n)
                            v.tensor_reduce(out=nm32[:, 32:33], in_=pf_ss[:, 0:w1_],
                                            axis=AX.X, op=AluOpType.max).then_inc(fV.sem, 1)
                            fV.n += 1
                            v.tensor_reduce(out=nm32[:, 33:34], in_=pf_ss[:, 512:512 + w2_],
                                            axis=AX.X, op=AluOpType.max).then_inc(fV.sem, 1)
                            fV.n += 1
                            v.wait_ge(fV.sem, fV.n)
                            v.tensor_tensor(out=nm32[:, 32:33], in0=nm32[:, 32:33],
                                            in1=nm32[:, 33:34], op=AluOpType.max
                                            ).then_inc(fV.sem, 1)
                            fV.n += 1
                        else:
                            v.tensor_tensor(out=pf_ss[:, w2_ - 128:w2_],
                                            in0=pf_ss[:, w2_ - 128:w2_],
                                            in1=trim_f[:, 128:256], op=AluOpType.add
                                            ).then_inc(fV.sem, 1)
                            fV.n += 1
                            v.wait_ge(fV.sem, fV.n)
                            v.tensor_reduce(out=nm32[:, 32:33], in_=pf_ss[:, 0:w2_],
                                            axis=AX.X, op=AluOpType.max).then_inc(fV.sem, 1)
                            fV.n += 1
                        v.wait_ge(fV.sem, fV.n)
                        nmc = (h % 2) * 16 + (h // 2) * 4 + qc
                        v.tensor_scalar(out=nm32[:, nmc:nmc + 1],
                                        in0=nm32[:, 32:33], scalar1=-1.0, scalar2=None,
                                        op0=AluOpType.mult).then_inc(fV.sem, 1)
                        fV.n += 1
                        rec(wp + f"stats{h}_{qc}", fV)
                v.wait_ge(fV.sem, fV.n)
                v.tensor_copy(out=nm32b[:], in_=nm32[:, 0:32]).then_inc(fV.sem, 1)
                fV.n += 1
                rec(wp + "nm32b", fV)
                for h in range(NHEAD):
                    for ki, (src_, j) in enumerate(kts):
                        kt_idx = j if src_ == "p" else 4 + j
                        dg = j * 128
                        msk = trilo16 if src_ == "p" else triup16
                        wge(v, fS, wp + f"exp{h}_{ki}")
                        v.tensor_tensor(out=pT[kt_idx][:, dg:dg + 128],
                                        in0=pT[kt_idx][:, dg:dg + 128],
                                        in1=msk[:], op=AluOpType.mult).then_inc(fV.sem, 1)
                        fV.n += 1
                        rec(wp + f"pm{h}_{ki}", fV)
                    wge(v, fS, wp + f"rdc{h}")
                    v.tensor_tensor(out=oT[h][:], in0=pool[5][0:64, :],
                                    in1=rd_sb[:], op=AluOpType.mult).then_inc(fV.sem, 1)
                    fV.n += 1
                    rec(wp + f"oT{h}", fV)
                    if h == NHEAD - 1:
                        rec(wp + "oT7", fV)
                for dc in range(DC):
                    wge(v, fP, wp + f"wo{dc}")
                    if dc >= 2:
                        wge(v, fWs[dc % 2], wp + f"sp{dc-2}")
                    elif gwp is not None:
                        wge(v, fWs[dc % 2], gwp + f"sp{dc+6}")
                    v.tensor_tensor(out=strt[dc % 2][:], in0=resid[dc][:],
                                    in1=pool[dc % 2][:], op=AluOpType.add
                                    ).then_inc(fV.sem, 1)
                    fV.n += 1
                    rec(wp + f"strm{dc}", fV)
                    if dc >= 4 and w < NW - 1:
                        sci = (b * NW + (w + 1)) * 4 + (dc - 4)
                        v.wait_ge(fV.sem, fV.n)
                        v.tensor_copy(out=shcolb[:, sci:sci + 1],
                                      in_=strt[dc % 2][:, 511:512]).then_inc(fV.sem, 1)
                        fV.n += 1
            rec(wp + "v_end", fV)

        def _emit_ffnmid_unit(l, b, u, strm):
            t0 = u * 512
            up = f"F{l}B{b}_{u}_"
            NU = T // 512
            idx = b * NU + u
            pup = None
            if idx > 0:
                pb, pu = divmod(idx - 1, NU)
                pup = f"F{l}B{pb}_{pu}_"

            @block.sync
            def _(sync):
                if pup is None:
                    wge(sync, fWs[0], f"F{l}_A_done_W0")
                    wge(sync, fWs[1], f"F{l}_A_done_W1")
                    wge(sync, fV, f"F{l}A{NB-1}_{NW-1}_v_end")
                else:
                    wge(sync, fV, pup + "v_end")
                    wge(sync, fP, pup + "pe_end")
                fl = fL[idx % 2]
                for c in range(DC):
                    sync.dma_start(out=res2[c][:],
                                   in_=strm[b, c, :, t0:t0 + 512]).then_inc(fl.sem, 16)
                    fl.n += 16
                for c in range(4, DC):
                    sync.dma_start(out=sh2[c][:, 1:512],
                                   in_=strm[b, c, :, t0:t0 + 511]).then_inc(fl.sem, 16)
                    fl.n += 16
            rec(up + "loads", fL[idx % 2])

            shf = [res2[0], res2[1], res2[2], res2[3], sh2[4], sh2[5], sh2[6], sh2[7]]

            def w1banka(ja, aw):
                return (pool[4][0:aw, :] if ja % 2 == 0 else pf_ss[0:aw, 0:512])

            def w1bankg(ja, aw):
                return (pool[5][0:aw, :] if ja % 2 == 0 else pf_ss[0:aw, 512:1024])

            @block.gpsimd
            def _(g):
                if idx == 0:
                    g.wait_ge(fGc[l].sem, WCAST[l])
                for ja in range(22):
                    aw = 128 if ja < 21 else HID - 21 * 128
                    pg = ja % 2
                    gja = idx * 22 + ja
                    if gja >= 2:
                        pg_ = gja - 2
                        pidx2, pja = divmod(pg_, 22)
                        pb2, pu2 = divmod(pidx2, NU)
                        wge(g, fP, f"F{l}B{pb2}_{pu2}_w1_{pja}")
                    elif l > 0 and ja < 2:
                        wge(g, fP, f"F{l-1}_B_pe_end")
                    gw = fGw1[pg]
                    g.dma_start(out=w1sl[pg][:],
                                in_=w1b[l, ja]).then_inc(gw.sem, 16)
                    gw.n += 16
                    rec(up + f"gw1_{ja}", gw)
                for jc in range(22):
                    aw = 128 if jc < 21 else HID - 21 * 128
                    pg = jc % 2
                    kidx = idx * 22 + jc
                    if kidx >= 2:
                        pg_ = kidx - 2
                        pidx2, pjc = divmod(pg_, 22)
                        pb2, pu2 = divmod(pidx2, NU)
                        wge(g, fP, f"F{l}B{pb2}_{pu2}_w2_{pjc}")
                    gw = fGw2[pg]
                    g.dma_start(out=w2sl[pg][0:aw, :],
                                in_=w2b[l, jc, 0:aw, :]).then_inc(gw.sem, 16)
                    gw.n += 16
                    rec(up + f"gw2_{jc}", gw)

            @block.tensor
            def _(te):
                wge(te, fS, up + "sqs")
                wge(te, fL[idx % 2], up + "loads")
                if pup is not None:
                    wge(te, fV, pup + "out_done")
                for c in range(DC):
                    mms = te.matmul(out=pool[0][0:1, :], lhsT=ones16[:, 0:1],
                                    rhs=shf[c][:], start=(c == 0), stop=(c == DC - 1))
                mms.then_inc(fP.sem, 1)
                fP.n += 1
                for c in range(DC):
                    mms2 = te.matmul(out=pool[1][0:1, :], lhsT=ones16[:, 0:1],
                                     rhs=h2[c][:], start=(c == 0), stop=(c == DC - 1))
                mms2.then_inc(fP.sem, 1)
                fP.n += 1
                rec(up + "lnsums", fP)
                wge(te, fS, up + "rsr")
                te.matmul(out=pool[2][:], lhsT=ones16[0:1, :], rhs=murow,
                          start=True, stop=True).then_inc(fP.sem, 1)
                fP.n += 1
                te.matmul(out=pool[3][:], lhsT=ones16[0:1, :], rhs=rstdrow,
                          start=True, stop=True).then_inc(fP.sem, 1)
                fP.n += 1
                rec(up + "lnbcast", fP)
                for ja in range(22):
                    aw = 128 if ja < 21 else HID - 21 * 128
                    pg = ja % 2
                    wge(te, fGw1[pg], up + f"gw1_{ja}")
                    if ja == 0:
                        wge(te, fV, up + "h2_7")
                    if ja >= 2:
                        wge(te, fV, up + f"hid{ja-2}")
                    elif pup is not None:
                        wge(te, fV, pup + "out_done")
                    for kc in range(DC):
                        mma = te.matmul(out=w1banka(ja, aw),
                                        lhsT=w1sl[pg][:, kc * 128:kc * 128 + aw],
                                        rhs=h2[kc][:], start=(kc == 0),
                                        stop=(kc == DC - 1))
                    mma.then_inc(fP.sem, 1)
                    fP.n += 1
                    for kc in range(DC):
                        mmg = te.matmul(out=w1bankg(ja, aw),
                                        lhsT=w1sl[pg][:, 1024 + kc * 128:1024 + kc * 128 + aw],
                                        rhs=h2[kc][:], start=(kc == 0),
                                        stop=(kc == DC - 1))
                    mmg.then_inc(fP.sem, 1)
                    fP.n += 1
                    rec(up + f"w1_{ja}", fP)
                def w2bank(dc):
                    if dc < 6:
                        return pool[dc][:]
                    return pf_ss[:, 0:512] if dc == 6 else pf_ss[:, 512:1024]
                for jc in range(22):
                    aw = 128 if jc < 21 else HID - 21 * 128
                    pg = jc % 2
                    wge(te, fGw2[pg], up + f"gw2_{jc}")
                    wge(te, fV, up + f"hid{jc}")
                    if jc == 0:
                        wge(te, fV, up + "h2_7")
                        wge(te, fV, up + "hid21")
                    for dc in range(DC):
                        te.matmul(out=w2bank(dc),
                                  lhsT=w2sl[pg][0:aw, dc * 128:(dc + 1) * 128],
                                  rhs=hidst[jc][0:aw, :],
                                  start=(jc == 0), stop=(jc == 21)
                                  ).then_inc(fP.sem, 1)
                        fP.n += 1
                        if jc == 21:
                            rec(up + f"w2f{dc}", fP)
                    rec(up + f"w2_{jc}", fP)
            rec(up + "pe_end", fP)

            @block.scalar
            def _(s):
                wge(s, fL[idx % 2], up + "loads")
                wge(s, fV, up + "shz")
                if pup is not None:
                    wge(s, fP, pup + "lnsums")
                    wge(s, fV, pup + "out_done")
                elif l > 0:
                    wge(s, fP, f"F{l-1}_B_pe_end")
                for c in range(DC):
                    s.activation(out=h2[c][:], in_=shf[c][:], func=AF.Square
                                 ).then_inc(fS.sem, 1)
                    fS.n += 1
                rec(up + "sqs", fS)
                wge(s, fV, up + "mu")
                s.activation(out=m2row, in_=murow, func=AF.Square).then_inc(fS.sem, 1)
                fS.n += 1
                rec(up + "m2", fS)
                wge(s, fV, up + "var")
                s.activation(out=varrow, in_=varrow, func=AF.Ln,
                             bias=cv[0:1, 3:4]).then_inc(fS.sem, 1)
                fS.n += 1
                s.wait_ge(fS.sem, fS.n)
                s.activation(out=rstdrow, in_=varrow, func=AF.Exp,
                             scale=-0.5).then_inc(fS.sem, 1)
                fS.n += 1
                rec(up + "rsr", fS)
                for ja in range(22):
                    aw = 128 if ja < 21 else HID - 21 * 128
                    wge(s, fP, up + f"w1_{ja}")
                    if ja >= 2:
                        wge(s, fV, up + f"hid{ja-2}")
                    s.activation(out=gel_sb[ja % 2][0:aw, :], in_=w1bankg(ja, aw),
                                 func=AF.Gelu, bias=b1gc[0:aw, ja:ja + 1]
                                 ).then_inc(fS.sem, 1)
                    fS.n += 1
                    rec(up + f"gel{ja}", fS)
                for dc in range(DC):
                    wge(s, fV, up + f"out{dc}")
                    s.dma_start(out=strm[b, dc, :, t0:t0 + 512],
                                in_=strt[dc % 2][:]).then_inc(fWs[dc % 2].sem, 16)
                    fWs[dc % 2].n += 16
                    rec(up + f"bsp{dc}", fWs[dc % 2])

            @block.vector
            def _(v):
                wge(v, fL[idx % 2], up + "loads")
                if pup is not None:
                    wge(v, fP, pup + "pe_end")
                elif l > 0:
                    wge(v, fP, f"F{l-1}_B_pe_end")
                if t0 == 0:
                    for c in range(4, DC):
                        v.memset(sh2[c][:, 0:1], 0.0).then_inc(fV.sem, 1)
                        fV.n += 1
                else:
                    for c in range(4, DC):
                        sci = (b * NU + u) * 4 + (c - 4)
                        v.tensor_copy(out=sh2[c][:, 0:1],
                                      in_=shcolb[:, sci:sci + 1]).then_inc(fV.sem, 1)
                        fV.n += 1
                rec(up + "shz", fV)
                wge(v, fP, up + "lnsums")
                v.tensor_scalar(out=murow, in0=pool[0][0:1, :], scalar1=1.0 / D,
                                scalar2=None, op0=AluOpType.mult).then_inc(fV.sem, 1)
                fV.n += 1
                rec(up + "mu", fV)
                wge(v, fS, up + "m2")
                v.scalar_tensor_tensor(out=varrow, in0=pool[1][0:1, :],
                                       scalar=1.0 / D, in1=m2row,
                                       op0=AluOpType.mult,
                                       op1=AluOpType.subtract).then_inc(fV.sem, 1)
                fV.n += 1
                rec(up + "var", fV)
                wge(v, fP, up + "lnbcast")
                for c in range(DC):
                    v.tensor_tensor(out=lntmp_ap, in0=shf[c][:], in1=pool[2][:],
                                    op=AluOpType.subtract).then_inc(fV.sem, 1)
                    fV.n += 1
                    v.wait_ge(fV.sem, fV.n)
                    v.tensor_tensor(out=lntmp2_ap, in0=lntmp_ap, in1=pool[3][:],
                                    op=AluOpType.mult).then_inc(fV.sem, 1)
                    fV.n += 1
                    v.wait_ge(fV.sem, fV.n)
                    v.tensor_scalar(out=h2[c][:], in0=lntmp2_ap,
                                    scalar1=lngc[:, c:c + 1], scalar2=lnbc[:, c:c + 1],
                                    op0=AluOpType.mult, op1=AluOpType.add
                                    ).then_inc(fV.sem, 1)
                    fV.n += 1
                    rec(up + f"h2_{c}", fV)
                rec(up + "h2_7", fV)
                for ja in range(22):
                    aw = 128 if ja < 21 else HID - 21 * 128
                    wge(v, fS, up + f"gel{ja}")
                    v.scalar_tensor_tensor(out=hidst[ja][0:aw, :], in0=w1banka(ja, aw),
                                           scalar=b1ac[0:aw, ja:ja + 1],
                                           in1=gel_sb[ja % 2][0:aw, :],
                                           op0=AluOpType.add, op1=AluOpType.mult
                                           ).then_inc(fV.sem, 1)
                    fV.n += 1
                    rec(up + f"hid{ja}", fV)
                def w2bank_v(dc):
                    if dc < 6:
                        return pool[dc][:]
                    return pf_ss[:, 0:512] if dc == 6 else pf_ss[:, 512:1024]
                if True:
                    for dc in range(DC):
                        wge(v, fP, up + f"w2f{dc}")
                        if dc >= 2:
                            wge(v, fWs[dc % 2], up + f"bsp{dc-2}")
                        elif pup is not None:
                            wge(v, fWs[dc % 2], pup + f"bsp{dc+6}")
                        v.scalar_tensor_tensor(out=strt[dc % 2][:], in0=w2bank_v(dc),
                                               scalar=b2c[:, dc:dc + 1],
                                               in1=res2[dc][:], op0=AluOpType.add,
                                               op1=AluOpType.add).then_inc(fV.sem, 1)
                        fV.n += 1
                        rec(up + f"out{dc}", fV)
                rec(up + "out_done", fV)
            rec(up + "v_end", fV)

        def emit_layer(l, strm_in, strm_out, with_B):
            lp = f"F{l}_"

            @block.gpsimd
            def _(g):
                g.wait_ge(bar.sem, BAR0)
                if l > 0:
                    wge(g, fP, f"F{l-1}_A_pe_end")
                    wge(g, fV, f"F{l-1}_tabs_done")
                    wge(g, fS, f"F{l-1}_B_s_end")
                    wge(g, fV, f"F{l-1}_B_v_end")
                for c in range(DC):
                    g.dma_start(out=wq16[c][:],
                                in_=wqkv_in[l, c * 128:(c + 1) * 128, :]).then_inc(fG.sem, 16)
                    fG.n += 16
                for h in range(NHEAD):
                    g.dma_start(out=wo16h[h][:],
                                in_=wo_in[l, h * 64:(h + 1) * 64, :]).then_inc(fG.sem, 16)
                    fG.n += 16
                for dst, src_ in [(qscol[:], qs_t[l]), (kscol[:], ks_t[l]),
                                  (lngc[:], lng_t[l]), (lnbc[:], lnb_t[l]),
                                  (b1ac[:], b1a_in[l]), (b1gc[:], b1g_in[l]),
                                  (b2c[:], b2_t[l]), (csqf[:], csq_in),
                                  (cskf[:], csk_in), (rperm16[:], rperm_in),
                                  (rpermq16[:], rpermq_in)]:
                    g.dma_start(out=dst, in_=src_).then_inc(fG.sem, 16)
                    fG.n += 16
            rec(lp + "wl", fG)

            @block.vector
            def _(v):
                wge(v, fG, lp + "wl")
                v.wait_ge(fGm.sem, FGM_ALL)
                if l > 0:
                    wge(v, fP, f"F{l-1}_A_pe_end")
                v.tensor_scalar(out=csqf[:], in0=csqf[:], scalar1=qscol[:, 0:1],
                                scalar2=None, op0=AluOpType.mult).then_inc(fV.sem, 1)
                fV.n += 1
                v.tensor_scalar(out=cskf[:], in0=cskf[:], scalar1=kscol[:, 0:1],
                                scalar2=None, op0=AluOpType.mult).then_inc(fV.sem, 1)
                fV.n += 1
                v.tensor_scalar(out=rpermq16[:], in0=rpermq16[:], scalar1=qscol[:, 0:1],
                                scalar2=None, op0=AluOpType.mult).then_inc(fV.sem, 1)
                fV.n += 1
                v.tensor_scalar(out=rperm16[:], in0=rperm16[:], scalar1=kscol[:, 0:1],
                                scalar2=None, op0=AluOpType.mult).then_inc(fV.sem, 1)
                fV.n += 1
            rec(lp + "tabs_done", fV)

            for b in range(NB):
                for w in range(NW):
                    _emit_attn_window(l, b, w, strm_in, strm_out)
            rec(lp + "A_pe_end", fP)
            rec(lp + "A_done_W0", fWs[0])
            rec(lp + "A_done_W1", fWs[1])

            if with_B:
                for b in range(NB):
                    for u in range(T // 512):
                        _emit_ffnmid_unit(l, b, u, strm_out)
            rec(lp + "B_done_W0", fWs[0])
            rec(lp + "B_done_W1", fWs[1])
            rec(lp + "B_v_end", fV)
            rec(lp + "B_pe_end", fP)
            rec(lp + "B_s_end", fS)

        emit_layer(0, xn16, strmA, with_B=(stage >= 4))
        if stage >= 5:
            emit_layer(1, strmA, strmB, with_B=True)

        if stage >= 6:
            fOut = Ch(nc, "f_out")
            ostage = w1sl[0][:, 0:1024]
            p2bf3 = pool[2][:].bitcast(BF16)
            NU = T // 512
            for b in range(NB):
                for u in range(NU):
                    t0 = u * 512
                    cp = f"P3_{b}_{u}_"
                    pidx = b * NU + u
                    ppp = None
                    if pidx > 0:
                        qb, qu = divmod(pidx - 1, NU)
                        ppp = f"P3_{qb}_{qu}_"

                    @block.sync
                    def _(sync):
                        if ppp is None:
                            wge(sync, fWs[0], "F1_B_done_W0")
                            wge(sync, fWs[1], "F1_B_done_W1")
                            wge(sync, fV, "F1_B_v_end")
                            wge(sync, fP, "F1_B_pe_end")
                            if stage >= 2:
                                sync.wait_ge(cWs[0].sem, MAMBA_W0)
                                sync.wait_ge(cWs[1].sem, MAMBA_W1)
                        else:
                            wge(sync, fV, ppp + "sg_done")
                        for dc in range(DC):
                            fl = fL[dc % 2]
                            sti = sq_sb[0] if dc % 2 == 0 else qn_sb[0]
                            fti = t1_sb[0] if dc % 2 == 0 else rb_sb[0]
                            if dc >= 2:
                                wge(sync, fV, cp + f"sg{dc-2}")
                            sync.dma_start(out=sti[:], in_=s16[b, dc, :, t0:t0 + 512]
                                           ).then_inc(fl.sem, 16)
                            fl.n += 16
                            sync.dma_start(out=fti[:], in_=strmB[b, dc, :, t0:t0 + 512]
                                           ).then_inc(fl.sem, 16)
                            fl.n += 16
                            rec(cp + f"ld{dc}", fl)
                        for ts in range(4):
                            wge(sync, fV, cp + f"ob{ts}")
                            sync.dma_start(
                                out=out[b, t0 + ts * 128:t0 + (ts + 1) * 128, :],
                                in_=obuf).then_inc(fOut.sem, 16)
                            fOut.n += 16
                            rec(cp + f"st{ts}", fOut)

                    @block.gpsimd
                    def _(g):
                        for ts in range(4):
                            if ppp is not None and ts == 0:
                                wge(g, fV, ppp + "ob3")
                            elif ts > 0:
                                wge(g, fV, cp + f"ob{ts-1}")
                            g.dma_start(out=xrl,
                                        in_=x_in[b, t0 + ts * 128:t0 + (ts + 1) * 128, :]
                                        ).then_inc(fG.sem, 16)
                            fG.n += 16
                            rec(cp + f"x{ts}", fG)

                    @block.scalar
                    def _(s):
                        for dc in range(DC):
                            fti = t1_sb[0] if dc % 2 == 0 else rb_sb[0]
                            wge(s, fL[dc % 2], cp + f"ld{dc}")
                            if dc >= 2:
                                wge(s, fV, cp + f"sg{dc-2}")
                            s.activation(out=gel_sb[dc % 2][:], in_=fti[:],
                                         func=AF.Silu).then_inc(fS.sem, 1)
                            fS.n += 1
                            rec(cp + f"sil{dc}", fS)
                        for ts in range(4):
                            wge(s, fP, cp + f"tr{ts}")
                            if ts >= 1:
                                wge(s, fV, cp + f"ob{ts-1}")
                            elif ppp is not None:
                                wge(s, fV, ppp + "ob3")
                            s.copy(out=ostage, in_=p2bf3[:, 0:1024]).then_inc(fS.sem, 1)
                            fS.n += 1
                            rec(cp + f"ost{ts}", fS)

                    @block.vector
                    def _(v):
                        for dc in range(DC):
                            sti = sq_sb[0] if dc % 2 == 0 else qn_sb[0]
                            wge(v, fS, cp + f"sil{dc}")
                            if dc == 0:
                                if ppp is not None:
                                    wge(v, fP, ppp + "tr3")
                                elif stage >= 5:
                                    wge(v, fP, "F1_B_pe_end")
                            v.tensor_tensor(out=h2[dc][:], in0=sti[:],
                                            in1=gel_sb[dc % 2][:], op=AluOpType.mult
                                            ).then_inc(fV.sem, 1)
                            fV.n += 1
                            rec(cp + f"sg{dc}", fV)
                        for ts in range(4):
                            wge(v, fG, cp + f"x{ts}")
                            wge(v, fS, cp + f"ost{ts}")
                            if ts >= 1 or ppp is not None:
                                wge(v, fOut, cp + f"st{ts-1}" if ts >= 1
                                    else ppp + "st3")
                            v.tensor_tensor(out=obuf, in0=xrl, in1=ostage,
                                            op=AluOpType.add).then_inc(fV.sem, 1)
                            fV.n += 1
                            rec(cp + f"ob{ts}", fV)
                            if ts == 3:
                                rec(cp + "ob3", fV)
                        rec(cp + "sg_done", fV)

                    @block.tensor
                    def _(te):
                        for ts in range(4):
                            for dc in range(DC):
                                wge(te, fV, cp + f"sg{dc}")
                                if ts > 0 and dc == 0:
                                    wge(te, fS, cp + f"ost{ts-1}")
                                elif ts == 0 and dc == 0 and ppp is not None:
                                    wge(te, fS, ppp + "ost3")
                                te.transpose(out=p2bf3[:, dc * 128:(dc + 1) * 128],
                                             in_=h2[dc][:, ts * 128:(ts + 1) * 128],
                                             identity=ident16[:]).then_inc(fP.sem, 1)
                                fP.n += 1
                            rec(cp + f"tr{ts}", fP)
                        rec(cp + "tr3", fP)

    @block.sync
    def _(sync):
        for c in [ch_hwl, ch_hwl2, ch_swl, ch_sp]:
            if c.n:
                sync.wait_ge(c.sem, c.n)
        if stage >= 2:
            for c in [cL, cG, cWs[0], cWs[1]]:
                if c.n:
                    sync.wait_ge(c.sem, c.n)
        if stage >= 3:
            for c in [fL[0], fL[1], fG, fGw1[0], fGw1[1], fGw2[0], fGw2[1],
                      fWs[0], fWs[1], fGc[0], fGc[1]]:
                if c.n:
                    sync.wait_ge(c.sem, c.n)
        if stage >= 6:
            sync.wait_ge(fOut.sem, fOut.n)

    blk.__exit__(None, None, None)
    if stage >= 3:
        for t_ in reversed(fctxs):
            t_.__exit__(None, None, None)
    else:
        if stage >= 2:
            for t_ in reversed(mamba_ctxs):
                t_.__exit__(None, None, None)
        else:
            for t_ in ps_tt_ctx:
                t_.__exit__(None, None, None)
        for t_ in reversed(p0_ctxs):
            t_.__exit__(None, None, None)
    for c in reversed(ctxs):
        c.__exit__(None, None, None)
    return nc, EV


def build2(T=2048, stage=10, ths_override=None, race=False, dbg=False, mstop=99):
    _, ev = build(T, stage, ths_override, evmap=None, dbg=dbg, mstop=mstop)
    nc, _ = build(T, stage, ths_override, evmap=ev, race=race, dbg=dbg, mstop=mstop)
    return nc



def _hd2_const():
    m = np.zeros((128, 2), np.float32)
    m[0:64, 0] = 1.0; m[64:128, 1] = 1.0
    return m


def _hd2T_const():
    m = np.zeros((2, 128), np.float32)
    m[0, 0:64] = 1.0; m[1, 64:128] = 1.0
    return m


def _rperm_const():
    m = np.zeros((128, 128), np.float32)
    for blkd in range(2):
        o = blkd * 64
        for i in range(32):
            m[o + i + 32, o + i] = -1.0
            m[o + i, o + i + 32] = 1.0
    return m


def _trimadd_const():
    m = np.zeros((128, 256), np.float32)
    for r in range(128):
        m[r, 0:r] = -1e30
        m[r, 128 + r + 1:256] = -1e30
    return m


def _rot_tables(T):
    half = np.arange(0, DH, 2, dtype=np.float32)
    inv_freq = 1.0 / 10000.0 ** (half / DH)
    t = np.arange(T, dtype=np.float32)
    fr = t[:, None] * inv_freq[None, :]
    fr = np.concatenate([fr, fr], axis=-1)
    base = (half + 0.4 * DH) / (1.4 * DH)
    power = (t - T // 2) / float(WIN // 2)
    sc = base[None, :] ** power[:, None]
    sc = np.concatenate([sc, sc], axis=-1)
    cos, sin = np.cos(fr), np.sin(fr)
    csq = (cos * sc) * 8.0; ssq = (sin * sc)
    csk = (cos / sc); ssk = (sin / sc)
    def lay(m):
        r = np.transpose(m)
        return np.concatenate([r, r], axis=0).astype(np.float32)
    return lay(csq), lay(ssq), lay(csk), lay(ssk)


def _selc_const():
    m = np.zeros((32, 32 * 128), np.float32)
    for s in range(32):
        m[s, s * 128:(s + 1) * 128] = 1.0
    return m


def _prep_inputs(inputs):
    T = np.asarray(inputs["L0_attn_x"]).shape[1]
    f = lambda a: np.ascontiguousarray(a, dtype=np.float32)
    x = np.asarray(inputs["L0_attn_x"])
    W_dt = np.asarray(inputs["L0_attn_W_dt"])
    b_dt = np.asarray(inputs["L0_attn_b_dt"])
    A_log = np.asarray(inputs["L0_attn_A_log"])
    Dv = np.asarray(inputs["L0_attn_D"])

    W_dtb = np.concatenate([W_dt, b_dt[None, :]], axis=0)
    A_log_t = np.transpose(A_log.reshape(DC, 128, DSTATE), (1, 0, 2)).reshape(128, DC * DSTATE)
    D_t = np.transpose(Dv.reshape(DC, 128), (1, 0))

    qs = np.asarray(inputs["L0_ffn_q_scale"])
    ks = np.asarray(inputs["L0_ffn_k_scale"])
    qs_t = np.tile(qs, (1, 2)).reshape(2, 128, 1)
    ks_t = np.tile(ks, (1, 2)).reshape(2, 128, 1)
    lng = np.asarray(inputs["L0_ffn_ln_g"])
    lnb = np.asarray(inputs["L0_ffn_ln_b"])
    lng_t = np.transpose(lng.reshape(2, DC, 128), (0, 2, 1))
    lnb_t = np.transpose(lnb.reshape(2, DC, 128), (0, 2, 1))
    b1 = np.asarray(inputs["L0_ffn_b1"])
    b1a = b1[:, :HID]
    b1g = b1[:, HID:]
    pad = np.zeros((2, 22 * 128), np.float32)
    b1a_p = pad.copy(); b1a_p[:, :HID] = b1a
    b1g_p = pad.copy(); b1g_p[:, :HID] = b1g
    b1a_t = np.transpose(b1a_p.reshape(2, 22, 128), (0, 2, 1))
    b1g_t = np.transpose(b1g_p.reshape(2, 22, 128), (0, 2, 1))
    b2 = np.asarray(inputs["L0_ffn_b2"])
    b2_t = np.transpose(b2.reshape(2, DC, 128), (0, 2, 1))

    _RT = _rot_tables(T)
    common = {
        "W_x": f(inputs["L0_attn_W_x"]),
        "W_dtb": f(W_dtb),
        "A_log_t": f(A_log_t),
        "D_t": f(D_t),
        "wqkv": f(inputs["L0_ffn_wqkv"]),
        "qs_t": f(qs_t), "ks_t": f(ks_t),
        "wo": f(inputs["L0_ffn_wo"]),
        "lng_t": f(lng_t), "lnb_t": f(lnb_t),
        "w1": f(inputs["L0_ffn_w1"]), "b1a_t": f(b1a_t), "b1g_t": f(b1g_t),
        "w2": f(inputs["L0_ffn_w2"]), "b2_t": f(b2_t),
        "ident": np.eye(128, dtype=np.float32),
        "cvec": np.tile(np.array([LN_EPS_D2, LN_LOGD, 1e-24, 1e-5, 0, 0, 0, 0],
                                 np.float32), (128, 1)),
        "selc": _selc_const(),
        "hd2": _hd2_const(), "hd2T": _hd2T_const(),
        "onesf": np.ones((128, 128), np.float32),
        "rperm": _rperm_const(), "rpermq": _rperm_const() * 8.0,
        "trimadd": _trimadd_const(),
        "trilo": np.tril(np.ones((128, 128), np.float32)),
        "triup": np.triu(np.ones((128, 128), np.float32)),
        "csq": _RT[0], "ssq": _RT[1], "csk": _RT[2], "ssk": _RT[3],
    }
    in_maps = []
    for c in range(NCORES):
        m = dict(common)
        m["x"] = f(x[c * NB:(c + 1) * NB])
        in_maps.append(m)
    return in_maps


def kernel(**inputs):
    T = np.asarray(inputs["L0_attn_x"]).shape[1]
    in_maps = _prep_inputs(inputs)
    nc = build2(T=T, stage=6)
    res = run_bass_kernel_spmd(nc, in_maps, core_ids=list(range(NCORES)))
    outs = [np.asarray(r["out"], np.float32) for r in res.results]
    return np.concatenate(outs, axis=0)
